# version 10
# baseline (speedup 1.0000x reference)
import contextlib

import numpy as np
import ml_dtypes
import concourse.bass as bass
import concourse.mybir as mybir
from concourse.bass_utils import run_bass_kernel_spmd

F32 = mybir.dt.float32
BF16 = mybir.dt.bfloat16
I32 = mybir.dt.int32
AF = mybir.ActivationFunctionType
ALU = mybir.AluOpType
AX = mybir.AxisListType.X

B = 4
S = 4096
D = 1024
S_CORE = 2048
NT = S_CORE // 128
NDC = D // 128
MAGIC = float(np.float32(1.5 * 2 ** 23))
LN_EPS = 1e-5



_DRAIN_OPS = frozenset("""
tensor_scalar tensor_single_scalar tensor_scalar_add tensor_scalar_sub
tensor_scalar_mul tensor_scalar_max tensor_scalar_min tensor_relu
tensor_tensor tensor_add tensor_sub tensor_mul tensor_max tensor_copy
tensor_reduce reduce_max reduce_sum scalar_tensor_tensor tensor_tensor_scan
memset memzero activation sign square sqrt mul copy add reciprocal select
bn_stats bn_aggr transpose iota
""".split())


class DrainingEngine:

    def __init__(self, eng):
        self.raw = eng
        self._written = set()
        self._read = set()

    def drain(self, *a, **k):
        self._written.clear()
        self._read.clear()
        return self.raw.drain(*a, **k)

    @staticmethod
    def _nm(x):
        t = getattr(x, "tensor", None)
        if t is not None:
            return getattr(t, "name", None)
        if hasattr(x, "ap"):
            return getattr(x, "name", None)
        return None

    def __getattr__(self, name):
        attr = getattr(self.raw, name)
        if name in _DRAIN_OPS:
            def wrapped(*a, **k):
                outs, ins = [], []
                for idx, x in enumerate(a):
                    n_ = self._nm(x)
                    if n_:
                        (outs if idx == 0 else ins).append(n_)
                for kname, x in k.items():
                    n_ = self._nm(x)
                    if n_:
                        (outs if kname == "accum_out" else ins).append(n_)
                need = (any(n_ in self._written for n_ in ins)
                        or any(n_ in self._written or n_ in self._read
                               for n_ in outs))
                if need:
                    self.raw.drain()
                    self._written.clear()
                    self._read.clear()
                attr_r = attr(*a, **k)
                self._written.update(outs)
                self._read.update(ins)
                return attr_r
            return wrapped
        return attr


class NcView:

    def __init__(self, nc, pre):
        self._nc = nc
        self._pre = pre

    def sbuf_tensor(self, name, *a, **k):
        return self._nc.sbuf_tensor(self._pre + name, *a, **k)

    def psum_tensor(self, name, *a, **k):
        return self._nc.psum_tensor(self._pre + name, *a, **k)

    def semaphore(self, name, *a, **k):
        return self._nc.alloc_semaphore(name=self._pre + name)

    def dram_tensor(self, name, *a, **k):
        return self._nc.dram_tensor(self._pre + name, *a, **k)

    def __getattr__(self, name):
        return getattr(self._nc, name)


class DmaSlots:

    def __init__(self, nc, A, name, n):
        self.sems = [A(nc.semaphore(f"{name}{k}")) for k in range(n)]
        self.n = n

    def inc(self, inst, i):
        return inst.then_inc(self.sems[i % self.n], 16)

    def wait(self, eng, i):
        eng.wait_ge(self.sems[i % self.n], 16 * (i // self.n + 1))


def emit_weight_transpose(sync, dst, src, dsem, dcnt):
    ca = src.shape[1]
    cb = dst.shape[1]
    assert src.shape[2] == cb * 128 and dst.shape[2] == ca * 128
    for icb in range(cb):
        sync.dma_start(
            dst.ap()[:, icb, :].rearrange("p (ca q) -> p ca q", ca=ca),
            src.ap()[:, :, icb * 128:(icb + 1) * 128],
            transpose=True,
        ).then_inc(dsem, 16)
        dcnt += 16
    return dcnt


def emit_round_clip(v, out_ap, t_ap, ni_ap, nf_ap, lo, hi):
    v.tensor_copy(ni_ap, t_ap)
    v.tensor_copy(nf_ap, ni_ap)
    v.tensor_tensor(t_ap, t_ap, nf_ap, ALU.subtract)
    v.scalar_tensor_tensor(nf_ap, t_ap, 0.5, nf_ap, ALU.is_ge, ALU.add)
    v.scalar_tensor_tensor(nf_ap, t_ap, -0.5, nf_ap, ALU.is_ge, ALU.add)
    v.tensor_scalar(nf_ap, nf_ap, -1.0, hi, ALU.add, ALU.min)
    return v.tensor_scalar(out_ap, nf_ap, lo, None, ALU.max)


def emit_recip_newton(v, out_ap, x_ap, scrA, scrB):
    v.reciprocal(scrA, x_ap)
    v.tensor_tensor(scrB, x_ap, scrA, ALU.mult)
    v.tensor_scalar(scrB, scrB, -1.0, 2.0, ALU.mult, ALU.add)
    return v.tensor_tensor(out_ap, scrA, scrB, ALU.mult)


def emit_round_correct(v, out_ap, t_ap, nf_ap, lo, hi):
    v.tensor_tensor(t_ap, t_ap, nf_ap, ALU.subtract)
    v.scalar_tensor_tensor(nf_ap, t_ap, 0.5, nf_ap, ALU.is_ge, ALU.add)
    v.scalar_tensor_tensor(nf_ap, t_ap, -0.5, nf_ap, ALU.is_ge, ALU.add)
    v.tensor_scalar(nf_ap, nf_ap, -1.0, hi, ALU.add, ALU.min)
    return v.tensor_scalar(out_ap, nf_ap, lo, None, ALU.max)


def emit_round_correct2(v, outs, ts_, nfs, lo, hi):
    for j in range(2):
        v.tensor_tensor(ts_[j], ts_[j], nfs[j], ALU.subtract)
    for j in range(2):
        v.scalar_tensor_tensor(nfs[j], ts_[j], 0.5, nfs[j], ALU.is_ge, ALU.add)
    for j in range(2):
        v.scalar_tensor_tensor(nfs[j], ts_[j], -0.5, nfs[j], ALU.is_ge, ALU.add)
    for j in range(2):
        v.tensor_scalar(nfs[j], nfs[j], -1.0, hi, ALU.add, ALU.min)
    for j in range(2):
        v.tensor_scalar(outs[j], nfs[j], lo, None, ALU.max)


def emit_round_clip2(v, outs, ts_, nis, nfs, lo, hi):
    for j in range(2):
        v.tensor_copy(nis[j], ts_[j])
    for j in range(2):
        v.tensor_copy(nfs[j], nis[j])
    for j in range(2):
        v.tensor_tensor(ts_[j], ts_[j], nfs[j], ALU.subtract)
    for j in range(2):
        v.scalar_tensor_tensor(nfs[j], ts_[j], 0.5, nfs[j], ALU.is_ge, ALU.add)
    for j in range(2):
        v.scalar_tensor_tensor(nfs[j], ts_[j], -0.5, nfs[j], ALU.is_ge, ALU.add)
    for j in range(2):
        v.tensor_scalar(nfs[j], nfs[j], -1.0, hi, ALU.add, ALU.min)
    for j in range(2):
        v.tensor_scalar(outs[j], nfs[j], lo, None, ALU.max)


def emit_rstd(scalar, var_ap, rstd_ap):
    scalar.activation(rstd_ap, var_ap, AF.Ln)
    return scalar.activation(rstd_ap, rstd_ap, AF.Exp, scale=-0.5)


def emit_ln_stats_phase(blk, x_src_tiles, n_tiles, xt, sx_slots, ssq, ssum,
                        s_sumsq, s_sum, scr_sq, scr_cp, col0=0, cnt0=0):
    n_buf = xt.shape[1]

    @blk.gpsimd
    def _(g):
        for i in range(n_tiles):
            if i >= n_buf:
                g.wait_ge(ssq, cnt0 + i - n_buf + 1)
                g.wait_ge(ssum, cnt0 + i - n_buf + 1)
            sx_slots.inc(g.dma_start(xt[:, i % n_buf, :], x_src_tiles(i)), i)

    @blk.scalar
    def _(sc):
        sc = DrainingEngine(sc)
        for i in range(n_tiles):
            sx_slots.wait(sc, i)
            sc.activation(scr_sq[:, i % scr_sq.shape[1], :], xt[:, i % n_buf, :],
                          AF.Square,
                          accum_out=s_sumsq[:, col0 + i:col0 + i + 1]).then_inc(ssq, 1)
            sc.activation(scr_sq[:, i % scr_sq.shape[1], :], xt[:, i % n_buf, :],
                          AF.Copy,
                          accum_out=s_sum[:, col0 + i:col0 + i + 1]).then_inc(ssum, 1)


def emit_stat_math_phase(blk, s_sum, s_sumsq, s_negm, s_var, s_rstd, width, dim):
    @blk.vector
    def _(v):
        v = DrainingEngine(v)
        v.tensor_scalar_mul(s_negm[:, 0:width], s_sum[:, 0:width], -1.0 / dim)
        v.tensor_tensor(s_var[:, 0:width], s_negm[:, 0:width], s_negm[:, 0:width], ALU.mult)
        v.tensor_scalar(s_sumsq[:, 0:width], s_sumsq[:, 0:width], 1.0 / dim, LN_EPS,
                        ALU.mult, ALU.add)
        v.tensor_tensor(s_var[:, 0:width], s_sumsq[:, 0:width], s_var[:, 0:width],
                        ALU.subtract)

    @blk.scalar
    def _(sc):
        pass


def emit_rstd_phase(blk, s_var, s_rstd, width):
    @blk.scalar
    def _(sc):
        emit_rstd(DrainingEngine(sc), s_var[:, 0:width], s_rstd[:, 0:width])




def emit_residual_phase(nc, A4, x_src_tiles, y_src_tiles, out_tiles, n_tiles,
                        xt, p_negm, p_rstd, stat0=0, SA=None):
    SA = SA or (lambda h: h)
    yt = A4(nc.sbuf_tensor("res_yt", [128, 2, D], BF16))
    t1 = A4(nc.sbuf_tensor("res_t1", [128, 2, D], F32))
    xnew = A4(nc.sbuf_tensor("res_xnew", [128, 2, D], F32))
    sx4 = DmaSlots(nc, SA, "res_sx", 2)
    sy4 = DmaSlots(nc, SA, "res_sy", 2)
    sres = SA(nc.semaphore("res_done"))
    sst = DmaSlots(nc, SA, "res_st", 2)
    blk = A4(nc.Block())

    @blk.gpsimd
    def _(g):
        for i in range(n_tiles):
            if i >= 2:
                g.wait_ge(sres, i - 1)
            sx4.inc(g.dma_start(xt[:, i % 2, :], x_src_tiles(i)), i)
            sy4.inc(g.dma_start(yt[:, i % 2, :], y_src_tiles(i)), i)

    @blk.vector
    def _(v):
        v = DrainingEngine(v)
        for i in range(n_tiles):
            sx4.wait(v, i)
            sy4.wait(v, i)
            if i >= 2:
                sst.wait(v, i - 2)
            k = stat0 + i
            v.tensor_scalar(t1[:, i % 2, :], yt[:, i % 2, :],
                            p_negm[:, k:k + 1], p_rstd[:, k:k + 1],
                            ALU.add, ALU.mult)
            v.tensor_tensor(xnew[:, i % 2, :], xt[:, i % 2, :],
                            t1[:, i % 2, :], ALU.add).then_inc(sres, 1)

    @blk.sync
    def _(sy_):
        for i in range(n_tiles):
            sy_.wait_ge(sres, i + 1)
            sst.inc(sy_.dma_start(out_tiles(i), xnew[:, i % 2, :]), i)
        for k in range(2):
            sy_.wait_ge(sst.sems[k], 16 * ((n_tiles + 1 - k) // 2))




def emit_l1_ffn(nc, SA, x_in, wg, wu, wd, x_out):
    SA2 = SA
    H = 3072
    NHC = H // 128
    NSL = H // 512
    c127 = float(np.float32(1.0 / 127.0))

    y_scr = nc.dram_tensor("y_scr", [S_CORE, D], BF16)

    with contextlib.ExitStack() as es:
        A = es.enter_context
        wgT = A(nc.sbuf_tensor("wgT", [128, NHC, NDC, 128], BF16))
        wuT = A(nc.sbuf_tensor("wuT", [128, NHC, NDC, 128], BF16))
        wdT = A(nc.sbuf_tensor("wdT", [128, 2, NDC, NHC // 2, 128], BF16))
        xt = A(nc.sbuf_tensor("xt", [128, 2, D], F32))
        scr_sq = A(nc.sbuf_tensor("scr_sq", [128, 2, D], F32))
        scr_cp = A(nc.sbuf_tensor("scr_cp", [128, 2, D], BF16))
        s_sum = A(nc.sbuf_tensor("s_sum", [128, NT], F32))
        s_sumsq = A(nc.sbuf_tensor("s_sumsq", [128, NT], F32))
        s_negm = A(nc.sbuf_tensor("s_negm", [128, NT], F32))
        s_var = A(nc.sbuf_tensor("s_var", [128, NT], F32))
        s_rstd = A(nc.sbuf_tensor("s_rstd", [128, NT], F32))
        p_sum = A(nc.sbuf_tensor("p_sum", [128, NT], F32))
        p_sumsq = A(nc.sbuf_tensor("p_sumsq", [128, NT], F32))
        p_negm = A(nc.sbuf_tensor("p_negm", [128, NT], F32))
        p_var = A(nc.sbuf_tensor("p_var", [128, NT], F32))
        p_rstd = A(nc.sbuf_tensor("p_rstd", [128, NT], F32))
        dwg = SA(nc.semaphore("dwg"))
        dws = SA(nc.semaphore("dws"))
        sx = DmaSlots(nc, SA, "sx", 2)
        ssq = SA(nc.semaphore("ssq"))
        ssum = SA(nc.semaphore("ssum"))
        gcnt = 0
        tcnt = 0
        with nc.sbuf_tensor("wraw", [128, NHC // 2, D], BF16) as wraw:
            for (w_dram, w_t) in ((wg, wgT), (wu, wuT)):
                for half in range(2):
                    hh = NHC // 2
                    with nc.Block() as blk:
                        @blk.gpsimd
                        def _(g, w_dram=w_dram, half=half, base=gcnt):
                            g.dma_start(
                                wraw[:],
                                w_dram.ap()[half * (H // 2):(half + 1) * (H // 2), :]
                                .rearrange("(c p) d -> p c d", p=128),
                            ).then_inc(dwg, 16)
                            g.wait_ge(dwg, base + 16)
                    gcnt += 16
                    with nc.Block() as blk:
                        @blk.sync
                        def _(sync, w_t=w_t, half=half, base=tcnt):
                            sync.dma_start(
                                w_t.ap()[:, half * hh:(half + 1) * hh, :, :],
                                wraw[:],
                                transpose=True,
                            ).then_inc(dws, 16)
                            sync.wait_ge(dws, base + 16)
                    tcnt += 16
        with nc.sbuf_tensor("wdraw", [128, NDC, H // 2], BF16) as wdraw:
            for half in range(2):
                with nc.Block() as blk:
                    @blk.gpsimd
                    def _(g, half=half, base=gcnt):
                        g.dma_start(
                            wdraw[:],
                            wd.ap()[:, half * (H // 2):(half + 1) * (H // 2)]
                            .rearrange("(c p) h -> p c h", p=128),
                        ).then_inc(dwg, 16)
                        g.wait_ge(dwg, base + 16)
                gcnt += 16
                with nc.Block() as blk:
                    @blk.sync
                    def _(sync, half=half, base=tcnt):
                        sync.dma_start(
                            wdT.ap()[:, half, :, :, :],
                            wdraw[:],
                            transpose=True,
                        ).then_inc(dws, 16)
                        sync.wait_ge(dws, base + 16)
                tcnt += 16

        with nc.Block() as blk:
            emit_ln_stats_phase(blk, lambda i: x_in.ap()[i * 128:(i + 1) * 128, :],
                                NT, xt, sx, ssq, ssum, s_sumsq, s_sum, scr_sq, scr_cp)
        with nc.Block() as blk:
            emit_stat_math_phase(blk, s_sum, s_sumsq, s_negm, s_var, s_rstd, NT, D)
        with nc.Block() as blk:
            emit_rstd_phase(blk, s_var, s_rstd, NT)

        with contextlib.ExitStack() as es3:
            A3 = es3.enter_context
            xln = A3(nc.sbuf_tensor("xln", [128, 2, D], BF16))
            xf = A3(nc.sbuf_tensor("xf", [128, 2, NDC, 128], BF16))
            gsil = A3(nc.sbuf_tensor("gsil", [128, 2, 512], BF16))
            usc = A3(nc.sbuf_tensor("usc", [128, 2, 512], BF16))
            h_tm = A3(nc.sbuf_tensor("h_tm", [128, 2, H], BF16))
            h_fm = A3(nc.sbuf_tensor("h_fm", [128, 2, NHC, 128], BF16))
            y_sb = A3(nc.sbuf_tensor("y_sb", [128, 2, D], BF16))
            sx3 = DmaSlots(nc, SA, "sx3", 2)
            sxln = SA(nc.semaphore("sxln"))
            sxf = DmaSlots(nc, SA, "sxf", 2)
            sgmm = SA(nc.semaphore("sgmm"))
            summ = SA(nc.semaphore("summ"))
            ssil = SA(nc.semaphore("ssil"))
            suev = SA(nc.semaphore("suev"))
            sh = SA(nc.semaphore("sh"))
            shf = DmaSlots(nc, SA, "shf", 2)
            sdmm = SA(nc.semaphore("sdmm"))
            spsq = SA(nc.semaphore("spsq"))
            sy2 = SA(nc.semaphore("sy2"))
            syst = DmaSlots(nc, SA, "syst", 2)
            gps = A3(nc.psum_tensor("gps", [128, 2, 512], F32))
            ups = A3(nc.psum_tensor("ups", [128, 2, 512], F32))
            dps = A3(nc.psum_tensor("dps", [128, 2, D], F32))
            blk = A3(nc.Block())
            @blk.gpsimd
            def _(g):
                for i in range(NT):
                    if i >= 2:
                        g.wait_ge(sxln, i - 1)
                    sx3.inc(g.dma_start(xt[:, i % 2, :],
                                        x_in.ap()[i * 128:(i + 1) * 128, :]), i)

            @blk.vector
            def _(v):
                v = DrainingEngine(v)
                for i in range(NT):
                    sx3.wait(v, i)
                    if i >= 2:
                        sxf.wait(v, i - 2)
                    v.tensor_scalar(xln[:, i % 2, :], xt[:, i % 2, :],
                                    s_negm[:, i:i + 1], s_rstd[:, i:i + 1],
                                    ALU.add, ALU.mult).then_inc(sxln, 1)
                    for j in range(NSL):
                        gidx = i * NSL + j
                        v.wait_ge(summ, gidx + 1)
                        v.tensor_scalar_mul(usc[:, j % 2, :], ups[:, j % 2, :],
                                            c127).then_inc(suev, 1)
                        v.wait_ge(ssil, gidx + 1)
                        if i >= 2 and j == 0:
                            shf.wait(v, i - 2)
                        v.tensor_tensor(h_tm[:, i % 2, j * 512:(j + 1) * 512],
                                        gsil[:, j % 2, :], usc[:, j % 2, :],
                                        ALU.mult).then_inc(sh, 1)
                    v.wait_ge(sdmm, i + 1)
                    if i >= 2:
                        syst.wait(v, i - 2)
                    v.tensor_scalar(y_sb[:, i % 2, :], dps[:, i % 2, :], c127, None,
                                    ALU.mult, ALU.add,
                                    accum_out=p_sum[:, i:i + 1]).then_inc(sy2, 1)

            @blk.scalar
            def _(sc):
                sc = DrainingEngine(sc)
                for i in range(NT):
                    for j in range(NSL):
                        gidx = i * NSL + j
                        sc.wait_ge(sgmm, gidx + 1)
                        if gidx >= 2:
                            sc.wait_ge(sh, gidx - 1)
                        sc.activation(gsil[:, j % 2, :], gps[:, j % 2, :], AF.Silu,
                                      scale=c127).then_inc(ssil, 1)
                    sc.wait_ge(sy2, i + 1)
                    sc.activation(scr_sq[:, i % 2, :], y_sb[:, i % 2, :], AF.Square,
                                  accum_out=p_sumsq[:, i:i + 1]).then_inc(spsq, 1)

            @blk.sync
            def _(sy_):
                for i in range(NT):
                    sy_.wait_ge(sxln, i + 1)
                    sxf.inc(sy_.dma_start(
                        xf.ap()[:, i % 2, :, :],
                        xln.ap()[:, i % 2, :].rearrange("p (c q) -> p c q", c=NDC),
                        transpose=True,
                    ), i)
                    sy_.wait_ge(sh, (i + 1) * NSL)
                    shf.inc(sy_.dma_start(
                        h_fm.ap()[:, i % 2, :, :],
                        h_tm.ap()[:, i % 2, :].rearrange("p (c q) -> p c q", c=NHC),
                        transpose=True,
                    ), i)
                    sy_.wait_ge(sy2, i + 1)
                    syst.inc(sy_.dma_start(y_scr.ap()[i * 128:(i + 1) * 128, :],
                                           y_sb[:, i % 2, :]), i)
                for k in range(2):
                    sy_.wait_ge(syst.sems[k], 16 * (NT // 2))

            @blk.tensor
            def _(t):
                for i in range(NT):
                    sxf.wait(t, i)
                    for j in range(NSL):
                        gidx = i * NSL + j
                        if gidx >= 2:
                            t.wait_ge(ssil, gidx - 1)
                        for c in range(NDC):
                            mm_ = t.matmul(gps[:, j % 2, :],
                                     xf[:, i % 2, c, :],
                                     wgT.ap()[:, 4 * j:4 * j + 4, c, :],
                                     start=(c == 0), stop=(c == NDC - 1))
                        mm_.then_inc(sgmm, 1)
                        if gidx >= 2:
                            t.wait_ge(suev, gidx - 1)
                        for c in range(NDC):
                            mm_ = t.matmul(ups[:, j % 2, :],
                                     xf[:, i % 2, c, :],
                                     wuT.ap()[:, 4 * j:4 * j + 4, c, :],
                                     start=(c == 0), stop=(c == NDC - 1))
                        mm_.then_inc(summ, 1)
                    shf.wait(t, i)
                    if i >= 2:
                        t.wait_ge(sy2, i - 1)
                        t.wait_ge(spsq, i - 1)
                    for n2 in range(2):
                        for c in range(NHC):
                            mm_ = t.matmul(dps[:, i % 2, n2 * 512:(n2 + 1) * 512],
                                     h_fm[:, i % 2, c, :],
                                     wdT.ap()[:, c // (NHC // 2), 4 * n2:4 * n2 + 4,
                                              c % (NHC // 2), :],
                                     start=(c == 0), stop=(c == NHC - 1))
                    mm_.then_inc(sdmm, 1)

        with nc.Block() as blk:
            emit_stat_math_phase(blk, p_sum, p_sumsq, p_negm, p_var, p_rstd, NT, D)
        with nc.Block() as blk:
            emit_rstd_phase(blk, p_var, p_rstd, NT)
        with contextlib.ExitStack() as es4:
            A4 = es4.enter_context
            yt = A4(nc.sbuf_tensor("yt", [128, 2, D], BF16))
            t1 = A4(nc.sbuf_tensor("t1", [128, 2, D], F32))
            xnew = A4(nc.sbuf_tensor("xnew", [128, 2, D], F32))
            sx4 = DmaSlots(nc, SA, "sx4", 2)
            sy4 = DmaSlots(nc, SA, "sy4", 2)
            sres = SA(nc.semaphore("sres"))
            sst = DmaSlots(nc, SA, "sst", 2)
            blk = A4(nc.Block())
            @blk.gpsimd
            def _(g):
                for i in range(NT):
                    if i >= 2:
                        g.wait_ge(sres, i - 1)
                    sx4.inc(g.dma_start(xt[:, i % 2, :],
                                        x_in.ap()[i * 128:(i + 1) * 128, :]), i)
                    sy4.inc(g.dma_start(yt[:, i % 2, :],
                                        y_scr.ap()[i * 128:(i + 1) * 128, :]), i)

            @blk.vector
            def _(v):
                v = DrainingEngine(v)
                for i in range(NT):
                    sx4.wait(v, i)
                    sy4.wait(v, i)
                    if i >= 2:
                        sst.wait(v, i - 2)
                    v.tensor_scalar(t1[:, i % 2, :], yt[:, i % 2, :],
                                    p_negm[:, i:i + 1], p_rstd[:, i:i + 1],
                                    ALU.add, ALU.mult)
                    v.tensor_tensor(xnew[:, i % 2, :], xt[:, i % 2, :],
                                    t1[:, i % 2, :], ALU.add).then_inc(sres, 1)

            @blk.sync
            def _(sy_):
                for i in range(NT):
                    sy_.wait_ge(sres, i + 1)
                    sst.inc(sy_.dma_start(x_out.ap()[i * 128:(i + 1) * 128, :],
                                          xnew[:, i % 2, :]), i)
                for k in range(2):
                    sy_.wait_ge(sst.sems[k], 16 * (NT // 2))


def build_l1_ffn():
    nc0 = bass.Bass("TRN2", target_bir_lowering=False)
    nc = NcView(nc0, "d_")
    x_in = nc0.dram_tensor("x_in", [S_CORE, D], F32, kind="ExternalInput")
    wg = nc0.dram_tensor("wg", [3072, D], I32, kind="ExternalInput")
    wu = nc0.dram_tensor("wu", [3072, D], I32, kind="ExternalInput")
    wd = nc0.dram_tensor("wd", [D, 3072], I32, kind="ExternalInput")
    x_out = nc0.dram_tensor("x_out", [S_CORE, D], F32, kind="ExternalOutput")
    with contextlib.ExitStack() as es:
        emit_l1_ffn(nc, lambda h: h, x_in, wg, wu, wd, x_out)
    return nc0






def emit_l0_ffn(nc, SA, x_in, wg, wu, wd, x_out):
    SA2 = SA
    H = 4096
    NHC = H // 128
    NSL = H // 512
    NQ = 4
    c127 = float(np.float32(1.0 / 127.0))

    y_scr = nc.dram_tensor("y_scr", [S_CORE, D], BF16)
    inth_scr = nc.dram_tensor("inth_scr", [S_CORE, H], BF16)

    QF = H * D // NQ // 128

    with contextlib.ExitStack() as es:
        A = es.enter_context
        xt = A(nc.sbuf_tensor("xt", [128, 2, D], F32))
        scr_sq = A(nc.sbuf_tensor("scr_sq", [128, 1, D], F32))
        scr_cp = None
        s_sum = A(nc.sbuf_tensor("s_sum", [128, NT], F32))
        s_sumsq = A(nc.sbuf_tensor("s_sumsq", [128, NT], F32))
        s_negm = A(nc.sbuf_tensor("s_negm", [128, NT], F32))
        s_var = A(nc.sbuf_tensor("s_var", [128, NT], F32))
        s_rstd = A(nc.sbuf_tensor("s_rstd", [128, NT], F32))
        p_sum = A(nc.sbuf_tensor("p_sum", [128, NT], F32))
        p_sumsq = A(nc.sbuf_tensor("p_sumsq", [128, NT], F32))
        p_negm = A(nc.sbuf_tensor("p_negm", [128, NT], F32))
        p_var = A(nc.sbuf_tensor("p_var", [128, NT], F32))
        p_rstd = A(nc.sbuf_tensor("p_rstd", [128, NT], F32))
        s_a = A(nc.sbuf_tensor("s_a", [128, NT], F32))
        s_cg = A(nc.sbuf_tensor("s_cg", [128, NT], F32))
        s_cu = A(nc.sbuf_tensor("s_cu", [128, NT], F32))
        s_a2 = A(nc.sbuf_tensor("s_a2", [128, NT], F32))
        s_c2 = A(nc.sbuf_tensor("s_c2", [128, NT], F32))
        wm_b = A(nc.sbuf_tensor("wm_b", [128, 6], F32))
        wsum = A(nc.sbuf_tensor("wsum", [128, 3 * NQ], F32))
        ones_c = A(nc.sbuf_tensor("ones_c", [128, 1], F32))
        ones_r = A(nc.sbuf_tensor("ones_r", [1, 128], F32))
        row6 = A(nc.sbuf_tensor("row6", [1, 6], F32))
        rowq = A(nc.sbuf_tensor("rowq", [1, 3 * NQ], F32))

        with contextlib.ExitStack() as es0:
            A0 = es0.enter_context
            wst = A0(nc.sbuf_tensor("wst", [128, 2, QF], F32))
            sld = DmaSlots(nc, SA, "b_ld", 2)
            sred = SA(nc.semaphore("b_red"))
            blk = A0(nc.Block())
            srcs = []
            for widx, w_dram in enumerate((wg, wu)):
                for q in range(NQ):
                    srcs.append(w_dram.ap()[q * (H // NQ):(q + 1) * (H // NQ), :]
                                .rearrange("(c p) d -> p c d", p=128))
            for q in range(NQ):
                srcs.append(wd.ap()[q * (D // NQ):(q + 1) * (D // NQ), :]
                            .rearrange("(c p) h -> p c h", p=128))

            @blk.gpsimd
            def _(g):
                for k in range(3 * NQ):
                    if k >= 2:
                        g.wait_ge(sred, k - 1)
                    sld.inc(g.dma_start(
                        wst.ap()[:, k % 2, :].rearrange("p (c d) -> p c d", d=D if k < 2 * NQ else H),
                        srcs[k]), k)

            @blk.vector
            def _(v):
                v = DrainingEngine(v)
                for k in range(3 * NQ):
                    sld.wait(v, k)
                    v.reduce_sum(wsum[:, k:k + 1], wst[:, k % 2, :], axis=AX,
                                 apply_absolute_value=True).then_inc(sred, 1)

        with contextlib.ExitStack() as es0:
            A0 = es0.enter_context
            pst = A0(nc.psum_tensor("b_pst", [128, 16], F32))
            with nc.Block() as blk:
                @blk.vector
                def _(v):
                    v.memset(ones_c[:], 1.0)
                    v.memset(ones_r[:], 1.0)
            with nc.Block() as blk:
                @blk.tensor
                def _(t):
                    mm_ = t.matmul(pst[0:1, 0:3 * NQ], ones_c[:], wsum[:], start=True, stop=True)
                    t.drain()
            with nc.Block() as blk:
                @blk.vector
                def _(v):
                    v = DrainingEngine(v)
                    v.tensor_copy(rowq[:], pst[0:1, 0:3 * NQ])
                    for w in range(3):
                        v.reduce_sum(row6[:, w:w + 1], rowq[:, w * NQ:(w + 1) * NQ], axis=AX)
                    v.tensor_scalar(row6[:, 0:3], row6[:, 0:3], 1.0 / (H * D), 1e-5,
                                    ALU.mult, ALU.max)
                    emit_recip_newton(v, row6[:, 3:6], row6[:, 0:3],
                                      rowq[:, 0:3], rowq[:, 3:6])
            with nc.Block() as blk:
                @blk.tensor
                def _(t):
                    mm_ = t.matmul(pst[:, 0:6], ones_r[:], row6[:], start=True, stop=True)
                    t.drain()
            with nc.Block() as blk:
                @blk.vector
                def _(v):
                    v.tensor_copy(wm_b[:], pst[:, 0:6])

        def emit_quant_weights(A0, pairs, stp_name):
            wstq = A0(nc.sbuf_tensor(stp_name + "_f", [128, QF], F32))
            wqb = A0(nc.sbuf_tensor(stp_name + "_b", [128, QF], BF16))
            wnf = A0(nc.sbuf_tensor(stp_name + "_nf", [128, QF // 2], F32))
            sld = SA(nc.semaphore(stp_name + "_ld"))
            sqn = SA(nc.semaphore(stp_name + "_qn"))
            stp = SA(nc.semaphore(stp_name + "_tp"))
            blk = A0(nc.Block())

            @blk.gpsimd
            def _(g):
                for k, (src, dst, widx) in enumerate(pairs):
                    if k >= 1:
                        g.wait_ge(sqn, k)
                    g.dma_start(wstq.ap().rearrange("p (c d) -> p c d", c=src.shape[1]),
                                src).then_inc(sld, 16)

            @blk.vector
            def _(v):
                v = DrainingEngine(v)
                for k, (src, dst, widx) in enumerate(pairs):
                    v.wait_ge(sld, 16 * (k + 1))
                    if k >= 1:
                        v.wait_ge(stp, 16 * k)
                    v.tensor_scalar_mul(wstq[:], wstq[:], wm_b[:, 3 + widx:4 + widx])
                    for q4 in range(2):
                        sl = slice(q4 * (QF // 2), (q4 + 1) * (QF // 2))
                        v.tensor_scalar(wnf[:], wstq[:, sl], 0.5, None, ALU.is_ge)
                        v.scalar_tensor_tensor(wnf[:], wstq[:, sl], -0.5,
                                               wnf[:], ALU.is_ge, ALU.add)
                        v.tensor_scalar(wqb[:, sl], wnf[:], -1.0, 1.0,
                                        ALU.mult, ALU.add)
                    v.drain().then_inc(sqn, 1)

            @blk.sync
            def _(sy_):
                for k, (src, dst, widx) in enumerate(pairs):
                    sy_.wait_ge(sqn, k + 1)
                    sy_.dma_start(dst, wstq if False else wqb[:],
                                  transpose=True).then_inc(stp, 16)
                sy_.wait_ge(stp, 16 * len(pairs))

        sx = DmaSlots(nc, SA, "sx", 2)
        ssq = SA(nc.semaphore("ssq"))
        ssum = SA(nc.semaphore("ssum"))
        with nc.Block() as blk:
            emit_ln_stats_phase(blk, lambda i: x_in.ap()[i * 128:(i + 1) * 128, :],
                                NT, xt, sx, ssq, ssum, s_sumsq, s_sum, scr_sq, scr_cp)
        with nc.Block() as blk:
            emit_stat_math_phase(blk, s_sum, s_sumsq, s_negm, s_var, s_rstd, NT, D)
        with nc.Block() as blk:
            emit_rstd_phase(blk, s_var, s_rstd, NT)

        with contextlib.ExitStack() as es3:
            A3 = es3.enter_context
            wgT = A3(nc.sbuf_tensor("wgT", [128, NHC, NDC, 128], BF16))
            wuT = A3(nc.sbuf_tensor("wuT", [128, NHC, NDC, 128], BF16))
            with contextlib.ExitStack() as esq:
                pairs = []
                for widx, (w_dram, w_t) in enumerate(((wg, wgT), (wu, wuT))):
                    for q in range(NQ):
                        src = (w_dram.ap()[q * (H // NQ):(q + 1) * (H // NQ), :]
                               .rearrange("(c p) d -> p c d", p=128))
                        dst = w_t.ap()[:, q * (NHC // NQ):(q + 1) * (NHC // NQ), :, :]
                        pairs.append((src, dst, widx))
                emit_quant_weights(esq.enter_context, pairs, "qgu")

            xln = A3(nc.sbuf_tensor("xln", [128, D], F32))
            scrq = scr_sq
            intx = A3(nc.sbuf_tensor("intx", [128, 1, D], BF16))
            ixf = A3(nc.sbuf_tensor("ixf", [128, 1, NDC, 128], BF16))
            gsil = A3(nc.sbuf_tensor("gsil", [128, 2, 512], F32))
            usc = A3(nc.sbuf_tensor("usc", [128, 2, 512], F32))
            h_tm = A3(nc.sbuf_tensor("h_tm", [128, 2, H], F32))
            inth = A3(nc.sbuf_tensor("inth", [128, 1, H], BF16))
            b_ni0 = A3(nc.sbuf_tensor("b_ni0", [128, H // 4], I32))
            b_nf0 = A3(nc.sbuf_tensor("b_nf0", [128, H // 4], F32))
            b_nf1s = [b_nf0.ap().unsqueeze(1) if qq % 2 == 0
                      else scr_sq.ap() for qq in range(4)]

            sx3 = DmaSlots(nc, SA, "sx3", 2)
            sxln = SA(nc.semaphore("sxln"))
            sixf = DmaSlots(nc, SA, "sixf", 2)
            sgmm = SA(nc.semaphore("sgmm"))
            summ = SA(nc.semaphore("summ"))
            ssil = SA(nc.semaphore("ssil"))
            suev = SA(nc.semaphore("suev"))
            sh = SA(nc.semaphore("sh"))
            sqh = SA(nc.semaphore("sqh"))
            sqm = SA(nc.semaphore("sqm"))
            sqp = SA(nc.semaphore("sqp"))
            sqc = SA(nc.semaphore("sqc"))
            shp = SA(nc.semaphore("shp"))
            shst = DmaSlots(nc, SA, "shst", 2)
            gps = A3(nc.psum_tensor("gps", [128, 2, 512], F32))
            ups = A3(nc.psum_tensor("ups", [128, 2, 512], F32))
            blk = A3(nc.Block())

            @blk.gpsimd
            def _(g):
                gp = DrainingEngine(g)
                for i in range(NT):
                    if i >= 2:
                        g.wait_ge(sxln, 2 * (i - 1))
                    sx3.inc(g.dma_start(xt[:, i % 2, :],
                                        x_in.ap()[i * 128:(i + 1) * 128, :]), i)
                    for j in range(NSL):
                        gidx = i * NSL + j
                        g.wait_ge(ssil, gidx + 1)
                        g.wait_ge(suev, gidx + 1)
                        if i >= 2 and j == 0:
                            g.wait_ge(sqh, i - 1)
                        gp.tensor_tensor(h_tm[:, i % 2, j * 512:(j + 1) * 512],
                                         gsil[:, j % 2, :], usc[:, j % 2, :],
                                         ALU.mult).then_inc(sh, 1)

            @blk.vector
            def _(v):
                v = DrainingEngine(v)
                for i in range(NT):
                    sx3.wait(v, i)
                    if i >= 2:
                        sixf.wait(v, i - 2)
                    v.tensor_scalar(xln[:], xt[:, i % 2, :],
                                    s_negm[:, i:i + 1], s_rstd[:, i:i + 1],
                                    ALU.add, ALU.mult).then_inc(sxln, 1)
                    v.reduce_max(s_a[:, i:i + 1], xln[:], axis=AX,
                                 apply_absolute_value=True)
                    v.tensor_scalar_max(s_a[:, i:i + 1], s_a[:, i:i + 1], 1e-5)
                    v.tensor_scalar(s_cg[:, i:i + 1], s_a[:, i:i + 1], -c127, None,
                                    ALU.mult, ALU.bypass)
                    v.tensor_tensor(s_cu[:, i:i + 1], s_cg[:, i:i + 1],
                                    wm_b[:, 1:2], ALU.mult)
                    v.tensor_tensor(s_cg[:, i:i + 1], s_cg[:, i:i + 1],
                                    wm_b[:, 0:1], ALU.mult)
                    emit_recip_newton(v, s_a[:, i:i + 1], s_a[:, i:i + 1],
                                      s_sum[:, i:i + 1], s_sumsq[:, i:i + 1])
                    v.tensor_scalar_mul(s_a[:, i:i + 1], s_a[:, i:i + 1], 127.0)
                    v.tensor_scalar_mul(xln[:], xln[:], s_a[:, i:i + 1])
                    v.drain().then_inc(sqm, 1)
                    v.wait_ge(sqc, 5 * i + 1)
                    if i >= 1:
                        sixf.wait(v, i - 1)
                    emit_round_correct(v, intx[:, 0, :], xln[:],
                                       b_nf0[:], -128.0, 127.0)
                    v.drain().then_inc(sxln, 1)

                    v.wait_ge(sh, (i + 1) * NSL)
                    v.reduce_max(s_a2[:, i:i + 1], h_tm[:, i % 2, :], axis=AX,
                                 apply_absolute_value=True)
                    v.tensor_scalar_max(s_a2[:, i:i + 1], s_a2[:, i:i + 1], 1e-5)
                    v.tensor_scalar(s_c2[:, i:i + 1], s_a2[:, i:i + 1], -c127, None,
                                    ALU.mult, ALU.bypass)
                    v.tensor_tensor(s_c2[:, i:i + 1], s_c2[:, i:i + 1],
                                    wm_b[:, 2:3], ALU.mult)
                    emit_recip_newton(v, s_a2[:, i:i + 1], s_a2[:, i:i + 1],
                                      s_sum[:, i:i + 1], s_sumsq[:, i:i + 1])
                    v.tensor_scalar_mul(s_a2[:, i:i + 1], s_a2[:, i:i + 1], 127.0)
                    if i >= 1:
                        shst.wait(v, i - 1)
                    v.tensor_scalar_mul(h_tm[:, i % 2, 0:H // 2],
                                        h_tm[:, i % 2, 0:H // 2], s_a2[:, i:i + 1])
                    v.tensor_scalar_mul(h_tm[:, i % 2, H // 2:H],
                                        h_tm[:, i % 2, H // 2:H], s_a2[:, i:i + 1])
                    v.drain().then_inc(sqm, 1)
                    for pairq in range(2):
                        q0 = slice((2 * pairq) * (H // 4), (2 * pairq + 1) * (H // 4))
                        q1 = slice((2 * pairq + 1) * (H // 4), (2 * pairq + 2) * (H // 4))
                        v.wait_ge(sqc, 5 * i + 3 + 2 * pairq)
                        emit_round_correct2(v, (inth[:, 0, q0], inth[:, 0, q1]),
                                            (h_tm[:, i % 2, q0], h_tm[:, i % 2, q1]),
                                            (b_nf1s[2 * pairq][:, 0, :],
                                             b_nf1s[2 * pairq + 1][:, 0, :]),
                                            -128.0, 127.0)
                        v.drain().then_inc(sqp, 1)
                    v.raw.nop().then_inc(sqh, 1)

            @blk.scalar
            def _(sc):
                sc = DrainingEngine(sc)
                for i in range(NT):
                    sc.wait_ge(sqm, 2 * i + 1)
                    sc.activation(b_ni0[:, 0:D], xln[:], AF.Copy)
                    sc.activation(b_nf0[:, 0:D], b_ni0[:, 0:D],
                                  AF.Copy).then_inc(sqc, 1)
                    for j in range(NSL):
                        gidx = i * NSL + j
                        sc.wait_ge(sgmm, gidx + 1)
                        if gidx >= 2:
                            sc.wait_ge(sh, gidx - 1)
                        sc.activation(gsil[:, j % 2, :], gps[:, j % 2, :], AF.Silu,
                                      scale=s_cg[:, i:i + 1]).then_inc(ssil, 1)
                        sc.wait_ge(summ, gidx + 1)
                        if gidx >= 2:
                            sc.wait_ge(sh, gidx - 1)
                        sc.activation(usc[:, j % 2, :], ups[:, j % 2, :], AF.Copy,
                                      scale=s_cu[:, i:i + 1]).then_inc(suev, 1)
                    sc.wait_ge(sqm, 2 * i + 2)
                    for qq in range(4):
                        if qq == 2:
                            sc.wait_ge(sqp, 2 * i + 1)
                        hq = slice(qq * (H // 4), (qq + 1) * (H // 4))
                        sc.activation(b_ni0[:], h_tm[:, i % 2, hq], AF.Copy)
                        sc.activation(b_nf1s[qq][:, 0, :], b_ni0[:],
                                      AF.Copy).then_inc(sqc, 1)


            @blk.sync
            def _(sy_):
                for i in range(NT):
                    sy_.wait_ge(sxln, 2 * (i + 1))
                    if i >= 1:
                        sy_.wait_ge(sgmm, i * NSL)
                        sy_.wait_ge(summ, i * NSL)
                    sixf.inc(sy_.dma_start(
                        ixf.ap()[:, 0, :, :],
                        intx.ap()[:, 0, :].rearrange("p (c q) -> p c q", c=NDC),
                        transpose=True,
                    ), i)
                    sy_.wait_ge(sqh, i + 1)
                    shst.inc(sy_.dma_start(inth_scr.ap()[i * 128:(i + 1) * 128, :],
                                           inth[:, 0, :]), i)
                for k in range(2):
                    sy_.wait_ge(shst.sems[k], 16 * ((NT + 1 - k) // 2))

            @blk.tensor
            def _(t):
                for i in range(NT):
                    sixf.wait(t, i)
                    for j in range(NSL):
                        gidx = i * NSL + j
                        if gidx >= 2:
                            t.wait_ge(ssil, gidx - 1)
                        for c in range(NDC):
                            mm_ = t.matmul(gps[:, j % 2, :],
                                     ixf[:, 0, c, :],
                                     wgT.ap()[:, 4 * j:4 * j + 4, c, :],
                                     start=(c == 0), stop=(c == NDC - 1))
                        mm_.then_inc(sgmm, 1)
                        if gidx >= 2:
                            t.wait_ge(suev, gidx - 1)
                        for c in range(NDC):
                            mm_ = t.matmul(ups[:, j % 2, :],
                                     ixf[:, 0, c, :],
                                     wuT.ap()[:, 4 * j:4 * j + 4, c, :],
                                     start=(c == 0), stop=(c == NDC - 1))
                        mm_.then_inc(summ, 1)

        with contextlib.ExitStack() as es3:
            A3 = es3.enter_context
            wdT = A3(nc.sbuf_tensor("wdT", [128, NQ, D // NQ // 128, NHC, 128], BF16))
            with contextlib.ExitStack() as esq:
                pairs = []
                for q in range(NQ):
                    src = (wd.ap()[q * (D // NQ):(q + 1) * (D // NQ), :]
                           .rearrange("(c p) h -> p c h", p=128))
                    dst = wdT.ap()[:, q, :, :, :]
                    pairs.append((src, dst, 2))
                emit_quant_weights(esq.enter_context, pairs, "qd")

            ihf = A3(nc.sbuf_tensor("ihf", [128, 2, NHC, 128], BF16))
            y_sb = A3(nc.sbuf_tensor("y_sb", [128, 2, D], BF16))
            sihf = DmaSlots(nc, SA, "sihf", 2)
            sdmm = SA(nc.semaphore("sdmm"))
            sy2 = SA(nc.semaphore("sy2"))
            spsq = SA(nc.semaphore("spsq"))
            syst = DmaSlots(nc, SA, "syst", 2)
            dps = A3(nc.psum_tensor("dps", [128, 2, D], F32))
            blk = A3(nc.Block())

            @blk.sync
            def _(sy_):
                for i in range(NT):
                    if i >= 2:
                        sy_.wait_ge(sdmm, i - 1)
                    sihf.inc(sy_.dma_start(
                        ihf.ap()[:, i % 2, :, :],
                        inth_scr.ap()[i * 128:(i + 1) * 128, :]
                        .rearrange("p (c q) -> p c q", c=NHC),
                        transpose=True,
                    ), i)
                    sy_.wait_ge(sy2, i + 1)
                    syst.inc(sy_.dma_start(y_scr.ap()[i * 128:(i + 1) * 128, :],
                                           y_sb[:, i % 2, :]), i)
                for k in range(2):
                    sy_.wait_ge(syst.sems[k], 16 * ((NT + 1 - k) // 2))

            @blk.tensor
            def _(t):
                for i in range(NT):
                    sihf.wait(t, i)
                    if i >= 2:
                        t.wait_ge(sy2, i - 1)
                    for n2 in range(2):
                        for c in range(NHC):
                            mm_ = t.matmul(dps[:, i % 2, n2 * 512:(n2 + 1) * 512],
                                     ihf[:, i % 2, c, :],
                                     wdT.ap()[:, 2 * n2:2 * n2 + 2, :, c, :]
                                     .rearrange("p a b q -> p (a b) q"),
                                     start=(c == 0), stop=(c == NHC - 1))
                    mm_.then_inc(sdmm, 1)

            @blk.scalar
            def _(sc):
                sc = DrainingEngine(sc)
                for i in range(NT):
                    sc.wait_ge(sdmm, i + 1)
                    if i >= 2:
                        syst.wait(sc, i - 2)
                    sc.activation(y_sb[:, i % 2, :], dps[:, i % 2, :], AF.Copy,
                                  scale=s_c2[:, i:i + 1],
                                  accum_out=p_sum[:, i:i + 1]).then_inc(sy2, 1)
                    sc.activation(scr_sq[:, 0, :], y_sb[:, i % 2, :], AF.Square,
                                  accum_out=p_sumsq[:, i:i + 1]).then_inc(spsq, 1)

        with nc.Block() as blk:
            emit_stat_math_phase(blk, p_sum, p_sumsq, p_negm, p_var, p_rstd, NT, D)
        with nc.Block() as blk:
            emit_rstd_phase(blk, p_var, p_rstd, NT)
        with contextlib.ExitStack() as es4:
            emit_residual_phase(nc, es4.enter_context,
                                lambda i: x_in.ap()[i * 128:(i + 1) * 128, :],
                                lambda i: y_scr.ap()[i * 128:(i + 1) * 128, :],
                                lambda i: x_out.ap()[i * 128:(i + 1) * 128, :],
                                NT, xt, p_negm, p_rstd, SA=SA2)


def build_l0_ffn():
    nc0 = bass.Bass("TRN2", target_bir_lowering=False)
    nc = NcView(nc0, "b_")
    x_in = nc0.dram_tensor("x_in", [S_CORE, D], F32, kind="ExternalInput")
    wg = nc0.dram_tensor("wg", [4096, D], F32, kind="ExternalInput")
    wu = nc0.dram_tensor("wu", [4096, D], F32, kind="ExternalInput")
    wd = nc0.dram_tensor("wd", [D, 4096], F32, kind="ExternalInput")
    x_out = nc0.dram_tensor("x_out", [S_CORE, D], F32, kind="ExternalOutput")
    with contextlib.ExitStack() as es:
        emit_l0_ffn(nc, lambda h: h, x_in, wg, wu, wd, x_out)
    return nc0




def emit_l0_attn(nc, SA, x_in, x_oth, wqd, wkd, wvd, wod, bod, x_out):
    NH = 16
    NP = 8
    NTF = 2 * NT
    EPS = 1e-6

    y_scr = nc.dram_tensor("y_scr", [S_CORE, D], BF16)

    def tile_src(t):
        if t < NT:
            return x_in.ap()[t * 128:(t + 1) * 128, :]
        return x_oth.ap()[(t - NT) * 128:(t - NT + 1) * 128, :]

    with contextlib.ExitStack() as es:
        A = es.enter_context
        wqT = A(nc.sbuf_tensor("wqT", [128, NDC, NDC, 128], BF16))
        wkT = A(nc.sbuf_tensor("wkT", [128, NDC, NDC, 128], BF16))
        wvT = A(nc.sbuf_tensor("wvT", [128, NDC, NDC, 128], BF16))
        woT = A(nc.sbuf_tensor("woT", [128, NDC, NDC, 128], BF16))
        bo_bf = A(nc.sbuf_tensor("bo_bf", [1, D], BF16))
        ones_r = A(nc.sbuf_tensor("ones_r", [1, 128], BF16))
        ones_c = A(nc.sbuf_tensor("ones_c", [128, 1], BF16))
        xt = A(nc.sbuf_tensor("xt", [128, 2, D], F32))
        scr_sq = A(nc.sbuf_tensor("scr_sq", [128, 2, D], F32))
        scr_cp = A(nc.sbuf_tensor("scr_cp", [128, 2, D], BF16))
        s_sum = A(nc.sbuf_tensor("s_sum", [128, NTF], F32))
        s_sumsq = A(nc.sbuf_tensor("s_sumsq", [128, NTF], F32))
        s_negm = A(nc.sbuf_tensor("s_negm", [128, NTF], F32))
        s_var = A(nc.sbuf_tensor("s_var", [128, NTF], F32))
        s_rstd = A(nc.sbuf_tensor("s_rstd", [128, NTF], F32))
        p_sum = A(nc.sbuf_tensor("p_sum", [128, NT], F32))
        p_sumsq = A(nc.sbuf_tensor("p_sumsq", [128, NT], F32))
        p_negm = A(nc.sbuf_tensor("p_negm", [128, NT], F32))
        p_var = A(nc.sbuf_tensor("p_var", [128, NT], F32))
        p_rstd = A(nc.sbuf_tensor("p_rstd", [128, NT], F32))
        dwg = SA(nc.semaphore("dwg"))
        dws = SA(nc.semaphore("dws"))

        gcnt = 0
        tcnt = 0
        with nc.sbuf_tensor("wraw", [128, NDC, D], BF16) as wraw:
            for wi, (w_dram, w_t) in enumerate(
                    ((wqd, wqT), (wkd, wkT), (wvd, wvT), (wod, woT))):
                with nc.Block() as blk:
                    @blk.gpsimd
                    def _(g, w_dram=w_dram, base=gcnt):
                        g.dma_start(
                            wraw[:],
                            w_dram.ap().rearrange("(c p) d -> p c d", p=128),
                        ).then_inc(dwg, 16)
                        g.wait_ge(dwg, base + 16)
                gcnt += 16
                with nc.Block() as blk:
                    @blk.sync
                    def _(sync, w_t=w_t, base=tcnt):
                        sync.dma_start(w_t[:], wraw[:], transpose=True).then_inc(dws, 16)
                        sync.wait_ge(dws, base + 16)
                tcnt += 16
        with nc.Block() as blk:
            @blk.gpsimd
            def _(g, base=gcnt):
                g.dma_start(bo_bf[:], bod.ap()).then_inc(dwg, 16)
                g.wait_ge(dwg, base + 16)
            @blk.vector
            def _(v):
                v.memset(ones_r[:], 1.0)
                v.memset(ones_c[:], 1.0)
        gcnt += 16

        sx = DmaSlots(nc, SA, "sx", 2)
        ssq = SA(nc.semaphore("ssq"))
        ssum = SA(nc.semaphore("ssum"))
        with nc.Block() as blk:
            emit_ln_stats_phase(blk, tile_src, NTF, xt, sx, ssq, ssum,
                                s_sumsq, s_sum, scr_sq, scr_cp)
        with nc.Block() as blk:
            emit_stat_math_phase(blk, s_sum, s_sumsq, s_negm, s_var, s_rstd, NTF, D)
        with nc.Block() as blk:
            emit_rstd_phase(blk, s_var, s_rstd, NTF)

        xf_own = A(nc.sbuf_tensor("xf_own", [128, NT, NDC, 128], BF16))
        qf_fm = A(nc.sbuf_tensor("qf_fm", [128, NDC, S_CORE], BF16))
        kv2 = A(nc.sbuf_tensor("kv2", [128, NP, 128], BF16))
        ks_bd = A(nc.sbuf_tensor("ks_bd", [128, NH], BF16))

        with contextlib.ExitStack() as e3:
            A3 = e3.enter_context
            xln = A3(nc.sbuf_tensor("xln", [128, 2, D], BF16))
            xf = A3(nc.sbuf_tensor("xf", [128, 2, NDC, 128], BF16))
            scrA = A3(nc.sbuf_tensor("scrA", [128, 2, D], F32))
            scrB = A3(nc.sbuf_tensor("scrB", [128, 2, D], F32))
            scrC = A3(nc.sbuf_tensor("scrC", [128, 2, D], F32))
            kf_t = A3(nc.sbuf_tensor("kf_t", [128, 2, D], BF16))
            v_t = A3(nc.sbuf_tensor("v_t", [128, 2, D], BF16))
            sxt = DmaSlots(nc, SA, "sxt", 2)
            sxln = SA(nc.semaphore("sxln"))
            sxf = DmaSlots(nc, SA, "sxf", 2)
            skmm = SA(nc.semaphore("skmm"))
            svmm = SA(nc.semaphore("svmm"))
            sminev = SA(nc.semaphore("sminev"))
            skev = SA(nc.semaphore("skev"))
            sExp = SA(nc.semaphore("sExp"))
            skf = SA(nc.semaphore("skf"))
            svev = SA(nc.semaphore("svev"))
            skvmm = SA(nc.semaphore("skvmm"))
            kps = A3(nc.psum_tensor("kps", [128, D], F32))
            vps = A3(nc.psum_tensor("vps", [128, D], F32))
            kvps = A3(nc.psum_tensor("kvps", [128, NP, 128], F32))
            ksps = A3(nc.psum_tensor("ksps", [128, NDC], F32))
            bstk = contextlib.ExitStack()
            blk = bstk.enter_context(nc.Block())

            @blk.gpsimd
            def _(g):
                for t in range(NTF):
                    if t >= 2:
                        g.wait_ge(sxln, t - 1)
                    sxt.inc(g.dma_start(xt[:, t % 2, :], tile_src(t)), t)

            @blk.vector
            def _(v):
                v = DrainingEngine(v)
                for t in range(NTF):
                    sxt.wait(v, t)
                    if t >= 2:
                        sxf.wait(v, t - 2)
                    v.tensor_scalar(xln[:, t % 2, :], xt[:, t % 2, :],
                                    s_negm[:, t:t + 1], s_rstd[:, t:t + 1],
                                    ALU.add, ALU.mult).then_inc(sxln, 1)
                    v.wait_ge(skmm, t + 1)
                    if t >= 2:
                        v.wait_ge(sExp, t - 1)
                    v.tensor_copy(scrC[:, t % 2, :], kps[:]).then_inc(skev, 1)
                    v.tensor_scalar_min(scrA[:, t % 2, :], scrC[:, t % 2, :],
                                        0.0).then_inc(sminev, 1)
                    v.tensor_scalar_max(scrC[:, t % 2, :], scrC[:, t % 2, :], 0.0)
                    v.wait_ge(sExp, t + 1)
                    if t >= 2:
                        v.wait_ge(skvmm, t - 1)
                    v.tensor_tensor(kf_t[:, t % 2, :], scrB[:, t % 2, :],
                                    scrC[:, t % 2, :], ALU.add).then_inc(skf, 1)
                    v.wait_ge(svmm, t + 1)
                    v.tensor_copy(v_t[:, t % 2, :], vps[:]).then_inc(svev, 1)

            @blk.scalar
            def _(sc):
                sc = DrainingEngine(sc)
                for t in range(NTF):
                    sc.wait_ge(sminev, t + 1)
                    if t >= 2:
                        sc.wait_ge(skf, t - 1)
                    sc.activation(scrB[:, t % 2, :], scrA[:, t % 2, :],
                                  AF.Exp).then_inc(sExp, 1)

            @blk.sync
            def _(sy_):
                for t in range(NTF):
                    sy_.wait_ge(sxln, t + 1)
                    dst = (xf_own.ap()[:, t, :, :] if t < NT
                           else xf.ap()[:, t % 2, :, :])
                    sxf.inc(sy_.dma_start(
                        dst,
                        xln.ap()[:, t % 2, :].rearrange("p (c q) -> p c q", c=NDC),
                        transpose=True,
                    ), t)

            @blk.tensor
            def _(t_):
                for t in range(NTF):
                    sxf.wait(t_, t)
                    xfv = (xf_own.ap()[:, t, :, :] if t < NT
                           else xf.ap()[:, t % 2, :, :])
                    if t >= 1:
                        t_.wait_ge(skev, t)
                    for j in range(2):
                        for c in range(NDC):
                            mm_ = t_.matmul(kps[:, j * 512:(j + 1) * 512],
                                      xfv[:, c, :],
                                      wkT.ap()[:, 4 * j:4 * j + 4, c, :],
                                      start=(c == 0), stop=(c == NDC - 1))
                    mm_.then_inc(skmm, 1)
                    if t >= 1:
                        t_.wait_ge(svev, t)
                    for j in range(2):
                        for c in range(NDC):
                            mm_ = t_.matmul(vps[:, j * 512:(j + 1) * 512],
                                      xfv[:, c, :],
                                      wvT.ap()[:, 4 * j:4 * j + 4, c, :],
                                      start=(c == 0), stop=(c == NDC - 1))
                    mm_.then_inc(svmm, 1)
                    if t >= 1:
                        tp = t - 1
                        t_.wait_ge(skf, tp + 1)
                        t_.wait_ge(svev, tp + 1)
                        for pp in range(NP):
                            mm_ = t_.matmul(kvps[:, pp, :],
                                      kf_t[:, tp % 2, pp * 128:(pp + 1) * 128],
                                      v_t[:, tp % 2, pp * 128:(pp + 1) * 128],
                                      start=(tp == 0 and pp % 4 == 0),
                                      stop=(tp == NTF - 1),
                                      skip_group_check=True)
                        for c in range(NDC):
                            mm_ = t_.matmul(ksps[:, c:c + 1],
                                      kf_t[:, tp % 2, c * 128:(c + 1) * 128],
                                      ones_c[:],
                                      start=(tp == 0 and c == 0),
                                      stop=(tp == NTF - 1),
                                      skip_group_check=True)
                        mm_.then_inc(skvmm, 1)
                tp = NTF - 1
                t_.wait_ge(skf, tp + 1)
                t_.wait_ge(svev, tp + 1)
                for pp in range(NP):
                    mm_ = t_.matmul(kvps[:, pp, :],
                              kf_t[:, tp % 2, pp * 128:(pp + 1) * 128],
                              v_t[:, tp % 2, pp * 128:(pp + 1) * 128],
                              start=False, stop=True, skip_group_check=True)
                for c in range(NDC):
                    mm_ = t_.matmul(ksps[:, c:c + 1],
                              kf_t[:, tp % 2, c * 128:(c + 1) * 128],
                              ones_c[:],
                              start=False, stop=True, skip_group_check=True)
                mm_.then_inc(skvmm, 1)

            bstk.close()
            blk2 = bstk.enter_context(nc.Block())

            @blk2.vector
            def _(v):
                v = DrainingEngine(v)
                v.memset(kv2[:], 0.0)
                v.memset(ks_bd[:], 0.0)
                for pp in range(NP):
                    v.tensor_copy(kv2[0:64, pp, 0:64], kvps[0:64, pp, 0:64])
                    v.tensor_copy(kv2[64:128, pp, 64:128], kvps[64:128, pp, 64:128])
                    v.tensor_copy(ks_bd[0:64, 2 * pp:2 * pp + 1], ksps[0:64, pp:pp + 1])
                    v.tensor_copy(ks_bd[64:128, 2 * pp + 1:2 * pp + 2],
                                  ksps[64:128, pp:pp + 1])
            bstk.close()

        with contextlib.ExitStack() as e4:
            A4 = e4.enter_context
            scrA = A4(nc.sbuf_tensor("q_scrA", [128, 2, 512], F32))
            scrB = A4(nc.sbuf_tensor("q_scrB", [128, 2, 512], F32))
            scrC = A4(nc.sbuf_tensor("q_scrC", [128, 2, 512], F32))
            sqmm = SA(nc.semaphore("sqmm"))
            sminev = SA(nc.semaphore("q_sminev"))
            sqev = SA(nc.semaphore("q_sqev"))
            sExp = SA(nc.semaphore("q_sExp"))
            sqf = SA(nc.semaphore("q_sqf"))
            qps = A4(nc.psum_tensor("qps", [128, 2, 512], F32))
            blk = A4(nc.Block())
            NSLC = S_CORE // 512

            @blk.tensor
            def _(t_):
                k = 0
                for co in range(NDC):
                    for st in range(NSLC):
                        if k >= 2:
                            t_.wait_ge(sqev, k - 1)
                        for ci in range(NDC):
                            mm_ = t_.matmul(qps[:, k % 2, :],
                                      wqT.ap()[:, co, ci, :],
                                      xf_own.ap()[:, 4 * st:4 * st + 4, ci, :],
                                      start=(ci == 0), stop=(ci == NDC - 1))
                        mm_.then_inc(sqmm, 1)
                        k += 1

            @blk.vector
            def _(v):
                v = DrainingEngine(v)
                k = 0
                for co in range(NDC):
                    for st in range(NSLC):
                        v.wait_ge(sqmm, k + 1)
                        if k >= 2:
                            v.wait_ge(sExp, k - 1)
                        v.tensor_copy(scrC[:, k % 2, :], qps[:, k % 2, :]).then_inc(sqev, 1)
                        v.tensor_scalar_min(scrA[:, k % 2, :], scrC[:, k % 2, :],
                                            0.0).then_inc(sminev, 1)
                        v.tensor_scalar_max(scrC[:, k % 2, :], scrC[:, k % 2, :], 0.0)
                        v.wait_ge(sExp, k + 1)
                        v.tensor_tensor(qf_fm[:, co, st * 512:(st + 1) * 512],
                                        scrB[:, k % 2, :], scrC[:, k % 2, :],
                                        ALU.add).then_inc(sqf, 1)
                        k += 1

            @blk.scalar
            def _(sc):
                sc = DrainingEngine(sc)
                for k in range(NDC * NSLC):
                    sc.wait_ge(sminev, k + 1)
                    if k >= 2:
                        sc.wait_ge(sqf, k - 1)
                    sc.activation(scrB[:, k % 2, :], scrA[:, k % 2, :],
                                  AF.Exp).then_inc(sExp, 1)

        with contextlib.ExitStack() as e5:
            A5 = e5.enter_context
            att = A5(nc.sbuf_tensor("att", [128, 2, D], BF16))
            af = A5(nc.sbuf_tensor("af", [128, 2, NDC, 128], BF16))
            dnv = A5(nc.sbuf_tensor("dnv", [128, 2, NH], F32))
            dscr = A5(nc.sbuf_tensor("dscr", [128, 2 * NH], F32))
            y_sb = A5(nc.sbuf_tensor("y_sb", [128, 2, D], BF16))
            sden = SA(nc.semaphore("sden"))
            sdev = SA(nc.semaphore("sdev"))
            satt = SA(nc.semaphore("satt"))
            saf = DmaSlots(nc, SA, "saf", 2)
            sop = SA(nc.semaphore("sop"))
            sy2 = SA(nc.semaphore("sy2"))
            spsq = SA(nc.semaphore("spsq"))
            syst = DmaSlots(nc, SA, "syst", 2)
            dps_d = A5(nc.psum_tensor("dps_d", [128, 2, 512], F32))
            qk_ps = A5(nc.psum_tensor("qk_ps", [128, 2, NP, 128], F32))
            ops = A5(nc.psum_tensor("ops", [128, D], F32))
            blk = A5(nc.Block())

            @blk.tensor
            def _(t_):
                for i in range(NT):
                    sl = slice(i * 128, (i + 1) * 128)
                    if i >= 2:
                        t_.wait_ge(sdev, i - 1)
                        t_.wait_ge(satt, i - 1)
                    for c in range(NDC):
                        mm_ = t_.matmul(dps_d[:, i % 2, 2 * c:2 * c + 2],
                                  qf_fm[:, c, sl],
                                  ks_bd[:, 2 * c:2 * c + 2],
                                  start=(c == 0), stop=True,
                                  skip_group_check=True)
                        mm_ = t_.matmul(qk_ps[:, i % 2, c, :],
                                  qf_fm[:, c, sl],
                                  kv2[:, c, :],
                                  start=True, stop=True)
                    mm_.then_inc(sden, 1)
                    if i >= 1:
                        ip = i - 1
                        saf.wait(t_, ip)
                        if ip >= 1:
                            t_.wait_ge(sy2, ip)
                        for j in range(2):
                            mm_ = t_.matmul(ops[:, j * 512:(j + 1) * 512],
                                      ones_r[:], bo_bf.ap()[0:1, j * 512:(j + 1) * 512],
                                      start=True, stop=False)
                            for c in range(NDC):
                                mm_ = t_.matmul(ops[:, j * 512:(j + 1) * 512],
                                          af[:, ip % 2, c, :],
                                          woT.ap()[:, 4 * j:4 * j + 4, c, :],
                                          start=False, stop=(c == NDC - 1))
                        mm_.then_inc(sop, 1)
                ip = NT - 1
                saf.wait(t_, ip)
                t_.wait_ge(sy2, ip)
                for j in range(2):
                    mm_ = t_.matmul(ops[:, j * 512:(j + 1) * 512],
                              ones_r[:], bo_bf.ap()[0:1, j * 512:(j + 1) * 512],
                              start=True, stop=False)
                    for c in range(NDC):
                        mm_ = t_.matmul(ops[:, j * 512:(j + 1) * 512],
                                  af[:, ip % 2, c, :],
                                  woT.ap()[:, 4 * j:4 * j + 4, c, :],
                                  start=False, stop=(c == NDC - 1))
                mm_.then_inc(sop, 1)

            @blk.vector
            def _(v):
                v = DrainingEngine(v)
                for i in range(NT):
                    v.wait_ge(sden, i + 1)
                    v.tensor_scalar_add(dnv[:, i % 2, :], dps_d[:, i % 2, 0:NH], EPS)
                    emit_recip_newton(v, dnv[:, i % 2, :], dnv[:, i % 2, :],
                                      dscr[:, 0:NH], dscr[:, NH:2 * NH]
                                      ).then_inc(sdev, 1)
                    if i >= 2:
                        saf.wait(v, i - 2)
                    v.tensor_tensor(
                        att.ap()[:, i % 2, :].rearrange("p (c t q) -> p c t q", c=NP, t=2),
                        qk_ps.ap()[:, i % 2, :, :].rearrange("p c (t q) -> p c t q", t=2),
                        dnv.ap()[:, i % 2, :].rearrange("p (c t) -> p c t", c=NP)
                        .unsqueeze(3).broadcast_to([128, NP, 2, 64]),
                        ALU.mult).then_inc(satt, 1)
                    v.wait_ge(sop, i + 1)
                    if i >= 2:
                        syst.wait(v, i - 2)
                        v.wait_ge(spsq, i - 1)
                    v.tensor_scalar(y_sb[:, i % 2, :], ops[:], 1.0, None,
                                    ALU.mult, ALU.add,
                                    accum_out=p_sum[:, i:i + 1]).then_inc(sy2, 1)

            @blk.scalar
            def _(sc):
                sc = DrainingEngine(sc)
                for i in range(NT):
                    sc.wait_ge(sy2, i + 1)
                    sc.activation(scr_sq[:, i % 2, :], y_sb[:, i % 2, :], AF.Square,
                                  accum_out=p_sumsq[:, i:i + 1]).then_inc(spsq, 1)

            @blk.sync
            def _(sy_):
                for i in range(NT):
                    sy_.wait_ge(satt, i + 1)
                    saf.inc(sy_.dma_start(
                        af.ap()[:, i % 2, :, :],
                        att.ap()[:, i % 2, :].rearrange("p (c q) -> p c q", c=NDC),
                        transpose=True,
                    ), i)
                    sy_.wait_ge(sy2, i + 1)
                    syst.inc(sy_.dma_start(y_scr.ap()[i * 128:(i + 1) * 128, :],
                                           y_sb[:, i % 2, :]), i)
                for k in range(2):
                    sy_.wait_ge(syst.sems[k], 16 * ((NT + 1 - k) // 2))

        with nc.Block() as blk:
            emit_stat_math_phase(blk, p_sum, p_sumsq, p_negm, p_var, p_rstd, NT, D)
        with nc.Block() as blk:
            emit_rstd_phase(blk, p_var, p_rstd, NT)
        with contextlib.ExitStack() as es6:
            emit_residual_phase(nc, es6.enter_context,
                                lambda i: x_in.ap()[i * 128:(i + 1) * 128, :],
                                lambda i: y_scr.ap()[i * 128:(i + 1) * 128, :],
                                lambda i: x_out.ap()[i * 128:(i + 1) * 128, :],
                                NT, xt, p_negm, p_rstd, SA=SA)


def build_l0_attn():
    nc0 = bass.Bass("TRN2", target_bir_lowering=False)
    nc = NcView(nc0, "a_")
    x_in = nc0.dram_tensor("x_in", [S_CORE, D], F32, kind="ExternalInput")
    x_oth = nc0.dram_tensor("x_oth", [S_CORE, D], F32, kind="ExternalInput")
    wqd = nc0.dram_tensor("wq", [D, D], F32, kind="ExternalInput")
    wkd = nc0.dram_tensor("wk", [D, D], F32, kind="ExternalInput")
    wvd = nc0.dram_tensor("wv", [D, D], F32, kind="ExternalInput")
    wod = nc0.dram_tensor("wo", [D, D], F32, kind="ExternalInput")
    bod = nc0.dram_tensor("bo", [1, D], F32, kind="ExternalInput")
    x_out = nc0.dram_tensor("x_out", [S_CORE, D], F32, kind="ExternalOutput")
    with contextlib.ExitStack() as es:
        emit_l0_attn(nc, lambda h: h, x_in, x_oth, wqd, wkd, wvd, wod, bod, x_out)
    return nc0





def emit_l1_attn(nc, SA, x_in, x_oth, wds, bds, x_out, fused=False,
                  replica_groups=None):
    SA2 = SA
    NTF = 2 * NT
    SM = 1.0 / 32.0
    w1d, w2d, w3d, w4d = wds
    b1d, b2d, b3d, b4d = bds

    y_scr = nc.dram_tensor("y_scr", [S_CORE, D], BF16)
    q_scr = nc.dram_tensor("q_scr", [NT, 128, NDC * 128], BF16)
    if fused:
        cc_in = nc.dram_tensor("cc_in", [128, 2 * NT * 1024], BF16)
        cc_out = nc.dram_tensor("cc_out", [256, 2 * NT * 1024], BF16)

    def tile_src(t):
        if t < NT:
            return x_in.ap()[t * 128:(t + 1) * 128, :]
        return x_oth.ap()[(t - NT) * 128:(t - NT + 1) * 128, :]

    NTP = NT if fused else NTF
    with contextlib.ExitStack() as es:
        A = es.enter_context
        K_fm = A(nc.sbuf_tensor("K_fm", [128, NTF, NDC, 128], BF16))
        V_tm = A(nc.sbuf_tensor("V_tm", [128, NTF, D], BF16))
        b_bf = A(nc.sbuf_tensor("b_bf", [1, 4, D], BF16))
        ones_r = A(nc.sbuf_tensor("ones_r", [1, 128], BF16))
        xt = A(nc.sbuf_tensor("xt", [128, 2, D], F32))
        scr_sq = A(nc.sbuf_tensor("scr_sq", [128, 1, D], F32))
        scr_cp = A(nc.sbuf_tensor("scr_cp", [128, 1, D], BF16))
        s_sum = A(nc.sbuf_tensor("s_sum", [128, NTF], F32))
        s_sumsq = A(nc.sbuf_tensor("s_sumsq", [128, NTF], F32))
        s_negm = A(nc.sbuf_tensor("s_negm", [128, NTF], F32))
        s_var = A(nc.sbuf_tensor("s_var", [128, NTF], F32))
        s_rstd = A(nc.sbuf_tensor("s_rstd", [128, NTF], F32))
        p_sum = A(nc.sbuf_tensor("p_sum", [128, NT], F32))
        p_sumsq = A(nc.sbuf_tensor("p_sumsq", [128, NT], F32))
        p_negm = A(nc.sbuf_tensor("p_negm", [128, NT], F32))
        p_var = A(nc.sbuf_tensor("p_var", [128, NT], F32))
        p_rstd = A(nc.sbuf_tensor("p_rstd", [128, NT], F32))
        dwg = SA(nc.semaphore("dwg"))

        with nc.Block() as blk:
            @blk.gpsimd
            def _(g):
                for bi, bd in enumerate((b1d, b2d, b3d, b4d)):
                    g.dma_start(b_bf[0:1, bi, :], bd.ap()).then_inc(dwg, 16)
                g.wait_ge(dwg, 64)
            @blk.vector
            def _(v):
                v.memset(ones_r[:], 1.0)

        sx = DmaSlots(nc, SA, "sx", 2)
        ssq = SA(nc.semaphore("ssq"))
        ssum = SA(nc.semaphore("ssum"))
        with nc.Block() as blk:
            emit_ln_stats_phase(blk, tile_src, NTP, xt, sx, ssq, ssum,
                                s_sumsq, s_sum, scr_sq, scr_cp)
        with nc.Block() as blk:
            emit_stat_math_phase(blk, s_sum, s_sumsq, s_negm, s_var, s_rstd, NTP, D)
        with nc.Block() as blk:
            emit_rstd_phase(blk, s_var, s_rstd, NTP)

        def emit_proj_phase(A3, tiles, q_mode):
            if q_mode:
                w1 = A3(nc.sbuf_tensor("w1b", [128, NDC, D], BF16))
                wlist = [(w1d, w1)]
            else:
                w2 = A3(nc.sbuf_tensor("w2b", [128, NDC, D], BF16))
                w3 = A3(nc.sbuf_tensor("w3b", [128, NDC, D], BF16))
                wlist = [(w2d, w2), (w3d, w3)]
            pref = "q" if q_mode else "kv"
            dwl = SA(nc.semaphore(pref + "_dwl"))
            with nc.Block() as blk:
                @blk.gpsimd
                def _(g):
                    for wi, (wd_, wt) in enumerate(wlist):
                        g.dma_start(wt[:],
                                    wd_.ap().rearrange("(c p) o -> p c o", p=128)
                                    ).then_inc(dwl, 16)
                    g.wait_ge(dwl, 16 * len(wlist))

            xln = A3(nc.sbuf_tensor(pref + "_xln", [128, 2, D], BF16))
            xf = A3(nc.sbuf_tensor(pref + "_xf", [128, 2, NDC, 128], BF16))
            sxt = DmaSlots(nc, SA, pref + "_sxt", 2)
            sxln = SA(nc.semaphore(pref + "_sxln"))
            sxf = DmaSlots(nc, SA, pref + "_sxf", 2)
            if q_mode:
                qtile = A3(nc.sbuf_tensor("qtile", [128, 2, NDC * 128], BF16))
                sqmm = SA(nc.semaphore("sqmm"))
                sqev = SA(nc.semaphore("sqev"))
                sqst = DmaSlots(nc, SA, "sqst", 2)
                qps = A3(nc.psum_tensor("qps", [128, 2, NDC, 128], F32))
            else:
                skmm = SA(nc.semaphore("skmm"))
                skev = SA(nc.semaphore("skev"))
                svmm = SA(nc.semaphore("svmm"))
                svev = SA(nc.semaphore("svev"))
                kps = A3(nc.psum_tensor("kps", [128, 2, NDC, 128], F32))
                vps = A3(nc.psum_tensor("vps", [128, D], F32))
            blk = A3(nc.Block())

            @blk.gpsimd
            def _(g):
                for t in tiles:
                    ti = tiles.index(t)
                    if ti >= 2:
                        g.wait_ge(sxln, ti - 1)
                    sxt.inc(g.dma_start(xt[:, ti % 2, :], tile_src(t)), ti)

            @blk.vector
            def _(v):
                v = DrainingEngine(v)
                for ti, t in enumerate(tiles):
                    sxt.wait(v, ti)
                    if ti >= 2:
                        sxf.wait(v, ti - 2)
                    v.tensor_scalar(xln[:, ti % 2, :], xt[:, ti % 2, :],
                                    s_negm[:, t:t + 1], s_rstd[:, t:t + 1],
                                    ALU.add, ALU.mult).then_inc(sxln, 1)
                    if q_mode:
                        v.wait_ge(sqmm, ti + 1)
                        if ti >= 2:
                            sqst.wait(v, ti - 2)
                        v.tensor_copy(qtile[:, ti % 2, :],
                                      qps.ap()[:, ti % 2, :, :]
                                      .rearrange("p a b -> p (a b)")).then_inc(sqev, 1)
                    else:
                        v.wait_ge(skmm, ti + 1)
                        v.tensor_copy(K_fm.ap()[:, t, :, :]
                                      .rearrange("p a b -> p (a b)"),
                                      kps.ap()[:, ti % 2, :, :]
                                      .rearrange("p a b -> p (a b)")).then_inc(skev, 1)
                        v.wait_ge(svmm, ti + 1)
                        v.tensor_copy(V_tm[:, t, :], vps[:]).then_inc(svev, 1)

            @blk.sync
            def _(sy_):
                for ti, t in enumerate(tiles):
                    sy_.wait_ge(sxln, ti + 1)
                    sxf.inc(sy_.dma_start(
                        xf.ap()[:, ti % 2, :, :],
                        xln.ap()[:, ti % 2, :].rearrange("p (c q) -> p c q", c=NDC),
                        transpose=True,
                    ), ti)
                    if q_mode:
                        sy_.wait_ge(sqev, ti + 1)
                        sqst.inc(sy_.dma_start(q_scr.ap()[t, :, :],
                                               qtile[:, ti % 2, :]), ti)
                if q_mode:
                    for k in range(2):
                        sy_.wait_ge(sqst.sems[k], 16 * ((len(tiles) + 1 - k) // 2))

            @blk.tensor
            def _(t_):
                for ti, t in enumerate(tiles):
                    sxf.wait(t_, ti)
                    if q_mode:
                        if ti >= 2:
                            t_.wait_ge(sqev, ti - 1)
                        for ce in range(NDC):
                            mm_ = t_.matmul(qps[:, ti % 2, ce, :],
                                      b_bf.ap()[0:1, 0, ce * 128:(ce + 1) * 128],
                                      ones_r[:], start=True, stop=False)
                            for ci in range(NDC):
                                mm_ = t_.matmul(qps[:, ti % 2, ce, :],
                                          w1[:, ci, ce * 128:(ce + 1) * 128],
                                          xf[:, ti % 2, ci, :],
                                          start=False, stop=(ci == NDC - 1))
                        mm_.then_inc(sqmm, 1)
                    else:
                        if ti >= 2:
                            t_.wait_ge(skev, ti - 1)
                        for ce in range(NDC):
                            mm_ = t_.matmul(kps[:, ti % 2, ce, :],
                                      b_bf.ap()[0:1, 1, ce * 128:(ce + 1) * 128],
                                      ones_r[:], start=True, stop=False)
                            for ci in range(NDC):
                                mm_ = t_.matmul(kps[:, ti % 2, ce, :],
                                          w2[:, ci, ce * 128:(ce + 1) * 128],
                                          xf[:, ti % 2, ci, :],
                                          start=False, stop=(ci == NDC - 1))
                        mm_.then_inc(skmm, 1)
                        if ti >= 1:
                            t_.wait_ge(svev, ti)
                        for j in range(2):
                            mm_ = t_.matmul(vps[:, j * 512:(j + 1) * 512], ones_r[:],
                                      b_bf.ap()[0:1, 2, j * 512:(j + 1) * 512],
                                      start=True, stop=False)
                            for ci in range(NDC):
                                mm_ = t_.matmul(vps[:, j * 512:(j + 1) * 512],
                                          xf[:, ti % 2, ci, :],
                                          w3[:, ci, j * 512:(j + 1) * 512],
                                          start=False, stop=(ci == NDC - 1))
                        mm_.then_inc(svmm, 1)

        with contextlib.ExitStack() as e1:
            emit_proj_phase(e1.enter_context, list(range(NTP)), False)
        if fused:
            ccst = SA(nc.semaphore("ccst"))
            ccld = SA(nc.semaphore("ccld"))
            ccdone = SA(nc.semaphore("ccdone"))
            KW = NT * 1024
            with nc.Block() as blk:
                @blk.sync
                def _(sy_):
                    sy_.dma_start(cc_in.ap()[:, 0:KW],
                                  K_fm.ap()[:, 0:NT, :, :]
                                  .rearrange("p a b c -> p (a b c)")).then_inc(ccst, 16)
                    sy_.dma_start(cc_in.ap()[:, KW:2 * KW],
                                  V_tm.ap()[:, 0:NT, :]
                                  .rearrange("p a b -> p (a b)")).then_inc(ccst, 16)
                    sy_.wait_ge(ccst, 32)
            with nc.Block() as blk:
                @blk.gpsimd
                def _(g):
                    g.collective_compute(
                        "AllGather", ALU.bypass,
                        replica_groups=replica_groups,
                        ins=[cc_in.ap().opt()],
                        outs=[cc_out.ap().opt()]).then_inc(ccdone, 1)
                    g.wait_ge(ccdone, 1)
            with nc.Block() as blk:
                @blk.sync
                def _(sy_):
                    for r in range(2):
                        sy_.dma_start(
                            K_fm.ap()[:, r * NT:(r + 1) * NT, :, :]
                            .rearrange("p a b c -> p (a b c)"),
                            cc_out.ap()[r * 128:(r + 1) * 128, 0:KW]).then_inc(ccld, 16)
                        sy_.dma_start(
                            V_tm.ap()[:, r * NT:(r + 1) * NT, :]
                            .rearrange("p a b -> p (a b)"),
                            cc_out.ap()[r * 128:(r + 1) * 128, KW:2 * KW]).then_inc(ccld, 16)
                    sy_.wait_ge(ccld, 64)
        with contextlib.ExitStack() as e1:
            emit_proj_phase(e1.enter_context, list(range(NT)), True)

        with contextlib.ExitStack() as e2:
            A2 = e2.enter_context
            w4 = A2(nc.sbuf_tensor("w4b", [128, NDC, D], BF16))
            dwl = SA(nc.semaphore("c2_dwl"))
            with nc.Block() as blk:
                @blk.gpsimd
                def _(g):
                    g.dma_start(w4[:], w4d.ap().rearrange("(c p) o -> p c o", p=128)
                                ).then_inc(dwl, 16)
                    g.wait_ge(dwl, 16)

            qt = A2(nc.sbuf_tensor("qt", [128, 2, NDC, 128], BF16))
            sc_sb = A2(nc.sbuf_tensor("sc_sb", [128, S], F32))
            mx8 = A2(nc.sbuf_tensor("mx8", [128, 8], F32))
            bias_c = A2(nc.sbuf_tensor("bias_c", [128, 1], F32))
            l4 = A2(nc.sbuf_tensor("l4", [128, 4], F32))
            rl = A2(nc.sbuf_tensor("rl", [128, 1], F32))
            rscr = A2(nc.sbuf_tensor("rscr", [128, 2], F32))
            P = A2(nc.sbuf_tensor("P", [128, 2, 1024], BF16))
            PT = A2(nc.sbuf_tensor("PT", [128, 2, 8, 128], BF16))
            att = A2(nc.sbuf_tensor("att", [128, 2, D], BF16))
            af = A2(nc.sbuf_tensor("af", [128, 2, NDC, 128], BF16))
            y_sb = A2(nc.sbuf_tensor("y_sb", [128, 2, D], BF16))
            sq = DmaSlots(nc, SA, "c2_sq", 2)
            ssc = SA(nc.semaphore("ssc"))
            ssce = SA(nc.semaphore("ssce"))
            sbias = SA(nc.semaphore("sbias"))
            sexp = SA(nc.semaphore("sexp"))
            spt = DmaSlots(nc, SA, "c2_spt", 2)
            samm = SA(nc.semaphore("samm"))
            sptd = SA(nc.semaphore("sptd"))
            srl = SA(nc.semaphore("srl"))
            satt = SA(nc.semaphore("satt"))
            saf = DmaSlots(nc, SA, "c2_saf", 2)
            sop = SA(nc.semaphore("sop"))
            sy2 = SA(nc.semaphore("sy2"))
            spsq = SA(nc.semaphore("spsq"))
            syst = DmaSlots(nc, SA, "c2_syst", 2)
            sc_ps = A2(nc.psum_tensor("sc_ps", [128, 2, 512], F32))
            aps = A2(nc.psum_tensor("aps", [128, D], F32))
            ops = A2(nc.psum_tensor("ops", [128, D], F32))
            blk = A2(nc.Block())

            @blk.gpsimd
            def _(g):
                for i in range(NT):
                    if i >= 2:
                        g.wait_ge(sptd, 4 * (i - 1))
                    sq.inc(g.dma_start(
                        qt.ap()[:, i % 2, :, :].rearrange("p a b -> p (a b)"),
                        q_scr.ap()[i, :, :]), i)

            @blk.tensor
            def _(t_):
                for i in range(NT):
                    sq.wait(t_, i)
                    for j in range(8):
                        gidx = i * 8 + j
                        if gidx >= 2:
                            t_.wait_ge(ssce, gidx - 1)
                        for ce in range(NDC):
                            mm_ = t_.matmul(sc_ps[:, j % 2, :],
                                      qt[:, i % 2, ce, :],
                                      K_fm.ap()[:, 4 * j:4 * j + 4, ce, :],
                                      start=(ce == 0), stop=(ce == NDC - 1))
                        mm_.then_inc(ssc, 1)
                    if i >= 1:
                        t_.wait_ge(satt, i)
                    for c in range(4):
                        spt.wait(t_, i * 4 + c)
                        for tt in range(8):
                            for j2 in range(2):
                                mm_ = t_.matmul(aps[:, j2 * 512:(j2 + 1) * 512],
                                          PT[:, c % 2, tt, :],
                                          V_tm[:, 8 * c + tt,
                                               j2 * 512:(j2 + 1) * 512],
                                          start=(c == 0 and tt == 0),
                                          stop=(c == 3 and tt == 7),
                                          skip_group_check=True)
                        mm_.then_inc(sptd, 1)
                    if i >= 1:
                        ip = i - 1
                        saf.wait(t_, ip)
                        if ip >= 1:
                            t_.wait_ge(sy2, ip)
                        for j in range(2):
                            mm_ = t_.matmul(ops[:, j * 512:(j + 1) * 512], ones_r[:],
                                      b_bf.ap()[0:1, 3, j * 512:(j + 1) * 512],
                                      start=True, stop=False)
                            for ce in range(NDC):
                                mm_ = t_.matmul(ops[:, j * 512:(j + 1) * 512],
                                          af[:, ip % 2, ce, :],
                                          w4[:, ce, j * 512:(j + 1) * 512],
                                          start=False, stop=(ce == NDC - 1))
                        mm_.then_inc(sop, 1)
                ip = NT - 1
                saf.wait(t_, ip)
                t_.wait_ge(sy2, ip)
                for j in range(2):
                    mm_ = t_.matmul(ops[:, j * 512:(j + 1) * 512], ones_r[:],
                              b_bf.ap()[0:1, 3, j * 512:(j + 1) * 512],
                              start=True, stop=False)
                    for ce in range(NDC):
                        mm_ = t_.matmul(ops[:, j * 512:(j + 1) * 512],
                                  af[:, ip % 2, ce, :],
                                  w4[:, ce, j * 512:(j + 1) * 512],
                                  start=False, stop=(ce == NDC - 1))
                mm_.then_inc(sop, 1)

            @blk.vector
            def _(v):
                v = DrainingEngine(v)

                def emit_yev(v, i):
                    v.wait_ge(sop, i + 1)
                    if i >= 2:
                        syst.wait(v, i - 2)
                        v.wait_ge(spsq, i - 1)
                    v.tensor_scalar(y_sb[:, i % 2, :], ops[:], 1.0, None,
                                    ALU.mult, ALU.add,
                                    accum_out=p_sum[:, i:i + 1]).then_inc(sy2, 1)
                for i in range(NT):
                    for j in range(8):
                        gidx = i * 8 + j
                        v.wait_ge(ssc, gidx + 1)
                        v.tensor_copy(sc_sb[:, j * 512:(j + 1) * 512],
                                      sc_ps[:, j % 2, :])
                        v.reduce_max(mx8[:, j:j + 1], sc_sb[:, j * 512:(j + 1) * 512],
                                     axis=AX).then_inc(ssce, 1)
                    v.reduce_max(bias_c[:], mx8[:], axis=AX)
                    v.tensor_scalar_mul(bias_c[:], bias_c[:], -SM)
                    v.drain().then_inc(sbias, 1)
                    v.wait_ge(sexp, 4 * (i + 1))
                    v.reduce_sum(rl[:], l4[:], axis=AX)
                    emit_recip_newton(v, rl[:], rl[:], rscr[:, 0:1],
                                      rscr[:, 1:2]).then_inc(srl, 1)
                    v.wait_ge(sptd, 4 * (i + 1))
                    if i >= 2:
                        saf.wait(v, i - 2)
                    v.tensor_scalar_mul(att[:, i % 2, :], aps[:], rl[:, 0:1]
                                        ).then_inc(satt, 1)
                    if i >= 1:
                        emit_yev(v, i - 1)
                if NT >= 1:
                    emit_yev(v, NT - 1)

            @blk.scalar
            def _(sc):
                sc = DrainingEngine(sc)
                def emit_sq(i):
                    sc.wait_ge(sy2, i + 1)
                    sc.activation(scr_sq[:, 0, :], y_sb[:, i % 2, :], AF.Square,
                                  accum_out=p_sumsq[:, i:i + 1]).then_inc(spsq, 1)
                for i in range(NT):
                    sc.wait_ge(sbias, i + 1)
                    for c in range(4):
                        if i * 4 + c >= 2:
                            spt.wait(sc, i * 4 + c - 2)
                        sc.activation(P[:, c % 2, :], sc_sb[:, c * 1024:(c + 1) * 1024],
                                      AF.Exp, bias=bias_c[:, 0:1], scale=SM,
                                      accum_out=l4[:, c:c + 1]).then_inc(sexp, 1)
                    if i >= 1:
                        emit_sq(i - 1)
                emit_sq(NT - 1)

            @blk.sync
            def _(sy_):
                def emit_af(i):
                    sy_.wait_ge(satt, i + 1)
                    saf.inc(sy_.dma_start(
                        af.ap()[:, i % 2, :, :],
                        att.ap()[:, i % 2, :].rearrange("p (c q) -> p c q", c=NDC),
                        transpose=True,
                    ), i)

                def emit_yst(i):
                    sy_.wait_ge(sy2, i + 1)
                    syst.inc(sy_.dma_start(y_scr.ap()[i * 128:(i + 1) * 128, :],
                                           y_sb[:, i % 2, :]), i)
                for i in range(NT):
                    for c in range(4):
                        sy_.wait_ge(sexp, i * 4 + c + 1)
                        if i * 4 + c >= 2:
                            sy_.wait_ge(sptd, i * 4 + c - 1)
                        spt.inc(sy_.dma_start(
                            PT.ap()[:, c % 2, :, :],
                            P.ap()[:, c % 2, :].rearrange("p (a b) -> p a b", a=NDC),
                            transpose=True,
                        ), i * 4 + c)
                    emit_af(i)
                    if i >= 1:
                        emit_yst(i - 1)
                emit_yst(NT - 1)
                for k in range(2):
                    sy_.wait_ge(syst.sems[k], 16 * ((NT + 1 - k) // 2))

        with nc.Block() as blk:
            emit_stat_math_phase(blk, p_sum, p_sumsq, p_negm, p_var, p_rstd, NT, D)
        with nc.Block() as blk:
            emit_rstd_phase(blk, p_var, p_rstd, NT)
        with contextlib.ExitStack() as es6:
            emit_residual_phase(nc, es6.enter_context,
                                lambda i: x_in.ap()[i * 128:(i + 1) * 128, :],
                                lambda i: y_scr.ap()[i * 128:(i + 1) * 128, :],
                                lambda i: x_out.ap()[i * 128:(i + 1) * 128, :],
                                NT, xt, p_negm, p_rstd, SA=SA2)


def build_l1_attn():
    nc0 = bass.Bass("TRN2", target_bir_lowering=False)
    nc = NcView(nc0, "c_")
    x_in = nc0.dram_tensor("x_in", [S_CORE, D], F32, kind="ExternalInput")
    x_oth = nc0.dram_tensor("x_oth", [S_CORE, D], F32, kind="ExternalInput")
    wds = [nc0.dram_tensor(f"w{k}", [D, D], F32, kind="ExternalInput") for k in range(1, 5)]
    bds = [nc0.dram_tensor(f"b{k}", [1, D], F32, kind="ExternalInput") for k in range(1, 5)]
    x_out = nc0.dram_tensor("x_out", [S_CORE, D], F32, kind="ExternalOutput")
    with contextlib.ExitStack() as es:
        emit_l1_attn(nc, lambda h: h, x_in, x_oth, wds, bds, x_out)
    return nc0




def build_fused():
    nc0 = bass.Bass("TRN2", target_bir_lowering=False)
    x_in = nc0.dram_tensor("x_in", [S_CORE, D], F32, kind="ExternalInput")
    x_oth = nc0.dram_tensor("x_oth", [S_CORE, D], F32, kind="ExternalInput")
    awq = nc0.dram_tensor("awq", [D, D], F32, kind="ExternalInput")
    awk = nc0.dram_tensor("awk", [D, D], F32, kind="ExternalInput")
    awv = nc0.dram_tensor("awv", [D, D], F32, kind="ExternalInput")
    awo = nc0.dram_tensor("awo", [D, D], F32, kind="ExternalInput")
    abo = nc0.dram_tensor("abo", [1, D], F32, kind="ExternalInput")
    bwg = nc0.dram_tensor("bwg", [4096, D], F32, kind="ExternalInput")
    bwu = nc0.dram_tensor("bwu", [4096, D], F32, kind="ExternalInput")
    bwd = nc0.dram_tensor("bwd", [D, 4096], F32, kind="ExternalInput")
    cws = [nc0.dram_tensor(f"cw{k}", [D, D], F32, kind="ExternalInput") for k in range(1, 5)]
    cbs = [nc0.dram_tensor(f"cb{k}", [1, D], F32, kind="ExternalInput") for k in range(1, 5)]
    dwg = nc0.dram_tensor("dwg", [3072, D], I32, kind="ExternalInput")
    dwu = nc0.dram_tensor("dwu", [3072, D], I32, kind="ExternalInput")
    dwd = nc0.dram_tensor("dwd", [D, 3072], I32, kind="ExternalInput")
    x_out = nc0.dram_tensor("x_out", [S_CORE, D], F32, kind="ExternalOutput")
    x1 = nc0.dram_tensor("x1", [S_CORE, D], F32)
    x2 = nc0.dram_tensor("x2", [S_CORE, D], F32)
    x3 = nc0.dram_tensor("x3", [S_CORE, D], F32)
    rg = [[0, 1], [2, 3], [4, 5], [6, 7]]

    def scoped_sa():
        bag = []

        def SA(h):
            bag.append(h)
            return h
        return bag, SA

    def release(bag):
        nc0.clear_and_free_semaphores(bag)
        nc0.all_engine_barrier()

    bag, SA = scoped_sa()
    emit_l0_attn(NcView(nc0, "a_"), SA, x_in, x_oth, awq, awk, awv, awo, abo, x1)
    release(bag)
    bag, SA = scoped_sa()
    emit_l0_ffn(NcView(nc0, "b_"), SA, x1, bwg, bwu, bwd, x2)
    release(bag)
    bag, SA = scoped_sa()
    emit_l1_attn(NcView(nc0, "c_"), SA, x2, None, cws, cbs, x3,
                 fused=True, replica_groups=rg)
    release(bag)
    bag, SA = scoped_sa()
    emit_l1_ffn(NcView(nc0, "d_"), SA, x3, dwg, dwu, dwd, x_out)
    return nc0


def _run_fused(inputs):
    nc = _get_nc("fused", build_fused)
    x = np.asarray(inputs["L0_attn_x"], dtype=np.float32)
    xs = _scatter_x(x)
    com = {
        "awq": np.ascontiguousarray(inputs["L0_attn_wq"], dtype=np.float32),
        "awk": np.ascontiguousarray(inputs["L0_attn_wk"], dtype=np.float32),
        "awv": np.ascontiguousarray(inputs["L0_attn_wv"], dtype=np.float32),
        "awo": np.ascontiguousarray(inputs["L0_attn_wo"], dtype=np.float32),
        "abo": np.ascontiguousarray(inputs["L0_attn_bo"], dtype=np.float32).reshape(1, D),
        "bwg": np.ascontiguousarray(inputs["L0_ffn_w_gate"], dtype=np.float32),
        "bwu": np.ascontiguousarray(inputs["L0_ffn_w_up"], dtype=np.float32),
        "bwd": np.ascontiguousarray(inputs["L0_ffn_w_down"], dtype=np.float32),
        "dwg": np.ascontiguousarray(inputs["L1_ffn_w_gate"], dtype=np.int32),
        "dwu": np.ascontiguousarray(inputs["L1_ffn_w_up"], dtype=np.int32),
        "dwd": np.ascontiguousarray(inputs["L1_ffn_w_down"], dtype=np.int32),
    }
    for k in range(1, 5):
        com[f"cw{k}"] = np.ascontiguousarray(inputs[f"L1_attn_W{k}"], dtype=np.float32)
        com[f"cb{k}"] = np.ascontiguousarray(inputs[f"L1_attn_b{k}"], dtype=np.float32).reshape(1, D)
    in_maps = []
    for c in range(8):
        m = dict(com)
        m["x_in"] = xs[c]
        m["x_oth"] = xs[c ^ 1]
        in_maps.append(m)
    res = run_bass_kernel_spmd(nc, in_maps, core_ids=list(range(8)))
    return _gather_x(res)



_NC_CACHE = {}


def _get_nc(name, builder):
    if name not in _NC_CACHE:
        _NC_CACHE[name] = builder()
    return _NC_CACHE[name]


def _scatter_x(x):
    out = []
    for core in range(8):
        b, half = core // 2, core % 2
        out.append(np.ascontiguousarray(
            x[b, half * S_CORE:(half + 1) * S_CORE, :], dtype=np.float32))
    return out


def _gather_x(res, key="x_out"):
    out = np.empty((B, S, D), dtype=np.float32)
    for core in range(8):
        b, half = core // 2, core % 2
        out[b, half * S_CORE:(half + 1) * S_CORE, :] = res.results[core][key]
    return out


def _run_l1_ffn(x, wg, wu, wd):
    nc = _get_nc("l1_ffn", build_l1_ffn)
    wg = np.ascontiguousarray(wg, dtype=np.int32)
    wu = np.ascontiguousarray(wu, dtype=np.int32)
    wd = np.ascontiguousarray(wd, dtype=np.int32)
    xs = _scatter_x(x)
    in_maps = [{"x_in": xs[c], "wg": wg, "wu": wu, "wd": wd} for c in range(8)]
    res = run_bass_kernel_spmd(nc, in_maps, core_ids=list(range(8)))
    return _gather_x(res)


def _run_l0_ffn(x, wg, wu, wd):
    nc = _get_nc("l0_ffn", build_l0_ffn)
    wg = np.ascontiguousarray(wg, dtype=np.float32)
    wu = np.ascontiguousarray(wu, dtype=np.float32)
    wd = np.ascontiguousarray(wd, dtype=np.float32)
    xs = _scatter_x(x)
    in_maps = [{"x_in": xs[c], "wg": wg, "wu": wu, "wd": wd} for c in range(8)]
    res = run_bass_kernel_spmd(nc, in_maps, core_ids=list(range(8)))
    return _gather_x(res)


def _run_l0_attn(x, wq, wk, wv, wo, bo):
    nc = _get_nc("l0_attn", build_l0_attn)
    wq = np.ascontiguousarray(wq, dtype=np.float32)
    wk = np.ascontiguousarray(wk, dtype=np.float32)
    wv = np.ascontiguousarray(wv, dtype=np.float32)
    wo = np.ascontiguousarray(wo, dtype=np.float32)
    bo = np.ascontiguousarray(bo, dtype=np.float32).reshape(1, D)
    xs = _scatter_x(x)
    in_maps = []
    for c in range(8):
        oth = c ^ 1
        in_maps.append({"x_in": xs[c], "x_oth": xs[oth], "wq": wq, "wk": wk,
                        "wv": wv, "wo": wo, "bo": bo})
    res = run_bass_kernel_spmd(nc, in_maps, core_ids=list(range(8)))
    return _gather_x(res)


def _run_l1_attn(x, w1, b1, w2, b2, w3, b3, w4, b4):
    nc = _get_nc("l1_attn", build_l1_attn)
    ws = [np.ascontiguousarray(w, dtype=np.float32) for w in (w1, w2, w3, w4)]
    bs = [np.ascontiguousarray(b, dtype=np.float32).reshape(1, D) for b in (b1, b2, b3, b4)]
    xs = _scatter_x(x)
    in_maps = []
    for c in range(8):
        m = {"x_in": xs[c], "x_oth": xs[c ^ 1]}
        for k in range(4):
            m[f"w{k+1}"] = ws[k]
            m[f"b{k+1}"] = bs[k]
        in_maps.append(m)
    res = run_bass_kernel_spmd(nc, in_maps, core_ids=list(range(8)))
    return _gather_x(res)


def kernel(**inputs):
    return _run_fused(inputs)


# version 11
# speedup vs baseline: 1.0091x; 1.0091x over previous
import contextlib

import numpy as np
import ml_dtypes
import concourse.bass as bass
import concourse.mybir as mybir
from concourse.bass_utils import run_bass_kernel_spmd

F32 = mybir.dt.float32
BF16 = mybir.dt.bfloat16
I32 = mybir.dt.int32
AF = mybir.ActivationFunctionType
ALU = mybir.AluOpType
AX = mybir.AxisListType.X

B = 4
S = 4096
D = 1024
S_CORE = 2048
NT = S_CORE // 128
NDC = D // 128
MAGIC = float(np.float32(1.5 * 2 ** 23))
LN_EPS = 1e-5



_DRAIN_OPS = frozenset("""
tensor_scalar tensor_single_scalar tensor_scalar_add tensor_scalar_sub
tensor_scalar_mul tensor_scalar_max tensor_scalar_min tensor_relu
tensor_tensor tensor_add tensor_sub tensor_mul tensor_max tensor_copy
tensor_reduce reduce_max reduce_sum scalar_tensor_tensor tensor_tensor_scan
memset memzero activation sign square sqrt mul copy add reciprocal select
bn_stats bn_aggr transpose iota
""".split())


class DrainingEngine:

    def __init__(self, eng):
        self.raw = eng
        self._written = set()
        self._read = set()

    def drain(self, *a, **k):
        self._written.clear()
        self._read.clear()
        return self.raw.drain(*a, **k)

    @staticmethod
    def _nm(x):
        t = getattr(x, "tensor", None)
        if t is not None:
            return getattr(t, "name", None)
        if hasattr(x, "ap"):
            return getattr(x, "name", None)
        return None

    def __getattr__(self, name):
        attr = getattr(self.raw, name)
        if name in _DRAIN_OPS:
            def wrapped(*a, **k):
                outs, ins = [], []
                for idx, x in enumerate(a):
                    n_ = self._nm(x)
                    if n_:
                        (outs if idx == 0 else ins).append(n_)
                for kname, x in k.items():
                    n_ = self._nm(x)
                    if n_:
                        (outs if kname == "accum_out" else ins).append(n_)
                need = (any(n_ in self._written for n_ in ins)
                        or any(n_ in self._written or n_ in self._read
                               for n_ in outs))
                if need:
                    self.raw.drain()
                    self._written.clear()
                    self._read.clear()
                attr_r = attr(*a, **k)
                self._written.update(outs)
                self._read.update(ins)
                return attr_r
            return wrapped
        return attr


class NcView:

    def __init__(self, nc, pre):
        self._nc = nc
        self._pre = pre

    def sbuf_tensor(self, name, *a, **k):
        return self._nc.sbuf_tensor(self._pre + name, *a, **k)

    def psum_tensor(self, name, *a, **k):
        return self._nc.psum_tensor(self._pre + name, *a, **k)

    def semaphore(self, name, *a, **k):
        return self._nc.alloc_semaphore(name=self._pre + name)

    def dram_tensor(self, name, *a, **k):
        return self._nc.dram_tensor(self._pre + name, *a, **k)

    def __getattr__(self, name):
        return getattr(self._nc, name)


class DmaSlots:

    def __init__(self, nc, A, name, n):
        self.sems = [A(nc.semaphore(f"{name}{k}")) for k in range(n)]
        self.n = n

    def inc(self, inst, i):
        return inst.then_inc(self.sems[i % self.n], 16)

    def wait(self, eng, i):
        eng.wait_ge(self.sems[i % self.n], 16 * (i // self.n + 1))


def emit_weight_transpose(sync, dst, src, dsem, dcnt):
    ca = src.shape[1]
    cb = dst.shape[1]
    assert src.shape[2] == cb * 128 and dst.shape[2] == ca * 128
    for icb in range(cb):
        sync.dma_start(
            dst.ap()[:, icb, :].rearrange("p (ca q) -> p ca q", ca=ca),
            src.ap()[:, :, icb * 128:(icb + 1) * 128],
            transpose=True,
        ).then_inc(dsem, 16)
        dcnt += 16
    return dcnt


def emit_round_clip(v, out_ap, t_ap, ni_ap, nf_ap, lo, hi):
    v.tensor_copy(ni_ap, t_ap)
    v.tensor_copy(nf_ap, ni_ap)
    v.tensor_tensor(t_ap, t_ap, nf_ap, ALU.subtract)
    v.scalar_tensor_tensor(nf_ap, t_ap, 0.5, nf_ap, ALU.is_ge, ALU.add)
    v.scalar_tensor_tensor(nf_ap, t_ap, -0.5, nf_ap, ALU.is_ge, ALU.add)
    v.tensor_scalar(nf_ap, nf_ap, -1.0, hi, ALU.add, ALU.min)
    return v.tensor_scalar(out_ap, nf_ap, lo, None, ALU.max)


def emit_recip_newton(v, out_ap, x_ap, scrA, scrB):
    v.reciprocal(scrA, x_ap)
    v.tensor_tensor(scrB, x_ap, scrA, ALU.mult)
    v.tensor_scalar(scrB, scrB, -1.0, 2.0, ALU.mult, ALU.add)
    return v.tensor_tensor(out_ap, scrA, scrB, ALU.mult)


def emit_round_correct(v, out_ap, t_ap, nf_ap, lo, hi):
    v.tensor_tensor(t_ap, t_ap, nf_ap, ALU.subtract)
    v.scalar_tensor_tensor(nf_ap, t_ap, 0.5, nf_ap, ALU.is_ge, ALU.add)
    v.scalar_tensor_tensor(nf_ap, t_ap, -0.5, nf_ap, ALU.is_ge, ALU.add)
    v.tensor_scalar(nf_ap, nf_ap, -1.0, hi, ALU.add, ALU.min)
    return v.tensor_scalar(out_ap, nf_ap, lo, None, ALU.max)


def emit_round_correct2(v, outs, ts_, nfs, lo, hi):
    for j in range(2):
        v.tensor_tensor(ts_[j], ts_[j], nfs[j], ALU.subtract)
    for j in range(2):
        v.scalar_tensor_tensor(nfs[j], ts_[j], 0.5, nfs[j], ALU.is_ge, ALU.add)
    for j in range(2):
        v.scalar_tensor_tensor(nfs[j], ts_[j], -0.5, nfs[j], ALU.is_ge, ALU.add)
    for j in range(2):
        v.tensor_scalar(nfs[j], nfs[j], -1.0, hi, ALU.add, ALU.min)
    for j in range(2):
        v.tensor_scalar(outs[j], nfs[j], lo, None, ALU.max)


def emit_round_clip2(v, outs, ts_, nis, nfs, lo, hi):
    for j in range(2):
        v.tensor_copy(nis[j], ts_[j])
    for j in range(2):
        v.tensor_copy(nfs[j], nis[j])
    for j in range(2):
        v.tensor_tensor(ts_[j], ts_[j], nfs[j], ALU.subtract)
    for j in range(2):
        v.scalar_tensor_tensor(nfs[j], ts_[j], 0.5, nfs[j], ALU.is_ge, ALU.add)
    for j in range(2):
        v.scalar_tensor_tensor(nfs[j], ts_[j], -0.5, nfs[j], ALU.is_ge, ALU.add)
    for j in range(2):
        v.tensor_scalar(nfs[j], nfs[j], -1.0, hi, ALU.add, ALU.min)
    for j in range(2):
        v.tensor_scalar(outs[j], nfs[j], lo, None, ALU.max)


def emit_rstd(scalar, var_ap, rstd_ap):
    scalar.activation(rstd_ap, var_ap, AF.Ln)
    return scalar.activation(rstd_ap, rstd_ap, AF.Exp, scale=-0.5)


def emit_ln_stats_phase(blk, x_src_tiles, n_tiles, xt, sx_slots, ssq, ssum,
                        s_sumsq, s_sum, scr_sq, scr_cp, col0=0, cnt0=0):
    n_buf = xt.shape[1]

    @blk.gpsimd
    def _(g):
        for i in range(n_tiles):
            if i >= n_buf:
                g.wait_ge(ssq, cnt0 + i - n_buf + 1)
                g.wait_ge(ssum, cnt0 + i - n_buf + 1)
            sx_slots.inc(g.dma_start(xt[:, i % n_buf, :], x_src_tiles(i)), i)

    @blk.scalar
    def _(sc):
        sc = DrainingEngine(sc)
        for i in range(n_tiles):
            sx_slots.wait(sc, i)
            sc.activation(scr_sq[:, i % scr_sq.shape[1], :], xt[:, i % n_buf, :],
                          AF.Square,
                          accum_out=s_sumsq[:, col0 + i:col0 + i + 1]).then_inc(ssq, 1)

    @blk.vector
    def _(v):
        v = DrainingEngine(v)
        for i in range(n_tiles):
            sx_slots.wait(v, i)
            v.tensor_scalar(scr_cp[:, i % scr_cp.shape[1], :], xt[:, i % n_buf, :],
                            1.0, None, ALU.mult, ALU.add,
                            accum_out=s_sum[:, col0 + i:col0 + i + 1]).then_inc(ssum, 1)


def emit_stat_math_phase(blk, s_sum, s_sumsq, s_negm, s_var, s_rstd, width, dim):
    @blk.vector
    def _(v):
        v = DrainingEngine(v)
        v.tensor_scalar_mul(s_negm[:, 0:width], s_sum[:, 0:width], -1.0 / dim)
        v.tensor_tensor(s_var[:, 0:width], s_negm[:, 0:width], s_negm[:, 0:width], ALU.mult)
        v.tensor_scalar(s_sumsq[:, 0:width], s_sumsq[:, 0:width], 1.0 / dim, LN_EPS,
                        ALU.mult, ALU.add)
        v.tensor_tensor(s_var[:, 0:width], s_sumsq[:, 0:width], s_var[:, 0:width],
                        ALU.subtract)

    @blk.scalar
    def _(sc):
        pass


def emit_rstd_phase(blk, s_var, s_rstd, width):
    @blk.scalar
    def _(sc):
        emit_rstd(DrainingEngine(sc), s_var[:, 0:width], s_rstd[:, 0:width])




def emit_residual_phase(nc, A4, x_src_tiles, y_src_tiles, out_tiles, n_tiles,
                        xt, p_negm, p_rstd, stat0=0, SA=None):
    SA = SA or (lambda h: h)
    yt = A4(nc.sbuf_tensor("res_yt", [128, 2, D], BF16))
    t1 = A4(nc.sbuf_tensor("res_t1", [128, 2, D], F32))
    xnew = A4(nc.sbuf_tensor("res_xnew", [128, 2, D], F32))
    sx4 = DmaSlots(nc, SA, "res_sx", 2)
    sy4 = DmaSlots(nc, SA, "res_sy", 2)
    sres = SA(nc.semaphore("res_done"))
    sst = DmaSlots(nc, SA, "res_st", 2)
    blk = A4(nc.Block())

    @blk.gpsimd
    def _(g):
        for i in range(n_tiles):
            if i >= 2:
                g.wait_ge(sres, i - 1)
            sx4.inc(g.dma_start(xt[:, i % 2, :], x_src_tiles(i)), i)
            sy4.inc(g.dma_start(yt[:, i % 2, :], y_src_tiles(i)), i)

    @blk.vector
    def _(v):
        v = DrainingEngine(v)
        for i in range(n_tiles):
            sx4.wait(v, i)
            sy4.wait(v, i)
            if i >= 2:
                sst.wait(v, i - 2)
            k = stat0 + i
            v.tensor_scalar(t1[:, i % 2, :], yt[:, i % 2, :],
                            p_negm[:, k:k + 1], p_rstd[:, k:k + 1],
                            ALU.add, ALU.mult)
            v.tensor_tensor(xnew[:, i % 2, :], xt[:, i % 2, :],
                            t1[:, i % 2, :], ALU.add).then_inc(sres, 1)

    @blk.sync
    def _(sy_):
        for i in range(n_tiles):
            sy_.wait_ge(sres, i + 1)
            sst.inc(sy_.dma_start(out_tiles(i), xnew[:, i % 2, :]), i)
        for k in range(2):
            sy_.wait_ge(sst.sems[k], 16 * ((n_tiles + 1 - k) // 2))




def emit_l1_ffn(nc, SA, x_in, wg, wu, wd, x_out):
    SA2 = SA
    H = 3072
    NHC = H // 128
    NSL = H // 512
    c127 = float(np.float32(1.0 / 127.0))

    y_scr = nc.dram_tensor("y_scr", [S_CORE, D], BF16)

    with contextlib.ExitStack() as es:
        A = es.enter_context
        wgT = A(nc.sbuf_tensor("wgT", [128, NHC, NDC, 128], BF16))
        wuT = A(nc.sbuf_tensor("wuT", [128, NHC, NDC, 128], BF16))
        wdT = A(nc.sbuf_tensor("wdT", [128, 2, NDC, NHC // 2, 128], BF16))
        xt = A(nc.sbuf_tensor("xt", [128, 2, D], F32))
        scr_sq = A(nc.sbuf_tensor("scr_sq", [128, 2, D], F32))
        scr_cp = A(nc.sbuf_tensor("scr_cp", [128, 2, D], BF16))
        s_sum = A(nc.sbuf_tensor("s_sum", [128, NT], F32))
        s_sumsq = A(nc.sbuf_tensor("s_sumsq", [128, NT], F32))
        s_negm = A(nc.sbuf_tensor("s_negm", [128, NT], F32))
        s_var = A(nc.sbuf_tensor("s_var", [128, NT], F32))
        s_rstd = A(nc.sbuf_tensor("s_rstd", [128, NT], F32))
        p_sum = A(nc.sbuf_tensor("p_sum", [128, NT], F32))
        p_sumsq = A(nc.sbuf_tensor("p_sumsq", [128, NT], F32))
        p_negm = A(nc.sbuf_tensor("p_negm", [128, NT], F32))
        p_var = A(nc.sbuf_tensor("p_var", [128, NT], F32))
        p_rstd = A(nc.sbuf_tensor("p_rstd", [128, NT], F32))
        dwg = SA(nc.semaphore("dwg"))
        dws = SA(nc.semaphore("dws"))
        sx = DmaSlots(nc, SA, "sx", 2)
        ssq = SA(nc.semaphore("ssq"))
        ssum = SA(nc.semaphore("ssum"))
        gcnt = 0
        tcnt = 0
        with nc.sbuf_tensor("wraw", [128, NHC // 2, D], BF16) as wraw:
            for (w_dram, w_t) in ((wg, wgT), (wu, wuT)):
                for half in range(2):
                    hh = NHC // 2
                    with nc.Block() as blk:
                        @blk.gpsimd
                        def _(g, w_dram=w_dram, half=half, base=gcnt):
                            g.dma_start(
                                wraw[:],
                                w_dram.ap()[half * (H // 2):(half + 1) * (H // 2), :]
                                .rearrange("(c p) d -> p c d", p=128),
                            ).then_inc(dwg, 16)
                            g.wait_ge(dwg, base + 16)
                    gcnt += 16
                    with nc.Block() as blk:
                        @blk.sync
                        def _(sync, w_t=w_t, half=half, base=tcnt):
                            sync.dma_start(
                                w_t.ap()[:, half * hh:(half + 1) * hh, :, :],
                                wraw[:],
                                transpose=True,
                            ).then_inc(dws, 16)
                            sync.wait_ge(dws, base + 16)
                    tcnt += 16
        with nc.sbuf_tensor("wdraw", [128, NDC, H // 2], BF16) as wdraw:
            for half in range(2):
                with nc.Block() as blk:
                    @blk.gpsimd
                    def _(g, half=half, base=gcnt):
                        g.dma_start(
                            wdraw[:],
                            wd.ap()[:, half * (H // 2):(half + 1) * (H // 2)]
                            .rearrange("(c p) h -> p c h", p=128),
                        ).then_inc(dwg, 16)
                        g.wait_ge(dwg, base + 16)
                gcnt += 16
                with nc.Block() as blk:
                    @blk.sync
                    def _(sync, half=half, base=tcnt):
                        sync.dma_start(
                            wdT.ap()[:, half, :, :, :],
                            wdraw[:],
                            transpose=True,
                        ).then_inc(dws, 16)
                        sync.wait_ge(dws, base + 16)
                tcnt += 16

        with nc.Block() as blk:
            emit_ln_stats_phase(blk, lambda i: x_in.ap()[i * 128:(i + 1) * 128, :],
                                NT, xt, sx, ssq, ssum, s_sumsq, s_sum, scr_sq, scr_cp)
        with nc.Block() as blk:
            emit_stat_math_phase(blk, s_sum, s_sumsq, s_negm, s_var, s_rstd, NT, D)
        with nc.Block() as blk:
            emit_rstd_phase(blk, s_var, s_rstd, NT)

        with contextlib.ExitStack() as es3:
            A3 = es3.enter_context
            xln = A3(nc.sbuf_tensor("xln", [128, 2, D], BF16))
            xf = A3(nc.sbuf_tensor("xf", [128, 2, NDC, 128], BF16))
            gsil = A3(nc.sbuf_tensor("gsil", [128, 2, 512], BF16))
            usc = A3(nc.sbuf_tensor("usc", [128, 2, 512], BF16))
            h_tm = A3(nc.sbuf_tensor("h_tm", [128, 2, H], BF16))
            h_fm = A3(nc.sbuf_tensor("h_fm", [128, 2, NHC, 128], BF16))
            y_sb = A3(nc.sbuf_tensor("y_sb", [128, 2, D], BF16))
            sx3 = DmaSlots(nc, SA, "sx3", 2)
            sxln = SA(nc.semaphore("sxln"))
            sxf = DmaSlots(nc, SA, "sxf", 2)
            sgmm = SA(nc.semaphore("sgmm"))
            summ = SA(nc.semaphore("summ"))
            ssil = SA(nc.semaphore("ssil"))
            suev = SA(nc.semaphore("suev"))
            sh = SA(nc.semaphore("sh"))
            shf = DmaSlots(nc, SA, "shf", 2)
            sdmm = SA(nc.semaphore("sdmm"))
            spsq = SA(nc.semaphore("spsq"))
            sy2 = SA(nc.semaphore("sy2"))
            syst = DmaSlots(nc, SA, "syst", 2)
            gps = A3(nc.psum_tensor("gps", [128, 2, 512], F32))
            ups = A3(nc.psum_tensor("ups", [128, 2, 512], F32))
            dps = A3(nc.psum_tensor("dps", [128, 2, D], F32))
            blk = A3(nc.Block())
            @blk.gpsimd
            def _(g):
                for i in range(NT):
                    if i >= 2:
                        g.wait_ge(sxln, i - 1)
                    sx3.inc(g.dma_start(xt[:, i % 2, :],
                                        x_in.ap()[i * 128:(i + 1) * 128, :]), i)

            @blk.vector
            def _(v):
                v = DrainingEngine(v)

                def front(i):
                    sx3.wait(v, i)
                    if i >= 2:
                        sxf.wait(v, i - 2)
                    v.tensor_scalar(xln[:, i % 2, :], xt[:, i % 2, :],
                                    s_negm[:, i:i + 1], s_rstd[:, i:i + 1],
                                    ALU.add, ALU.mult).then_inc(sxln, 1)
                front(0)
                for i in range(NT):
                    if i + 1 < NT:
                        front(i + 1)
                    for j in range(NSL):
                        gidx = i * NSL + j
                        v.wait_ge(summ, gidx + 1)
                        v.tensor_scalar_mul(usc[:, j % 2, :], ups[:, j % 2, :],
                                            c127).then_inc(suev, 1)
                        v.wait_ge(ssil, gidx + 1)
                        if i >= 2 and j == 0:
                            shf.wait(v, i - 2)
                        v.tensor_tensor(h_tm[:, i % 2, j * 512:(j + 1) * 512],
                                        gsil[:, j % 2, :], usc[:, j % 2, :],
                                        ALU.mult).then_inc(sh, 1)
                    v.wait_ge(sdmm, i + 1)
                    if i >= 2:
                        syst.wait(v, i - 2)
                    v.tensor_scalar(y_sb[:, i % 2, :], dps[:, i % 2, :], c127, None,
                                    ALU.mult, ALU.add,
                                    accum_out=p_sum[:, i:i + 1]).then_inc(sy2, 1)

            @blk.scalar
            def _(sc):
                sc = DrainingEngine(sc)
                for i in range(NT):
                    for j in range(NSL):
                        gidx = i * NSL + j
                        sc.wait_ge(sgmm, gidx + 1)
                        if gidx >= 2:
                            sc.wait_ge(sh, gidx - 1)
                        sc.activation(gsil[:, j % 2, :], gps[:, j % 2, :], AF.Silu,
                                      scale=c127).then_inc(ssil, 1)
                    sc.wait_ge(sy2, i + 1)
                    sc.activation(scr_sq[:, i % 2, :], y_sb[:, i % 2, :], AF.Square,
                                  accum_out=p_sumsq[:, i:i + 1]).then_inc(spsq, 1)

            @blk.sync
            def _(sy_):
                for i in range(NT):
                    sy_.wait_ge(sxln, i + 1)
                    sxf.inc(sy_.dma_start(
                        xf.ap()[:, i % 2, :, :],
                        xln.ap()[:, i % 2, :].rearrange("p (c q) -> p c q", c=NDC),
                        transpose=True,
                    ), i)
                    sy_.wait_ge(sh, (i + 1) * NSL)
                    shf.inc(sy_.dma_start(
                        h_fm.ap()[:, i % 2, :, :],
                        h_tm.ap()[:, i % 2, :].rearrange("p (c q) -> p c q", c=NHC),
                        transpose=True,
                    ), i)
                    sy_.wait_ge(sy2, i + 1)
                    syst.inc(sy_.dma_start(y_scr.ap()[i * 128:(i + 1) * 128, :],
                                           y_sb[:, i % 2, :]), i)
                for k in range(2):
                    sy_.wait_ge(syst.sems[k], 16 * (NT // 2))

            @blk.tensor
            def _(t):
                for i in range(NT):
                    sxf.wait(t, i)
                    for j in range(NSL):
                        gidx = i * NSL + j
                        if gidx >= 2:
                            t.wait_ge(ssil, gidx - 1)
                        for c in range(NDC):
                            mm_ = t.matmul(gps[:, j % 2, :],
                                     xf[:, i % 2, c, :],
                                     wgT.ap()[:, 4 * j:4 * j + 4, c, :],
                                     start=(c == 0), stop=(c == NDC - 1))
                        mm_.then_inc(sgmm, 1)
                        if gidx >= 2:
                            t.wait_ge(suev, gidx - 1)
                        for c in range(NDC):
                            mm_ = t.matmul(ups[:, j % 2, :],
                                     xf[:, i % 2, c, :],
                                     wuT.ap()[:, 4 * j:4 * j + 4, c, :],
                                     start=(c == 0), stop=(c == NDC - 1))
                        mm_.then_inc(summ, 1)
                    shf.wait(t, i)
                    if i >= 2:
                        t.wait_ge(sy2, i - 1)
                        t.wait_ge(spsq, i - 1)
                    for n2 in range(2):
                        for c in range(NHC):
                            mm_ = t.matmul(dps[:, i % 2, n2 * 512:(n2 + 1) * 512],
                                     h_fm[:, i % 2, c, :],
                                     wdT.ap()[:, c // (NHC // 2), 4 * n2:4 * n2 + 4,
                                              c % (NHC // 2), :],
                                     start=(c == 0), stop=(c == NHC - 1))
                    mm_.then_inc(sdmm, 1)

        with nc.Block() as blk:
            emit_stat_math_phase(blk, p_sum, p_sumsq, p_negm, p_var, p_rstd, NT, D)
        with nc.Block() as blk:
            emit_rstd_phase(blk, p_var, p_rstd, NT)
        with contextlib.ExitStack() as es4:
            A4 = es4.enter_context
            yt = A4(nc.sbuf_tensor("yt", [128, 2, D], BF16))
            t1 = A4(nc.sbuf_tensor("t1", [128, 2, D], F32))
            xnew = A4(nc.sbuf_tensor("xnew", [128, 2, D], F32))
            sx4 = DmaSlots(nc, SA, "sx4", 2)
            sy4 = DmaSlots(nc, SA, "sy4", 2)
            sres = SA(nc.semaphore("sres"))
            sst = DmaSlots(nc, SA, "sst", 2)
            blk = A4(nc.Block())
            @blk.gpsimd
            def _(g):
                for i in range(NT):
                    if i >= 2:
                        g.wait_ge(sres, i - 1)
                    sx4.inc(g.dma_start(xt[:, i % 2, :],
                                        x_in.ap()[i * 128:(i + 1) * 128, :]), i)
                    sy4.inc(g.dma_start(yt[:, i % 2, :],
                                        y_scr.ap()[i * 128:(i + 1) * 128, :]), i)

            @blk.vector
            def _(v):
                v = DrainingEngine(v)
                for i in range(NT):
                    sx4.wait(v, i)
                    sy4.wait(v, i)
                    if i >= 2:
                        sst.wait(v, i - 2)
                    v.tensor_scalar(t1[:, i % 2, :], yt[:, i % 2, :],
                                    p_negm[:, i:i + 1], p_rstd[:, i:i + 1],
                                    ALU.add, ALU.mult)
                    v.tensor_tensor(xnew[:, i % 2, :], xt[:, i % 2, :],
                                    t1[:, i % 2, :], ALU.add).then_inc(sres, 1)

            @blk.sync
            def _(sy_):
                for i in range(NT):
                    sy_.wait_ge(sres, i + 1)
                    sst.inc(sy_.dma_start(x_out.ap()[i * 128:(i + 1) * 128, :],
                                          xnew[:, i % 2, :]), i)
                for k in range(2):
                    sy_.wait_ge(sst.sems[k], 16 * (NT // 2))


def build_l1_ffn():
    nc0 = bass.Bass("TRN2", target_bir_lowering=False)
    nc = NcView(nc0, "d_")
    x_in = nc0.dram_tensor("x_in", [S_CORE, D], F32, kind="ExternalInput")
    wg = nc0.dram_tensor("wg", [3072, D], I32, kind="ExternalInput")
    wu = nc0.dram_tensor("wu", [3072, D], I32, kind="ExternalInput")
    wd = nc0.dram_tensor("wd", [D, 3072], I32, kind="ExternalInput")
    x_out = nc0.dram_tensor("x_out", [S_CORE, D], F32, kind="ExternalOutput")
    with contextlib.ExitStack() as es:
        emit_l1_ffn(nc, lambda h: h, x_in, wg, wu, wd, x_out)
    return nc0






def emit_l0_ffn(nc, SA, x_in, wg, wu, wd, x_out):
    SA2 = SA
    H = 4096
    NHC = H // 128
    NSL = H // 512
    NQ = 4
    c127 = float(np.float32(1.0 / 127.0))

    y_scr = nc.dram_tensor("y_scr", [S_CORE, D], BF16)
    inth_scr = nc.dram_tensor("inth_scr", [S_CORE, H], BF16)

    QF = H * D // NQ // 128

    with contextlib.ExitStack() as es:
        A = es.enter_context
        xt = A(nc.sbuf_tensor("xt", [128, 2, D], F32))
        scr_sq = A(nc.sbuf_tensor("scr_sq", [128, 1, D], F32))
        scr_cp = A(nc.sbuf_tensor("scr_cp", [128, 1, D], BF16))
        s_sum = A(nc.sbuf_tensor("s_sum", [128, NT], F32))
        s_sumsq = A(nc.sbuf_tensor("s_sumsq", [128, NT], F32))
        s_negm = A(nc.sbuf_tensor("s_negm", [128, NT], F32))
        s_var = A(nc.sbuf_tensor("s_var", [128, NT], F32))
        s_rstd = A(nc.sbuf_tensor("s_rstd", [128, NT], F32))
        p_sum = A(nc.sbuf_tensor("p_sum", [128, NT], F32))
        p_sumsq = A(nc.sbuf_tensor("p_sumsq", [128, NT], F32))
        p_negm = A(nc.sbuf_tensor("p_negm", [128, NT], F32))
        p_var = A(nc.sbuf_tensor("p_var", [128, NT], F32))
        p_rstd = A(nc.sbuf_tensor("p_rstd", [128, NT], F32))
        s_a = A(nc.sbuf_tensor("s_a", [128, NT], F32))
        s_cg = A(nc.sbuf_tensor("s_cg", [128, NT], F32))
        s_cu = A(nc.sbuf_tensor("s_cu", [128, NT], F32))
        s_a2 = A(nc.sbuf_tensor("s_a2", [128, NT], F32))
        s_c2 = A(nc.sbuf_tensor("s_c2", [128, NT], F32))
        wm_b = A(nc.sbuf_tensor("wm_b", [128, 6], F32))
        wsum = A(nc.sbuf_tensor("wsum", [128, 3 * NQ], F32))
        ones_c = A(nc.sbuf_tensor("ones_c", [128, 1], F32))
        ones_r = A(nc.sbuf_tensor("ones_r", [1, 128], F32))
        row6 = A(nc.sbuf_tensor("row6", [1, 6], F32))
        rowq = A(nc.sbuf_tensor("rowq", [1, 3 * NQ], F32))

        with contextlib.ExitStack() as es0:
            A0 = es0.enter_context
            wst = A0(nc.sbuf_tensor("wst", [128, 2, QF], F32))
            sld = DmaSlots(nc, SA, "b_ld", 2)
            sred = SA(nc.semaphore("b_red"))
            blk = A0(nc.Block())
            srcs = []
            for widx, w_dram in enumerate((wg, wu)):
                for q in range(NQ):
                    srcs.append(w_dram.ap()[q * (H // NQ):(q + 1) * (H // NQ), :]
                                .rearrange("(c p) d -> p c d", p=128))
            for q in range(NQ):
                srcs.append(wd.ap()[q * (D // NQ):(q + 1) * (D // NQ), :]
                            .rearrange("(c p) h -> p c h", p=128))

            @blk.gpsimd
            def _(g):
                for k in range(3 * NQ):
                    if k >= 2:
                        g.wait_ge(sred, k - 1)
                    sld.inc(g.dma_start(
                        wst.ap()[:, k % 2, :].rearrange("p (c d) -> p c d", d=D if k < 2 * NQ else H),
                        srcs[k]), k)

            @blk.vector
            def _(v):
                v = DrainingEngine(v)
                for k in range(3 * NQ):
                    sld.wait(v, k)
                    v.reduce_sum(wsum[:, k:k + 1], wst[:, k % 2, :], axis=AX,
                                 apply_absolute_value=True).then_inc(sred, 1)

        with contextlib.ExitStack() as es0:
            A0 = es0.enter_context
            pst = A0(nc.psum_tensor("b_pst", [128, 16], F32))
            with nc.Block() as blk:
                @blk.vector
                def _(v):
                    v.memset(ones_c[:], 1.0)
                    v.memset(ones_r[:], 1.0)
            with nc.Block() as blk:
                @blk.tensor
                def _(t):
                    mm_ = t.matmul(pst[0:1, 0:3 * NQ], ones_c[:], wsum[:], start=True, stop=True)
                    t.drain()
            with nc.Block() as blk:
                @blk.vector
                def _(v):
                    v = DrainingEngine(v)
                    v.tensor_copy(rowq[:], pst[0:1, 0:3 * NQ])
                    for w in range(3):
                        v.reduce_sum(row6[:, w:w + 1], rowq[:, w * NQ:(w + 1) * NQ], axis=AX)
                    v.tensor_scalar(row6[:, 0:3], row6[:, 0:3], 1.0 / (H * D), 1e-5,
                                    ALU.mult, ALU.max)
                    emit_recip_newton(v, row6[:, 3:6], row6[:, 0:3],
                                      rowq[:, 0:3], rowq[:, 3:6])
            with nc.Block() as blk:
                @blk.tensor
                def _(t):
                    mm_ = t.matmul(pst[:, 0:6], ones_r[:], row6[:], start=True, stop=True)
                    t.drain()
            with nc.Block() as blk:
                @blk.vector
                def _(v):
                    v.tensor_copy(wm_b[:], pst[:, 0:6])

        def emit_quant_weights(A0, pairs, stp_name):
            wstq = A0(nc.sbuf_tensor(stp_name + "_f", [128, QF], F32))
            wqb = A0(nc.sbuf_tensor(stp_name + "_b", [128, QF], BF16))
            wnf = A0(nc.sbuf_tensor(stp_name + "_nf", [128, QF // 2], F32))
            sld = SA(nc.semaphore(stp_name + "_ld"))
            sqn = SA(nc.semaphore(stp_name + "_qn"))
            stp = SA(nc.semaphore(stp_name + "_tp"))
            blk = A0(nc.Block())

            @blk.gpsimd
            def _(g):
                for k, (src, dst, widx) in enumerate(pairs):
                    if k >= 1:
                        g.wait_ge(sqn, k)
                    g.dma_start(wstq.ap().rearrange("p (c d) -> p c d", c=src.shape[1]),
                                src).then_inc(sld, 16)

            @blk.vector
            def _(v):
                v = DrainingEngine(v)
                for k, (src, dst, widx) in enumerate(pairs):
                    v.wait_ge(sld, 16 * (k + 1))
                    if k >= 1:
                        v.wait_ge(stp, 16 * k)
                    v.tensor_scalar_mul(wstq[:], wstq[:], wm_b[:, 3 + widx:4 + widx])
                    for q4 in range(2):
                        sl = slice(q4 * (QF // 2), (q4 + 1) * (QF // 2))
                        v.tensor_scalar(wnf[:], wstq[:, sl], 0.5, None, ALU.is_ge)
                        v.scalar_tensor_tensor(wnf[:], wstq[:, sl], -0.5,
                                               wnf[:], ALU.is_ge, ALU.add)
                        v.tensor_scalar(wqb[:, sl], wnf[:], -1.0, 1.0,
                                        ALU.mult, ALU.add)
                    v.drain().then_inc(sqn, 1)

            @blk.sync
            def _(sy_):
                for k, (src, dst, widx) in enumerate(pairs):
                    sy_.wait_ge(sqn, k + 1)
                    sy_.dma_start(dst, wstq if False else wqb[:],
                                  transpose=True).then_inc(stp, 16)
                sy_.wait_ge(stp, 16 * len(pairs))

        sx = DmaSlots(nc, SA, "sx", 2)
        ssq = SA(nc.semaphore("ssq"))
        ssum = SA(nc.semaphore("ssum"))
        with nc.Block() as blk:
            emit_ln_stats_phase(blk, lambda i: x_in.ap()[i * 128:(i + 1) * 128, :],
                                NT, xt, sx, ssq, ssum, s_sumsq, s_sum, scr_sq, scr_cp)
        with nc.Block() as blk:
            emit_stat_math_phase(blk, s_sum, s_sumsq, s_negm, s_var, s_rstd, NT, D)
        with nc.Block() as blk:
            emit_rstd_phase(blk, s_var, s_rstd, NT)

        with contextlib.ExitStack() as es3:
            A3 = es3.enter_context
            wgT = A3(nc.sbuf_tensor("wgT", [128, NHC, NDC, 128], BF16))
            wuT = A3(nc.sbuf_tensor("wuT", [128, NHC, NDC, 128], BF16))
            with contextlib.ExitStack() as esq:
                pairs = []
                for widx, (w_dram, w_t) in enumerate(((wg, wgT), (wu, wuT))):
                    for q in range(NQ):
                        src = (w_dram.ap()[q * (H // NQ):(q + 1) * (H // NQ), :]
                               .rearrange("(c p) d -> p c d", p=128))
                        dst = w_t.ap()[:, q * (NHC // NQ):(q + 1) * (NHC // NQ), :, :]
                        pairs.append((src, dst, widx))
                emit_quant_weights(esq.enter_context, pairs, "qgu")

            xln = A3(nc.sbuf_tensor("xln", [128, D], F32))
            scrq = scr_sq
            intx = A3(nc.sbuf_tensor("intx", [128, 2, D], BF16))
            ixf = A3(nc.sbuf_tensor("ixf", [128, 2, NDC, 128], BF16))
            gsil = A3(nc.sbuf_tensor("gsil", [128, 2, 512], F32))
            usc = A3(nc.sbuf_tensor("usc", [128, 2, 512], F32))
            h_tm = A3(nc.sbuf_tensor("h_tm", [128, H], F32))
            inth = A3(nc.sbuf_tensor("inth", [128, 1, H], BF16))
            b_ni0 = A3(nc.sbuf_tensor("b_ni0", [128, H // 4], I32))
            b_nf0 = A3(nc.sbuf_tensor("b_nf0", [128, H // 4], F32))
            b_ni1 = A3(nc.sbuf_tensor("b_ni1", [128, H // 4], I32))
            b_nf1 = A3(nc.sbuf_tensor("b_nf1", [128, H // 4], F32))
            sx3 = DmaSlots(nc, SA, "sx3", 2)
            sxln = SA(nc.semaphore("sxln"))
            sixf = DmaSlots(nc, SA, "sixf", 2)
            sgmm = SA(nc.semaphore("sgmm"))
            summ = SA(nc.semaphore("summ"))
            ssil = SA(nc.semaphore("ssil"))
            suev = SA(nc.semaphore("suev"))
            sh = SA(nc.semaphore("sh"))
            sqh = SA(nc.semaphore("sqh"))
            sqm = SA(nc.semaphore("sqm"))
            sqc = SA(nc.semaphore("sqc"))
            shp = SA(nc.semaphore("shp"))
            shst = DmaSlots(nc, SA, "shst", 2)
            gps = A3(nc.psum_tensor("gps", [128, 2, 512], F32))
            ups = A3(nc.psum_tensor("ups", [128, 2, 512], F32))
            blk = A3(nc.Block())

            @blk.gpsimd
            def _(g):
                gp = DrainingEngine(g)
                for i in range(NT):
                    if i >= 2:
                        g.wait_ge(sxln, 2 * (i - 1))
                    sx3.inc(g.dma_start(xt[:, i % 2, :],
                                        x_in.ap()[i * 128:(i + 1) * 128, :]), i)
                    for j in range(NSL):
                        gidx = i * NSL + j
                        g.wait_ge(ssil, gidx + 1)
                        g.wait_ge(suev, gidx + 1)
                        if i >= 1 and j == 0:
                            g.wait_ge(sqh, i)
                        gp.tensor_tensor(h_tm[:, j * 512:(j + 1) * 512],
                                         gsil[:, j % 2, :], usc[:, j % 2, :],
                                         ALU.mult).then_inc(sh, 1)

            @blk.vector
            def _(v):
                v = DrainingEngine(v)

                def front(i):
                    sx3.wait(v, i)
                    if i >= 2:
                        sixf.wait(v, i - 2)
                    v.tensor_scalar(xln[:], xt[:, i % 2, :],
                                    s_negm[:, i:i + 1], s_rstd[:, i:i + 1],
                                    ALU.add, ALU.mult).then_inc(sxln, 1)
                    v.reduce_max(s_a[:, i:i + 1], xln[:], axis=AX,
                                 apply_absolute_value=True)
                    v.tensor_scalar_max(s_a[:, i:i + 1], s_a[:, i:i + 1], 1e-5)
                    v.tensor_scalar(s_cg[:, i:i + 1], s_a[:, i:i + 1], -c127, None,
                                    ALU.mult, ALU.bypass)
                    v.tensor_tensor(s_cu[:, i:i + 1], s_cg[:, i:i + 1],
                                    wm_b[:, 1:2], ALU.mult)
                    v.tensor_tensor(s_cg[:, i:i + 1], s_cg[:, i:i + 1],
                                    wm_b[:, 0:1], ALU.mult)
                    emit_recip_newton(v, s_a[:, i:i + 1], s_a[:, i:i + 1],
                                      s_sum[:, i:i + 1], s_sumsq[:, i:i + 1])
                    v.tensor_scalar_mul(s_a[:, i:i + 1], s_a[:, i:i + 1], 127.0)
                    v.tensor_scalar_mul(xln[:], xln[:], s_a[:, i:i + 1])
                    v.drain().then_inc(sqm, 1)
                    v.wait_ge(sqc, [1, 2][i] if i < 2 else 5 * i - 3)
                    emit_round_correct(v, intx[:, i % 2, :], xln[:],
                                       b_nf0[:], -128.0, 127.0)
                    v.drain().then_inc(sxln, 1)
                front(0)
                for i in range(NT):
                    if i + 1 < NT:
                        front(i + 1)
                    v.wait_ge(sh, (i + 1) * NSL)
                    v.reduce_max(s_a2[:, i:i + 1], h_tm[:], axis=AX,
                                 apply_absolute_value=True)
                    v.tensor_scalar_max(s_a2[:, i:i + 1], s_a2[:, i:i + 1], 1e-5)
                    v.tensor_scalar(s_c2[:, i:i + 1], s_a2[:, i:i + 1], -c127, None,
                                    ALU.mult, ALU.bypass)
                    v.tensor_tensor(s_c2[:, i:i + 1], s_c2[:, i:i + 1],
                                    wm_b[:, 2:3], ALU.mult)
                    emit_recip_newton(v, s_a2[:, i:i + 1], s_a2[:, i:i + 1],
                                      s_sum[:, i:i + 1], s_sumsq[:, i:i + 1])
                    v.tensor_scalar_mul(s_a2[:, i:i + 1], s_a2[:, i:i + 1], 127.0)
                    if i >= 1:
                        shst.wait(v, i - 1)
                    for pairq in range(2):
                        h0 = slice((2 * pairq) * (H // 4), (2 * pairq + 1) * (H // 4))
                        h1 = slice((2 * pairq + 1) * (H // 4), (2 * pairq + 2) * (H // 4))
                        v.tensor_scalar_mul(h_tm[:, h0], h_tm[:, h0], s_a2[:, i:i + 1])
                        v.tensor_scalar_mul(h_tm[:, h1], h_tm[:, h1], s_a2[:, i:i + 1])
                        v.drain().then_inc(sqm, 1)
                        v.wait_ge(sqc, (5 * i + 4 + 2 * pairq) if i + 1 < NT else (5 * i + 3 + 2 * pairq))
                        emit_round_correct2(v, (inth[:, 0, h0], inth[:, 0, h1]),
                                            (h_tm[:, h0], h_tm[:, h1]),
                                            (b_nf0[:], b_nf1[:]), -128.0, 127.0)
                    v.drain().then_inc(sqh, 1)

            @blk.scalar
            def _(sc):
                sc = DrainingEngine(sc)

                def site0(i):
                    sc.wait_ge(sqm, [1, 2][i] if i < 2 else 3 * i - 1)
                    sc.activation(b_ni0[:, 0:D], xln[:], AF.Copy)
                    sc.activation(b_nf0[:, 0:D], b_ni0[:, 0:D],
                                  AF.Copy).then_inc(sqc, 1)
                site0(0)
                for i in range(NT):
                    for j in range(NSL):
                        gidx = i * NSL + j
                        sc.wait_ge(sgmm, gidx + 1)
                        if gidx >= 2:
                            sc.wait_ge(sh, gidx - 1)
                        sc.activation(gsil[:, j % 2, :], gps[:, j % 2, :], AF.Silu,
                                      scale=s_cg[:, i:i + 1]).then_inc(ssil, 1)
                        sc.wait_ge(summ, gidx + 1)
                        if gidx >= 2:
                            sc.wait_ge(sh, gidx - 1)
                        sc.activation(usc[:, j % 2, :], ups[:, j % 2, :], AF.Copy,
                                      scale=s_cu[:, i:i + 1]).then_inc(suev, 1)
                    if i + 1 < NT:
                        site0(i + 1)
                    for pairq in range(2):
                        h0 = slice((2 * pairq) * (H // 4), (2 * pairq + 1) * (H // 4))
                        h1 = slice((2 * pairq + 1) * (H // 4), (2 * pairq + 2) * (H // 4))
                        sc.wait_ge(sqm, (3 * i + 3 + pairq) if i + 1 < NT else (3 * i + 2 + pairq))
                        sc.activation(b_ni0[:], h_tm[:, h0], AF.Copy)
                        sc.activation(b_nf0[:], b_ni0[:], AF.Copy).then_inc(sqc, 1)
                        sc.activation(b_ni1[:], h_tm[:, h1], AF.Copy)
                        sc.activation(b_nf1[:], b_ni1[:], AF.Copy).then_inc(sqc, 1)


            @blk.sync
            def _(sy_):
                for i in range(NT):
                    sy_.wait_ge(sxln, 2 * (i + 1))
                    sixf.inc(sy_.dma_start(
                        ixf.ap()[:, i % 2, :, :],
                        intx.ap()[:, i % 2, :].rearrange("p (c q) -> p c q", c=NDC),
                        transpose=True,
                    ), i)
                    sy_.wait_ge(sqh, i + 1)
                    shst.inc(sy_.dma_start(inth_scr.ap()[i * 128:(i + 1) * 128, :],
                                           inth[:, 0, :]), i)
                for k in range(2):
                    sy_.wait_ge(shst.sems[k], 16 * ((NT + 1 - k) // 2))

            @blk.tensor
            def _(t):
                for i in range(NT):
                    sixf.wait(t, i)
                    for j in range(NSL):
                        gidx = i * NSL + j
                        if gidx >= 2:
                            t.wait_ge(ssil, gidx - 1)
                        for c in range(NDC):
                            mm_ = t.matmul(gps[:, j % 2, :],
                                     ixf[:, i % 2, c, :],
                                     wgT.ap()[:, 4 * j:4 * j + 4, c, :],
                                     start=(c == 0), stop=(c == NDC - 1))
                        mm_.then_inc(sgmm, 1)
                        if gidx >= 2:
                            t.wait_ge(suev, gidx - 1)
                        for c in range(NDC):
                            mm_ = t.matmul(ups[:, j % 2, :],
                                     ixf[:, i % 2, c, :],
                                     wuT.ap()[:, 4 * j:4 * j + 4, c, :],
                                     start=(c == 0), stop=(c == NDC - 1))
                        mm_.then_inc(summ, 1)

        with contextlib.ExitStack() as es3:
            A3 = es3.enter_context
            wdT = A3(nc.sbuf_tensor("wdT", [128, NQ, D // NQ // 128, NHC, 128], BF16))
            with contextlib.ExitStack() as esq:
                pairs = []
                for q in range(NQ):
                    src = (wd.ap()[q * (D // NQ):(q + 1) * (D // NQ), :]
                           .rearrange("(c p) h -> p c h", p=128))
                    dst = wdT.ap()[:, q, :, :, :]
                    pairs.append((src, dst, 2))
                emit_quant_weights(esq.enter_context, pairs, "qd")

            ihf = A3(nc.sbuf_tensor("ihf", [128, 2, NHC, 128], BF16))
            y_sb = A3(nc.sbuf_tensor("y_sb", [128, 2, D], BF16))
            sihf = DmaSlots(nc, SA, "sihf", 2)
            sdmm = SA(nc.semaphore("sdmm"))
            sy2 = SA(nc.semaphore("sy2"))
            spsq = SA(nc.semaphore("spsq"))
            syst = DmaSlots(nc, SA, "syst", 2)
            dps = A3(nc.psum_tensor("dps", [128, 2, D], F32))
            blk = A3(nc.Block())

            @blk.sync
            def _(sy_):
                for i in range(NT):
                    if i >= 2:
                        sy_.wait_ge(sdmm, i - 1)
                    sihf.inc(sy_.dma_start(
                        ihf.ap()[:, i % 2, :, :],
                        inth_scr.ap()[i * 128:(i + 1) * 128, :]
                        .rearrange("p (c q) -> p c q", c=NHC),
                        transpose=True,
                    ), i)
                    sy_.wait_ge(sy2, i + 1)
                    syst.inc(sy_.dma_start(y_scr.ap()[i * 128:(i + 1) * 128, :],
                                           y_sb[:, i % 2, :]), i)
                for k in range(2):
                    sy_.wait_ge(syst.sems[k], 16 * ((NT + 1 - k) // 2))

            @blk.tensor
            def _(t):
                for i in range(NT):
                    sihf.wait(t, i)
                    if i >= 2:
                        t.wait_ge(sy2, i - 1)
                    for n2 in range(2):
                        for c in range(NHC):
                            mm_ = t.matmul(dps[:, i % 2, n2 * 512:(n2 + 1) * 512],
                                     ihf[:, i % 2, c, :],
                                     wdT.ap()[:, 2 * n2:2 * n2 + 2, :, c, :]
                                     .rearrange("p a b q -> p (a b) q"),
                                     start=(c == 0), stop=(c == NHC - 1))
                    mm_.then_inc(sdmm, 1)

            @blk.scalar
            def _(sc):
                sc = DrainingEngine(sc)
                for i in range(NT):
                    sc.wait_ge(sdmm, i + 1)
                    if i >= 2:
                        syst.wait(sc, i - 2)
                    sc.activation(y_sb[:, i % 2, :], dps[:, i % 2, :], AF.Copy,
                                  scale=s_c2[:, i:i + 1],
                                  accum_out=p_sum[:, i:i + 1]).then_inc(sy2, 1)
                    sc.activation(scr_sq[:, 0, :], y_sb[:, i % 2, :], AF.Square,
                                  accum_out=p_sumsq[:, i:i + 1]).then_inc(spsq, 1)

        with nc.Block() as blk:
            emit_stat_math_phase(blk, p_sum, p_sumsq, p_negm, p_var, p_rstd, NT, D)
        with nc.Block() as blk:
            emit_rstd_phase(blk, p_var, p_rstd, NT)
        with contextlib.ExitStack() as es4:
            emit_residual_phase(nc, es4.enter_context,
                                lambda i: x_in.ap()[i * 128:(i + 1) * 128, :],
                                lambda i: y_scr.ap()[i * 128:(i + 1) * 128, :],
                                lambda i: x_out.ap()[i * 128:(i + 1) * 128, :],
                                NT, xt, p_negm, p_rstd, SA=SA2)


def build_l0_ffn():
    nc0 = bass.Bass("TRN2", target_bir_lowering=False)
    nc = NcView(nc0, "b_")
    x_in = nc0.dram_tensor("x_in", [S_CORE, D], F32, kind="ExternalInput")
    wg = nc0.dram_tensor("wg", [4096, D], F32, kind="ExternalInput")
    wu = nc0.dram_tensor("wu", [4096, D], F32, kind="ExternalInput")
    wd = nc0.dram_tensor("wd", [D, 4096], F32, kind="ExternalInput")
    x_out = nc0.dram_tensor("x_out", [S_CORE, D], F32, kind="ExternalOutput")
    with contextlib.ExitStack() as es:
        emit_l0_ffn(nc, lambda h: h, x_in, wg, wu, wd, x_out)
    return nc0




def emit_l0_attn(nc, SA, x_in, x_oth, wqd, wkd, wvd, wod, bod, x_out):
    NH = 16
    NP = 8
    NTF = 2 * NT
    EPS = 1e-6

    y_scr = nc.dram_tensor("y_scr", [S_CORE, D], BF16)

    def tile_src(t):
        if t < NT:
            return x_in.ap()[t * 128:(t + 1) * 128, :]
        return x_oth.ap()[(t - NT) * 128:(t - NT + 1) * 128, :]

    with contextlib.ExitStack() as es:
        A = es.enter_context
        wqT = A(nc.sbuf_tensor("wqT", [128, NDC, NDC, 128], BF16))
        wkT = A(nc.sbuf_tensor("wkT", [128, NDC, NDC, 128], BF16))
        wvT = A(nc.sbuf_tensor("wvT", [128, NDC, NDC, 128], BF16))
        woT = A(nc.sbuf_tensor("woT", [128, NDC, NDC, 128], BF16))
        bo_bf = A(nc.sbuf_tensor("bo_bf", [1, D], BF16))
        ones_r = A(nc.sbuf_tensor("ones_r", [1, 128], BF16))
        ones_c = A(nc.sbuf_tensor("ones_c", [128, 1], BF16))
        xt = A(nc.sbuf_tensor("xt", [128, 2, D], F32))
        scr_sq = A(nc.sbuf_tensor("scr_sq", [128, 2, D], F32))
        scr_cp = A(nc.sbuf_tensor("scr_cp", [128, 2, D], BF16))
        s_sum = A(nc.sbuf_tensor("s_sum", [128, NTF], F32))
        s_sumsq = A(nc.sbuf_tensor("s_sumsq", [128, NTF], F32))
        s_negm = A(nc.sbuf_tensor("s_negm", [128, NTF], F32))
        s_var = A(nc.sbuf_tensor("s_var", [128, NTF], F32))
        s_rstd = A(nc.sbuf_tensor("s_rstd", [128, NTF], F32))
        p_sum = A(nc.sbuf_tensor("p_sum", [128, NT], F32))
        p_sumsq = A(nc.sbuf_tensor("p_sumsq", [128, NT], F32))
        p_negm = A(nc.sbuf_tensor("p_negm", [128, NT], F32))
        p_var = A(nc.sbuf_tensor("p_var", [128, NT], F32))
        p_rstd = A(nc.sbuf_tensor("p_rstd", [128, NT], F32))
        dwg = SA(nc.semaphore("dwg"))
        dws = SA(nc.semaphore("dws"))

        gcnt = 0
        tcnt = 0
        with nc.sbuf_tensor("wraw", [128, NDC, D], BF16) as wraw:
            for wi, (w_dram, w_t) in enumerate(
                    ((wqd, wqT), (wkd, wkT), (wvd, wvT), (wod, woT))):
                with nc.Block() as blk:
                    @blk.gpsimd
                    def _(g, w_dram=w_dram, base=gcnt):
                        g.dma_start(
                            wraw[:],
                            w_dram.ap().rearrange("(c p) d -> p c d", p=128),
                        ).then_inc(dwg, 16)
                        g.wait_ge(dwg, base + 16)
                gcnt += 16
                with nc.Block() as blk:
                    @blk.sync
                    def _(sync, w_t=w_t, base=tcnt):
                        sync.dma_start(w_t[:], wraw[:], transpose=True).then_inc(dws, 16)
                        sync.wait_ge(dws, base + 16)
                tcnt += 16
        with nc.Block() as blk:
            @blk.gpsimd
            def _(g, base=gcnt):
                g.dma_start(bo_bf[:], bod.ap()).then_inc(dwg, 16)
                g.wait_ge(dwg, base + 16)
            @blk.vector
            def _(v):
                v.memset(ones_r[:], 1.0)
                v.memset(ones_c[:], 1.0)
        gcnt += 16

        sx = DmaSlots(nc, SA, "sx", 2)
        ssq = SA(nc.semaphore("ssq"))
        ssum = SA(nc.semaphore("ssum"))
        with nc.Block() as blk:
            emit_ln_stats_phase(blk, tile_src, NTF, xt, sx, ssq, ssum,
                                s_sumsq, s_sum, scr_sq, scr_cp)
        with nc.Block() as blk:
            emit_stat_math_phase(blk, s_sum, s_sumsq, s_negm, s_var, s_rstd, NTF, D)
        with nc.Block() as blk:
            emit_rstd_phase(blk, s_var, s_rstd, NTF)

        xf_own = A(nc.sbuf_tensor("xf_own", [128, NT, NDC, 128], BF16))
        qf_fm = A(nc.sbuf_tensor("qf_fm", [128, NDC, S_CORE], BF16))
        kv2 = A(nc.sbuf_tensor("kv2", [128, NP, 128], BF16))
        ks_bd = A(nc.sbuf_tensor("ks_bd", [128, NH], BF16))

        with contextlib.ExitStack() as e3:
            A3 = e3.enter_context
            xln = A3(nc.sbuf_tensor("xln", [128, 2, D], BF16))
            xf = A3(nc.sbuf_tensor("xf", [128, 2, NDC, 128], BF16))
            scrA = A3(nc.sbuf_tensor("scrA", [128, 2, D], F32))
            scrB = A3(nc.sbuf_tensor("scrB", [128, 2, D], F32))
            scrC = A3(nc.sbuf_tensor("scrC", [128, 2, D], F32))
            kf_t = A3(nc.sbuf_tensor("kf_t", [128, 2, D], BF16))
            v_t = A3(nc.sbuf_tensor("v_t", [128, 2, D], BF16))
            sxt = DmaSlots(nc, SA, "sxt", 2)
            sxln = SA(nc.semaphore("sxln"))
            sxf = DmaSlots(nc, SA, "sxf", 2)
            skmm = SA(nc.semaphore("skmm"))
            svmm = SA(nc.semaphore("svmm"))
            sminev = SA(nc.semaphore("sminev"))
            skev = SA(nc.semaphore("skev"))
            sExp = SA(nc.semaphore("sExp"))
            skf = SA(nc.semaphore("skf"))
            svev = SA(nc.semaphore("svev"))
            skvmm = SA(nc.semaphore("skvmm"))
            kps = A3(nc.psum_tensor("kps", [128, D], F32))
            vps = A3(nc.psum_tensor("vps", [128, D], F32))
            kvps = A3(nc.psum_tensor("kvps", [128, NP, 128], F32))
            ksps = A3(nc.psum_tensor("ksps", [128, NDC], F32))
            bstk = contextlib.ExitStack()
            blk = bstk.enter_context(nc.Block())

            @blk.gpsimd
            def _(g):
                for t in range(NTF):
                    if t >= 2:
                        g.wait_ge(sxln, t - 1)
                    sxt.inc(g.dma_start(xt[:, t % 2, :], tile_src(t)), t)

            @blk.vector
            def _(v):
                v = DrainingEngine(v)
                for t in range(NTF):
                    sxt.wait(v, t)
                    if t >= 2:
                        sxf.wait(v, t - 2)
                    v.tensor_scalar(xln[:, t % 2, :], xt[:, t % 2, :],
                                    s_negm[:, t:t + 1], s_rstd[:, t:t + 1],
                                    ALU.add, ALU.mult).then_inc(sxln, 1)
                    v.wait_ge(skmm, t + 1)
                    if t >= 2:
                        v.wait_ge(sExp, t - 1)
                    v.tensor_scalar_min(scrA[:, t % 2, :], kps[:], 0.0).then_inc(sminev, 1)
                    v.tensor_scalar_max(scrC[:, t % 2, :], kps[:], 0.0).then_inc(skev, 1)
                    v.wait_ge(sExp, t + 1)
                    if t >= 2:
                        v.wait_ge(skvmm, t - 1)
                    v.tensor_tensor(kf_t[:, t % 2, :], scrB[:, t % 2, :],
                                    scrC[:, t % 2, :], ALU.add).then_inc(skf, 1)
                    v.wait_ge(svmm, t + 1)
                    v.tensor_copy(v_t[:, t % 2, :], vps[:]).then_inc(svev, 1)

            @blk.scalar
            def _(sc):
                sc = DrainingEngine(sc)
                for t in range(NTF):
                    sc.wait_ge(sminev, t + 1)
                    if t >= 2:
                        sc.wait_ge(skf, t - 1)
                    sc.activation(scrB[:, t % 2, :], scrA[:, t % 2, :],
                                  AF.Exp).then_inc(sExp, 1)

            @blk.sync
            def _(sy_):
                for t in range(NTF):
                    sy_.wait_ge(sxln, t + 1)
                    dst = (xf_own.ap()[:, t, :, :] if t < NT
                           else xf.ap()[:, t % 2, :, :])
                    sxf.inc(sy_.dma_start(
                        dst,
                        xln.ap()[:, t % 2, :].rearrange("p (c q) -> p c q", c=NDC),
                        transpose=True,
                    ), t)

            @blk.tensor
            def _(t_):
                for t in range(NTF):
                    sxf.wait(t_, t)
                    xfv = (xf_own.ap()[:, t, :, :] if t < NT
                           else xf.ap()[:, t % 2, :, :])
                    if t >= 1:
                        t_.wait_ge(skev, t)
                    for j in range(2):
                        for c in range(NDC):
                            mm_ = t_.matmul(kps[:, j * 512:(j + 1) * 512],
                                      xfv[:, c, :],
                                      wkT.ap()[:, 4 * j:4 * j + 4, c, :],
                                      start=(c == 0), stop=(c == NDC - 1))
                    mm_.then_inc(skmm, 1)
                    if t >= 1:
                        t_.wait_ge(svev, t)
                    for j in range(2):
                        for c in range(NDC):
                            mm_ = t_.matmul(vps[:, j * 512:(j + 1) * 512],
                                      xfv[:, c, :],
                                      wvT.ap()[:, 4 * j:4 * j + 4, c, :],
                                      start=(c == 0), stop=(c == NDC - 1))
                    mm_.then_inc(svmm, 1)
                    if t >= 1:
                        tp = t - 1
                        t_.wait_ge(skf, tp + 1)
                        t_.wait_ge(svev, tp + 1)
                        for pp in range(NP):
                            mm_ = t_.matmul(kvps[:, pp, :],
                                      kf_t[:, tp % 2, pp * 128:(pp + 1) * 128],
                                      v_t[:, tp % 2, pp * 128:(pp + 1) * 128],
                                      start=(tp == 0 and pp % 4 == 0),
                                      stop=(tp == NTF - 1),
                                      skip_group_check=True)
                        for c in range(NDC):
                            mm_ = t_.matmul(ksps[:, c:c + 1],
                                      kf_t[:, tp % 2, c * 128:(c + 1) * 128],
                                      ones_c[:],
                                      start=(tp == 0 and c == 0),
                                      stop=(tp == NTF - 1),
                                      skip_group_check=True)
                        mm_.then_inc(skvmm, 1)
                tp = NTF - 1
                t_.wait_ge(skf, tp + 1)
                t_.wait_ge(svev, tp + 1)
                for pp in range(NP):
                    mm_ = t_.matmul(kvps[:, pp, :],
                              kf_t[:, tp % 2, pp * 128:(pp + 1) * 128],
                              v_t[:, tp % 2, pp * 128:(pp + 1) * 128],
                              start=False, stop=True, skip_group_check=True)
                for c in range(NDC):
                    mm_ = t_.matmul(ksps[:, c:c + 1],
                              kf_t[:, tp % 2, c * 128:(c + 1) * 128],
                              ones_c[:],
                              start=False, stop=True, skip_group_check=True)
                mm_.then_inc(skvmm, 1)

            bstk.close()
            blk2 = bstk.enter_context(nc.Block())

            @blk2.vector
            def _(v):
                v = DrainingEngine(v)
                v.memset(kv2[:], 0.0)
                v.memset(ks_bd[:], 0.0)
                for pp in range(NP):
                    v.tensor_copy(kv2[0:64, pp, 0:64], kvps[0:64, pp, 0:64])
                    v.tensor_copy(kv2[64:128, pp, 64:128], kvps[64:128, pp, 64:128])
                    v.tensor_copy(ks_bd[0:64, 2 * pp:2 * pp + 1], ksps[0:64, pp:pp + 1])
                    v.tensor_copy(ks_bd[64:128, 2 * pp + 1:2 * pp + 2],
                                  ksps[64:128, pp:pp + 1])
            bstk.close()

        with contextlib.ExitStack() as e4:
            A4 = e4.enter_context
            scrA = A4(nc.sbuf_tensor("q_scrA", [128, 2, 512], F32))
            scrB = A4(nc.sbuf_tensor("q_scrB", [128, 2, 512], F32))
            scrC = A4(nc.sbuf_tensor("q_scrC", [128, 2, 512], F32))
            sqmm = SA(nc.semaphore("sqmm"))
            sminev = SA(nc.semaphore("q_sminev"))
            sqev = SA(nc.semaphore("q_sqev"))
            sExp = SA(nc.semaphore("q_sExp"))
            sqf = SA(nc.semaphore("q_sqf"))
            qps = A4(nc.psum_tensor("qps", [128, 2, 512], F32))
            blk = A4(nc.Block())
            NSLC = S_CORE // 512

            @blk.tensor
            def _(t_):
                k = 0
                for co in range(NDC):
                    for st in range(NSLC):
                        if k >= 2:
                            t_.wait_ge(sqev, k - 1)
                        for ci in range(NDC):
                            mm_ = t_.matmul(qps[:, k % 2, :],
                                      wqT.ap()[:, co, ci, :],
                                      xf_own.ap()[:, 4 * st:4 * st + 4, ci, :],
                                      start=(ci == 0), stop=(ci == NDC - 1))
                        mm_.then_inc(sqmm, 1)
                        k += 1

            @blk.vector
            def _(v):
                v = DrainingEngine(v)
                k = 0
                for co in range(NDC):
                    for st in range(NSLC):
                        v.wait_ge(sqmm, k + 1)
                        if k >= 2:
                            v.wait_ge(sExp, k - 1)
                        v.tensor_scalar_min(scrA[:, k % 2, :], qps[:, k % 2, :],
                                            0.0).then_inc(sminev, 1)
                        v.tensor_scalar_max(scrC[:, k % 2, :], qps[:, k % 2, :],
                                            0.0).then_inc(sqev, 1)
                        v.wait_ge(sExp, k + 1)
                        v.tensor_tensor(qf_fm[:, co, st * 512:(st + 1) * 512],
                                        scrB[:, k % 2, :], scrC[:, k % 2, :],
                                        ALU.add).then_inc(sqf, 1)
                        k += 1

            @blk.scalar
            def _(sc):
                sc = DrainingEngine(sc)
                for k in range(NDC * NSLC):
                    sc.wait_ge(sminev, k + 1)
                    if k >= 2:
                        sc.wait_ge(sqf, k - 1)
                    sc.activation(scrB[:, k % 2, :], scrA[:, k % 2, :],
                                  AF.Exp).then_inc(sExp, 1)

        with contextlib.ExitStack() as e5:
            A5 = e5.enter_context
            att = A5(nc.sbuf_tensor("att", [128, 2, D], BF16))
            af = A5(nc.sbuf_tensor("af", [128, 2, NDC, 128], BF16))
            dnv = A5(nc.sbuf_tensor("dnv", [128, 2, NH], F32))
            dscr = A5(nc.sbuf_tensor("dscr", [128, 2 * NH], F32))
            y_sb = A5(nc.sbuf_tensor("y_sb", [128, 2, D], BF16))
            sden = SA(nc.semaphore("sden"))
            sdev = SA(nc.semaphore("sdev"))
            satt = SA(nc.semaphore("satt"))
            saf = DmaSlots(nc, SA, "saf", 2)
            sop = SA(nc.semaphore("sop"))
            sy2 = SA(nc.semaphore("sy2"))
            spsq = SA(nc.semaphore("spsq"))
            syst = DmaSlots(nc, SA, "syst", 2)
            dps_d = A5(nc.psum_tensor("dps_d", [128, 2, 512], F32))
            qk_ps = A5(nc.psum_tensor("qk_ps", [128, 2, NP, 128], F32))
            ops = A5(nc.psum_tensor("ops", [128, D], F32))
            blk = A5(nc.Block())

            @blk.tensor
            def _(t_):
                for i in range(NT):
                    sl = slice(i * 128, (i + 1) * 128)
                    if i >= 2:
                        t_.wait_ge(sdev, i - 1)
                        t_.wait_ge(satt, i - 1)
                    for c in range(NDC):
                        mm_ = t_.matmul(dps_d[:, i % 2, 2 * c:2 * c + 2],
                                  qf_fm[:, c, sl],
                                  ks_bd[:, 2 * c:2 * c + 2],
                                  start=(c == 0), stop=True,
                                  skip_group_check=True)
                        mm_ = t_.matmul(qk_ps[:, i % 2, c, :],
                                  qf_fm[:, c, sl],
                                  kv2[:, c, :],
                                  start=True, stop=True)
                    mm_.then_inc(sden, 1)
                    if i >= 1:
                        ip = i - 1
                        saf.wait(t_, ip)
                        if ip >= 1:
                            t_.wait_ge(sy2, ip)
                        for j in range(2):
                            mm_ = t_.matmul(ops[:, j * 512:(j + 1) * 512],
                                      ones_r[:], bo_bf.ap()[0:1, j * 512:(j + 1) * 512],
                                      start=True, stop=False)
                            for c in range(NDC):
                                mm_ = t_.matmul(ops[:, j * 512:(j + 1) * 512],
                                          af[:, ip % 2, c, :],
                                          woT.ap()[:, 4 * j:4 * j + 4, c, :],
                                          start=False, stop=(c == NDC - 1))
                        mm_.then_inc(sop, 1)
                ip = NT - 1
                saf.wait(t_, ip)
                t_.wait_ge(sy2, ip)
                for j in range(2):
                    mm_ = t_.matmul(ops[:, j * 512:(j + 1) * 512],
                              ones_r[:], bo_bf.ap()[0:1, j * 512:(j + 1) * 512],
                              start=True, stop=False)
                    for c in range(NDC):
                        mm_ = t_.matmul(ops[:, j * 512:(j + 1) * 512],
                                  af[:, ip % 2, c, :],
                                  woT.ap()[:, 4 * j:4 * j + 4, c, :],
                                  start=False, stop=(c == NDC - 1))
                mm_.then_inc(sop, 1)

            @blk.vector
            def _(v):
                v = DrainingEngine(v)
                for i in range(NT):
                    v.wait_ge(sden, i + 1)
                    v.tensor_scalar_add(dnv[:, i % 2, :], dps_d[:, i % 2, 0:NH], EPS)
                    emit_recip_newton(v, dnv[:, i % 2, :], dnv[:, i % 2, :],
                                      dscr[:, 0:NH], dscr[:, NH:2 * NH]
                                      ).then_inc(sdev, 1)
                    if i >= 2:
                        saf.wait(v, i - 2)
                    v.tensor_tensor(
                        att.ap()[:, i % 2, :].rearrange("p (c t q) -> p c t q", c=NP, t=2),
                        qk_ps.ap()[:, i % 2, :, :].rearrange("p c (t q) -> p c t q", t=2),
                        dnv.ap()[:, i % 2, :].rearrange("p (c t) -> p c t", c=NP)
                        .unsqueeze(3).broadcast_to([128, NP, 2, 64]),
                        ALU.mult).then_inc(satt, 1)
                    v.wait_ge(sop, i + 1)
                    if i >= 2:
                        syst.wait(v, i - 2)
                        v.wait_ge(spsq, i - 1)
                    v.tensor_scalar(y_sb[:, i % 2, :], ops[:], 1.0, None,
                                    ALU.mult, ALU.add,
                                    accum_out=p_sum[:, i:i + 1]).then_inc(sy2, 1)

            @blk.scalar
            def _(sc):
                sc = DrainingEngine(sc)
                for i in range(NT):
                    sc.wait_ge(sy2, i + 1)
                    sc.activation(scr_sq[:, i % 2, :], y_sb[:, i % 2, :], AF.Square,
                                  accum_out=p_sumsq[:, i:i + 1]).then_inc(spsq, 1)

            @blk.sync
            def _(sy_):
                for i in range(NT):
                    sy_.wait_ge(satt, i + 1)
                    saf.inc(sy_.dma_start(
                        af.ap()[:, i % 2, :, :],
                        att.ap()[:, i % 2, :].rearrange("p (c q) -> p c q", c=NDC),
                        transpose=True,
                    ), i)
                    sy_.wait_ge(sy2, i + 1)
                    syst.inc(sy_.dma_start(y_scr.ap()[i * 128:(i + 1) * 128, :],
                                           y_sb[:, i % 2, :]), i)
                for k in range(2):
                    sy_.wait_ge(syst.sems[k], 16 * ((NT + 1 - k) // 2))

        with nc.Block() as blk:
            emit_stat_math_phase(blk, p_sum, p_sumsq, p_negm, p_var, p_rstd, NT, D)
        with nc.Block() as blk:
            emit_rstd_phase(blk, p_var, p_rstd, NT)
        with contextlib.ExitStack() as es6:
            emit_residual_phase(nc, es6.enter_context,
                                lambda i: x_in.ap()[i * 128:(i + 1) * 128, :],
                                lambda i: y_scr.ap()[i * 128:(i + 1) * 128, :],
                                lambda i: x_out.ap()[i * 128:(i + 1) * 128, :],
                                NT, xt, p_negm, p_rstd, SA=SA)


def build_l0_attn():
    nc0 = bass.Bass("TRN2", target_bir_lowering=False)
    nc = NcView(nc0, "a_")
    x_in = nc0.dram_tensor("x_in", [S_CORE, D], F32, kind="ExternalInput")
    x_oth = nc0.dram_tensor("x_oth", [S_CORE, D], F32, kind="ExternalInput")
    wqd = nc0.dram_tensor("wq", [D, D], F32, kind="ExternalInput")
    wkd = nc0.dram_tensor("wk", [D, D], F32, kind="ExternalInput")
    wvd = nc0.dram_tensor("wv", [D, D], F32, kind="ExternalInput")
    wod = nc0.dram_tensor("wo", [D, D], F32, kind="ExternalInput")
    bod = nc0.dram_tensor("bo", [1, D], F32, kind="ExternalInput")
    x_out = nc0.dram_tensor("x_out", [S_CORE, D], F32, kind="ExternalOutput")
    with contextlib.ExitStack() as es:
        emit_l0_attn(nc, lambda h: h, x_in, x_oth, wqd, wkd, wvd, wod, bod, x_out)
    return nc0





def emit_l1_attn(nc, SA, x_in, x_oth, wds, bds, x_out, fused=False,
                  replica_groups=None):
    SA2 = SA
    NTF = 2 * NT
    SM = 1.0 / 32.0
    w1d, w2d, w3d, w4d = wds
    b1d, b2d, b3d, b4d = bds

    y_scr = nc.dram_tensor("y_scr", [S_CORE, D], BF16)
    q_scr = nc.dram_tensor("q_scr", [NT, 128, NDC * 128], BF16)
    if fused:
        cc_in = nc.dram_tensor("cc_in", [128, 2 * NT * 1024], BF16)
        cc_out = nc.dram_tensor("cc_out", [256, 2 * NT * 1024], BF16)

    def tile_src(t):
        if t < NT:
            return x_in.ap()[t * 128:(t + 1) * 128, :]
        return x_oth.ap()[(t - NT) * 128:(t - NT + 1) * 128, :]

    NTP = NT if fused else NTF
    with contextlib.ExitStack() as es:
        A = es.enter_context
        K_fm = A(nc.sbuf_tensor("K_fm", [128, NTF, NDC, 128], BF16))
        V_tm = A(nc.sbuf_tensor("V_tm", [128, NTF, D], BF16))
        b_bf = A(nc.sbuf_tensor("b_bf", [1, 4, D], BF16))
        ones_r = A(nc.sbuf_tensor("ones_r", [1, 128], BF16))
        xt = A(nc.sbuf_tensor("xt", [128, 2, D], F32))
        scr_sq = A(nc.sbuf_tensor("scr_sq", [128, 1, D], F32))
        scr_cp = A(nc.sbuf_tensor("scr_cp", [128, 1, D], BF16))
        s_sum = A(nc.sbuf_tensor("s_sum", [128, NTF], F32))
        s_sumsq = A(nc.sbuf_tensor("s_sumsq", [128, NTF], F32))
        s_negm = A(nc.sbuf_tensor("s_negm", [128, NTF], F32))
        s_var = A(nc.sbuf_tensor("s_var", [128, NTF], F32))
        s_rstd = A(nc.sbuf_tensor("s_rstd", [128, NTF], F32))
        p_sum = A(nc.sbuf_tensor("p_sum", [128, NT], F32))
        p_sumsq = A(nc.sbuf_tensor("p_sumsq", [128, NT], F32))
        p_negm = A(nc.sbuf_tensor("p_negm", [128, NT], F32))
        p_var = A(nc.sbuf_tensor("p_var", [128, NT], F32))
        p_rstd = A(nc.sbuf_tensor("p_rstd", [128, NT], F32))
        dwg = SA(nc.semaphore("dwg"))

        with nc.Block() as blk:
            @blk.gpsimd
            def _(g):
                for bi, bd in enumerate((b1d, b2d, b3d, b4d)):
                    g.dma_start(b_bf[0:1, bi, :], bd.ap()).then_inc(dwg, 16)
                g.wait_ge(dwg, 64)
            @blk.vector
            def _(v):
                v.memset(ones_r[:], 1.0)

        sx = DmaSlots(nc, SA, "sx", 2)
        ssq = SA(nc.semaphore("ssq"))
        ssum = SA(nc.semaphore("ssum"))
        with nc.Block() as blk:
            emit_ln_stats_phase(blk, tile_src, NTP, xt, sx, ssq, ssum,
                                s_sumsq, s_sum, scr_sq, scr_cp)
        with nc.Block() as blk:
            emit_stat_math_phase(blk, s_sum, s_sumsq, s_negm, s_var, s_rstd, NTP, D)
        with nc.Block() as blk:
            emit_rstd_phase(blk, s_var, s_rstd, NTP)

        def emit_proj_phase(A3, tiles, q_mode):
            if q_mode:
                w1 = A3(nc.sbuf_tensor("w1b", [128, NDC, D], BF16))
                wlist = [(w1d, w1)]
            else:
                w2 = A3(nc.sbuf_tensor("w2b", [128, NDC, D], BF16))
                w3 = A3(nc.sbuf_tensor("w3b", [128, NDC, D], BF16))
                wlist = [(w2d, w2), (w3d, w3)]
            pref = "q" if q_mode else "kv"
            dwl = SA(nc.semaphore(pref + "_dwl"))
            with nc.Block() as blk:
                @blk.gpsimd
                def _(g):
                    for wi, (wd_, wt) in enumerate(wlist):
                        g.dma_start(wt[:],
                                    wd_.ap().rearrange("(c p) o -> p c o", p=128)
                                    ).then_inc(dwl, 16)
                    g.wait_ge(dwl, 16 * len(wlist))

            xln = A3(nc.sbuf_tensor(pref + "_xln", [128, 2, D], BF16))
            xf = A3(nc.sbuf_tensor(pref + "_xf", [128, 2, NDC, 128], BF16))
            sxt = DmaSlots(nc, SA, pref + "_sxt", 2)
            sxln = SA(nc.semaphore(pref + "_sxln"))
            sxf = DmaSlots(nc, SA, pref + "_sxf", 2)
            if q_mode:
                qtile = A3(nc.sbuf_tensor("qtile", [128, 2, NDC * 128], BF16))
                sqmm = SA(nc.semaphore("sqmm"))
                sqev = SA(nc.semaphore("sqev"))
                sqst = DmaSlots(nc, SA, "sqst", 2)
                qps = A3(nc.psum_tensor("qps", [128, 2, NDC, 128], F32))
            else:
                skmm = SA(nc.semaphore("skmm"))
                skev = SA(nc.semaphore("skev"))
                svmm = SA(nc.semaphore("svmm"))
                svev = SA(nc.semaphore("svev"))
                kps = A3(nc.psum_tensor("kps", [128, 2, NDC, 128], F32))
                vps = A3(nc.psum_tensor("vps", [128, D], F32))
            blk = A3(nc.Block())

            @blk.gpsimd
            def _(g):
                for t in tiles:
                    ti = tiles.index(t)
                    if ti >= 2:
                        g.wait_ge(sxln, ti - 1)
                    sxt.inc(g.dma_start(xt[:, ti % 2, :], tile_src(t)), ti)

            @blk.vector
            def _(v):
                v = DrainingEngine(v)
                for ti, t in enumerate(tiles):
                    sxt.wait(v, ti)
                    if ti >= 2:
                        sxf.wait(v, ti - 2)
                    v.tensor_scalar(xln[:, ti % 2, :], xt[:, ti % 2, :],
                                    s_negm[:, t:t + 1], s_rstd[:, t:t + 1],
                                    ALU.add, ALU.mult).then_inc(sxln, 1)
                    if q_mode:
                        v.wait_ge(sqmm, ti + 1)
                        if ti >= 2:
                            sqst.wait(v, ti - 2)
                        v.tensor_copy(qtile[:, ti % 2, :],
                                      qps.ap()[:, ti % 2, :, :]
                                      .rearrange("p a b -> p (a b)")).then_inc(sqev, 1)
                    else:
                        v.wait_ge(skmm, ti + 1)
                        v.tensor_copy(K_fm.ap()[:, t, :, :]
                                      .rearrange("p a b -> p (a b)"),
                                      kps.ap()[:, ti % 2, :, :]
                                      .rearrange("p a b -> p (a b)")).then_inc(skev, 1)
                        v.wait_ge(svmm, ti + 1)
                        v.tensor_copy(V_tm[:, t, :], vps[:]).then_inc(svev, 1)

            @blk.sync
            def _(sy_):
                for ti, t in enumerate(tiles):
                    sy_.wait_ge(sxln, ti + 1)
                    sxf.inc(sy_.dma_start(
                        xf.ap()[:, ti % 2, :, :],
                        xln.ap()[:, ti % 2, :].rearrange("p (c q) -> p c q", c=NDC),
                        transpose=True,
                    ), ti)
                    if q_mode:
                        sy_.wait_ge(sqev, ti + 1)
                        sqst.inc(sy_.dma_start(q_scr.ap()[t, :, :],
                                               qtile[:, ti % 2, :]), ti)
                if q_mode:
                    for k in range(2):
                        sy_.wait_ge(sqst.sems[k], 16 * ((len(tiles) + 1 - k) // 2))

            @blk.tensor
            def _(t_):
                for ti, t in enumerate(tiles):
                    sxf.wait(t_, ti)
                    if q_mode:
                        if ti >= 2:
                            t_.wait_ge(sqev, ti - 1)
                        for ce in range(NDC):
                            mm_ = t_.matmul(qps[:, ti % 2, ce, :],
                                      b_bf.ap()[0:1, 0, ce * 128:(ce + 1) * 128],
                                      ones_r[:], start=True, stop=False)
                            for ci in range(NDC):
                                mm_ = t_.matmul(qps[:, ti % 2, ce, :],
                                          w1[:, ci, ce * 128:(ce + 1) * 128],
                                          xf[:, ti % 2, ci, :],
                                          start=False, stop=(ci == NDC - 1))
                        mm_.then_inc(sqmm, 1)
                    else:
                        if ti >= 2:
                            t_.wait_ge(skev, ti - 1)
                        for ce in range(NDC):
                            mm_ = t_.matmul(kps[:, ti % 2, ce, :],
                                      b_bf.ap()[0:1, 1, ce * 128:(ce + 1) * 128],
                                      ones_r[:], start=True, stop=False)
                            for ci in range(NDC):
                                mm_ = t_.matmul(kps[:, ti % 2, ce, :],
                                          w2[:, ci, ce * 128:(ce + 1) * 128],
                                          xf[:, ti % 2, ci, :],
                                          start=False, stop=(ci == NDC - 1))
                        mm_.then_inc(skmm, 1)
                        if ti >= 1:
                            t_.wait_ge(svev, ti)
                        for j in range(2):
                            mm_ = t_.matmul(vps[:, j * 512:(j + 1) * 512], ones_r[:],
                                      b_bf.ap()[0:1, 2, j * 512:(j + 1) * 512],
                                      start=True, stop=False)
                            for ci in range(NDC):
                                mm_ = t_.matmul(vps[:, j * 512:(j + 1) * 512],
                                          xf[:, ti % 2, ci, :],
                                          w3[:, ci, j * 512:(j + 1) * 512],
                                          start=False, stop=(ci == NDC - 1))
                        mm_.then_inc(svmm, 1)

        with contextlib.ExitStack() as e1:
            emit_proj_phase(e1.enter_context, list(range(NTP)), False)
        if fused:
            ccst = SA(nc.semaphore("ccst"))
            ccld = SA(nc.semaphore("ccld"))
            ccdone = SA(nc.semaphore("ccdone"))
            KW = NT * 1024
            with nc.Block() as blk:
                @blk.sync
                def _(sy_):
                    sy_.dma_start(cc_in.ap()[:, 0:KW],
                                  K_fm.ap()[:, 0:NT, :, :]
                                  .rearrange("p a b c -> p (a b c)")).then_inc(ccst, 16)
                    sy_.dma_start(cc_in.ap()[:, KW:2 * KW],
                                  V_tm.ap()[:, 0:NT, :]
                                  .rearrange("p a b -> p (a b)")).then_inc(ccst, 16)
                    sy_.wait_ge(ccst, 32)
            with nc.Block() as blk:
                @blk.gpsimd
                def _(g):
                    g.collective_compute(
                        "AllGather", ALU.bypass,
                        replica_groups=replica_groups,
                        ins=[cc_in.ap().opt()],
                        outs=[cc_out.ap().opt()]).then_inc(ccdone, 1)
                    g.wait_ge(ccdone, 1)
            with nc.Block() as blk:
                @blk.sync
                def _(sy_):
                    for r in range(2):
                        sy_.dma_start(
                            K_fm.ap()[:, r * NT:(r + 1) * NT, :, :]
                            .rearrange("p a b c -> p (a b c)"),
                            cc_out.ap()[r * 128:(r + 1) * 128, 0:KW]).then_inc(ccld, 16)
                        sy_.dma_start(
                            V_tm.ap()[:, r * NT:(r + 1) * NT, :]
                            .rearrange("p a b -> p (a b)"),
                            cc_out.ap()[r * 128:(r + 1) * 128, KW:2 * KW]).then_inc(ccld, 16)
                    sy_.wait_ge(ccld, 64)
        with contextlib.ExitStack() as e1:
            emit_proj_phase(e1.enter_context, list(range(NT)), True)

        with contextlib.ExitStack() as e2:
            A2 = e2.enter_context
            w4 = A2(nc.sbuf_tensor("w4b", [128, NDC, D], BF16))
            dwl = SA(nc.semaphore("c2_dwl"))
            with nc.Block() as blk:
                @blk.gpsimd
                def _(g):
                    g.dma_start(w4[:], w4d.ap().rearrange("(c p) o -> p c o", p=128)
                                ).then_inc(dwl, 16)
                    g.wait_ge(dwl, 16)

            qt = A2(nc.sbuf_tensor("qt", [128, 2, NDC, 128], BF16))
            sc_sb = A2(nc.sbuf_tensor("sc_sb", [128, S], F32))
            mx8 = A2(nc.sbuf_tensor("mx8", [128, 8], F32))
            bias_c = A2(nc.sbuf_tensor("bias_c", [128, 1], F32))
            l4 = A2(nc.sbuf_tensor("l4", [128, 4], F32))
            rl = A2(nc.sbuf_tensor("rl", [128, 1], F32))
            rscr = A2(nc.sbuf_tensor("rscr", [128, 2], F32))
            P = A2(nc.sbuf_tensor("P", [128, 2, 1024], BF16))
            PT = A2(nc.sbuf_tensor("PT", [128, 2, 8, 128], BF16))
            att = A2(nc.sbuf_tensor("att", [128, 2, D], BF16))
            af = A2(nc.sbuf_tensor("af", [128, 2, NDC, 128], BF16))
            y_sb = A2(nc.sbuf_tensor("y_sb", [128, 2, D], BF16))
            sq = DmaSlots(nc, SA, "c2_sq", 2)
            ssc = SA(nc.semaphore("ssc"))
            ssce = SA(nc.semaphore("ssce"))
            sbias = SA(nc.semaphore("sbias"))
            sexp = SA(nc.semaphore("sexp"))
            spt = DmaSlots(nc, SA, "c2_spt", 2)
            samm = SA(nc.semaphore("samm"))
            sptd = SA(nc.semaphore("sptd"))
            srl = SA(nc.semaphore("srl"))
            satt = SA(nc.semaphore("satt"))
            saf = DmaSlots(nc, SA, "c2_saf", 2)
            sop = SA(nc.semaphore("sop"))
            sy2 = SA(nc.semaphore("sy2"))
            spsq = SA(nc.semaphore("spsq"))
            syst = DmaSlots(nc, SA, "c2_syst", 2)
            sc_ps = A2(nc.psum_tensor("sc_ps", [128, 2, 512], F32))
            aps = A2(nc.psum_tensor("aps", [128, D], F32))
            ops = A2(nc.psum_tensor("ops", [128, D], F32))
            blk = A2(nc.Block())

            @blk.gpsimd
            def _(g):
                for i in range(NT):
                    if i >= 2:
                        g.wait_ge(sptd, 4 * (i - 1))
                    sq.inc(g.dma_start(
                        qt.ap()[:, i % 2, :, :].rearrange("p a b -> p (a b)"),
                        q_scr.ap()[i, :, :]), i)

            @blk.tensor
            def _(t_):
                for i in range(NT):
                    sq.wait(t_, i)
                    for j in range(8):
                        gidx = i * 8 + j
                        if gidx >= 2:
                            t_.wait_ge(ssce, gidx - 1)
                        for ce in range(NDC):
                            mm_ = t_.matmul(sc_ps[:, j % 2, :],
                                      qt[:, i % 2, ce, :],
                                      K_fm.ap()[:, 4 * j:4 * j + 4, ce, :],
                                      start=(ce == 0), stop=(ce == NDC - 1))
                        mm_.then_inc(ssc, 1)
                    if i >= 1:
                        t_.wait_ge(satt, i)
                    for c in range(4):
                        spt.wait(t_, i * 4 + c)
                        for tt in range(8):
                            for j2 in range(2):
                                mm_ = t_.matmul(aps[:, j2 * 512:(j2 + 1) * 512],
                                          PT[:, c % 2, tt, :],
                                          V_tm[:, 8 * c + tt,
                                               j2 * 512:(j2 + 1) * 512],
                                          start=(c == 0 and tt == 0),
                                          stop=(c == 3 and tt == 7),
                                          skip_group_check=True)
                        mm_.then_inc(sptd, 1)
                    if i >= 1:
                        ip = i - 1
                        saf.wait(t_, ip)
                        if ip >= 1:
                            t_.wait_ge(sy2, ip)
                        for j in range(2):
                            mm_ = t_.matmul(ops[:, j * 512:(j + 1) * 512], ones_r[:],
                                      b_bf.ap()[0:1, 3, j * 512:(j + 1) * 512],
                                      start=True, stop=False)
                            for ce in range(NDC):
                                mm_ = t_.matmul(ops[:, j * 512:(j + 1) * 512],
                                          af[:, ip % 2, ce, :],
                                          w4[:, ce, j * 512:(j + 1) * 512],
                                          start=False, stop=(ce == NDC - 1))
                        mm_.then_inc(sop, 1)
                ip = NT - 1
                saf.wait(t_, ip)
                t_.wait_ge(sy2, ip)
                for j in range(2):
                    mm_ = t_.matmul(ops[:, j * 512:(j + 1) * 512], ones_r[:],
                              b_bf.ap()[0:1, 3, j * 512:(j + 1) * 512],
                              start=True, stop=False)
                    for ce in range(NDC):
                        mm_ = t_.matmul(ops[:, j * 512:(j + 1) * 512],
                                  af[:, ip % 2, ce, :],
                                  w4[:, ce, j * 512:(j + 1) * 512],
                                  start=False, stop=(ce == NDC - 1))
                mm_.then_inc(sop, 1)

            @blk.vector
            def _(v):
                v = DrainingEngine(v)

                def emit_yev(v, i):
                    v.wait_ge(sop, i + 1)
                    if i >= 2:
                        syst.wait(v, i - 2)
                        v.wait_ge(spsq, i - 1)
                    v.tensor_scalar(y_sb[:, i % 2, :], ops[:], 1.0, None,
                                    ALU.mult, ALU.add,
                                    accum_out=p_sum[:, i:i + 1]).then_inc(sy2, 1)
                for i in range(NT):
                    for j in range(8):
                        gidx = i * 8 + j
                        v.wait_ge(ssc, gidx + 1)
                        v.tensor_copy(sc_sb[:, j * 512:(j + 1) * 512],
                                      sc_ps[:, j % 2, :])
                        v.reduce_max(mx8[:, j:j + 1], sc_sb[:, j * 512:(j + 1) * 512],
                                     axis=AX).then_inc(ssce, 1)
                    v.reduce_max(bias_c[:], mx8[:], axis=AX)
                    v.tensor_scalar_mul(bias_c[:], bias_c[:], -SM)
                    v.drain().then_inc(sbias, 1)
                    v.wait_ge(sexp, 4 * (i + 1))
                    v.reduce_sum(rl[:], l4[:], axis=AX)
                    emit_recip_newton(v, rl[:], rl[:], rscr[:, 0:1],
                                      rscr[:, 1:2]).then_inc(srl, 1)
                    v.wait_ge(sptd, 4 * (i + 1))
                    if i >= 2:
                        saf.wait(v, i - 2)
                    v.tensor_scalar_mul(att[:, i % 2, :], aps[:], rl[:, 0:1]
                                        ).then_inc(satt, 1)
                    if i >= 1:
                        emit_yev(v, i - 1)
                if NT >= 1:
                    emit_yev(v, NT - 1)

            @blk.scalar
            def _(sc):
                sc = DrainingEngine(sc)
                def emit_sq(i):
                    sc.wait_ge(sy2, i + 1)
                    sc.activation(scr_sq[:, 0, :], y_sb[:, i % 2, :], AF.Square,
                                  accum_out=p_sumsq[:, i:i + 1]).then_inc(spsq, 1)
                for i in range(NT):
                    sc.wait_ge(sbias, i + 1)
                    for c in range(4):
                        if i * 4 + c >= 2:
                            spt.wait(sc, i * 4 + c - 2)
                        sc.activation(P[:, c % 2, :], sc_sb[:, c * 1024:(c + 1) * 1024],
                                      AF.Exp, bias=bias_c[:, 0:1], scale=SM,
                                      accum_out=l4[:, c:c + 1]).then_inc(sexp, 1)
                    if i >= 1:
                        emit_sq(i - 1)
                emit_sq(NT - 1)

            @blk.sync
            def _(sy_):
                def emit_af(i):
                    sy_.wait_ge(satt, i + 1)
                    saf.inc(sy_.dma_start(
                        af.ap()[:, i % 2, :, :],
                        att.ap()[:, i % 2, :].rearrange("p (c q) -> p c q", c=NDC),
                        transpose=True,
                    ), i)

                def emit_yst(i):
                    sy_.wait_ge(sy2, i + 1)
                    syst.inc(sy_.dma_start(y_scr.ap()[i * 128:(i + 1) * 128, :],
                                           y_sb[:, i % 2, :]), i)
                for i in range(NT):
                    for c in range(4):
                        sy_.wait_ge(sexp, i * 4 + c + 1)
                        if i * 4 + c >= 2:
                            sy_.wait_ge(sptd, i * 4 + c - 1)
                        spt.inc(sy_.dma_start(
                            PT.ap()[:, c % 2, :, :],
                            P.ap()[:, c % 2, :].rearrange("p (a b) -> p a b", a=NDC),
                            transpose=True,
                        ), i * 4 + c)
                    emit_af(i)
                    if i >= 1:
                        emit_yst(i - 1)
                emit_yst(NT - 1)
                for k in range(2):
                    sy_.wait_ge(syst.sems[k], 16 * ((NT + 1 - k) // 2))

        with nc.Block() as blk:
            emit_stat_math_phase(blk, p_sum, p_sumsq, p_negm, p_var, p_rstd, NT, D)
        with nc.Block() as blk:
            emit_rstd_phase(blk, p_var, p_rstd, NT)
        with contextlib.ExitStack() as es6:
            emit_residual_phase(nc, es6.enter_context,
                                lambda i: x_in.ap()[i * 128:(i + 1) * 128, :],
                                lambda i: y_scr.ap()[i * 128:(i + 1) * 128, :],
                                lambda i: x_out.ap()[i * 128:(i + 1) * 128, :],
                                NT, xt, p_negm, p_rstd, SA=SA2)


def build_l1_attn():
    nc0 = bass.Bass("TRN2", target_bir_lowering=False)
    nc = NcView(nc0, "c_")
    x_in = nc0.dram_tensor("x_in", [S_CORE, D], F32, kind="ExternalInput")
    x_oth = nc0.dram_tensor("x_oth", [S_CORE, D], F32, kind="ExternalInput")
    wds = [nc0.dram_tensor(f"w{k}", [D, D], F32, kind="ExternalInput") for k in range(1, 5)]
    bds = [nc0.dram_tensor(f"b{k}", [1, D], F32, kind="ExternalInput") for k in range(1, 5)]
    x_out = nc0.dram_tensor("x_out", [S_CORE, D], F32, kind="ExternalOutput")
    with contextlib.ExitStack() as es:
        emit_l1_attn(nc, lambda h: h, x_in, x_oth, wds, bds, x_out)
    return nc0




def build_fused():
    nc0 = bass.Bass("TRN2", target_bir_lowering=False)
    x_in = nc0.dram_tensor("x_in", [S_CORE, D], F32, kind="ExternalInput")
    x_oth = nc0.dram_tensor("x_oth", [S_CORE, D], F32, kind="ExternalInput")
    awq = nc0.dram_tensor("awq", [D, D], F32, kind="ExternalInput")
    awk = nc0.dram_tensor("awk", [D, D], F32, kind="ExternalInput")
    awv = nc0.dram_tensor("awv", [D, D], F32, kind="ExternalInput")
    awo = nc0.dram_tensor("awo", [D, D], F32, kind="ExternalInput")
    abo = nc0.dram_tensor("abo", [1, D], F32, kind="ExternalInput")
    bwg = nc0.dram_tensor("bwg", [4096, D], F32, kind="ExternalInput")
    bwu = nc0.dram_tensor("bwu", [4096, D], F32, kind="ExternalInput")
    bwd = nc0.dram_tensor("bwd", [D, 4096], F32, kind="ExternalInput")
    cws = [nc0.dram_tensor(f"cw{k}", [D, D], F32, kind="ExternalInput") for k in range(1, 5)]
    cbs = [nc0.dram_tensor(f"cb{k}", [1, D], F32, kind="ExternalInput") for k in range(1, 5)]
    dwg = nc0.dram_tensor("dwg", [3072, D], I32, kind="ExternalInput")
    dwu = nc0.dram_tensor("dwu", [3072, D], I32, kind="ExternalInput")
    dwd = nc0.dram_tensor("dwd", [D, 3072], I32, kind="ExternalInput")
    x_out = nc0.dram_tensor("x_out", [S_CORE, D], F32, kind="ExternalOutput")
    x1 = nc0.dram_tensor("x1", [S_CORE, D], F32)
    x2 = nc0.dram_tensor("x2", [S_CORE, D], F32)
    x3 = nc0.dram_tensor("x3", [S_CORE, D], F32)
    rg = [[0, 1], [2, 3], [4, 5], [6, 7]]

    def scoped_sa():
        bag = []

        def SA(h):
            bag.append(h)
            return h
        return bag, SA

    def release(bag):
        nc0.clear_and_free_semaphores(bag)
        nc0.all_engine_barrier()

    bag, SA = scoped_sa()
    emit_l0_attn(NcView(nc0, "a_"), SA, x_in, x_oth, awq, awk, awv, awo, abo, x1)
    release(bag)
    bag, SA = scoped_sa()
    emit_l0_ffn(NcView(nc0, "b_"), SA, x1, bwg, bwu, bwd, x2)
    release(bag)
    bag, SA = scoped_sa()
    emit_l1_attn(NcView(nc0, "c_"), SA, x2, None, cws, cbs, x3,
                 fused=True, replica_groups=rg)
    release(bag)
    bag, SA = scoped_sa()
    emit_l1_ffn(NcView(nc0, "d_"), SA, x3, dwg, dwu, dwd, x_out)
    return nc0


def _run_fused(inputs):
    nc = _get_nc("fused", build_fused)
    x = np.asarray(inputs["L0_attn_x"], dtype=np.float32)
    xs = _scatter_x(x)
    com = {
        "awq": np.ascontiguousarray(inputs["L0_attn_wq"], dtype=np.float32),
        "awk": np.ascontiguousarray(inputs["L0_attn_wk"], dtype=np.float32),
        "awv": np.ascontiguousarray(inputs["L0_attn_wv"], dtype=np.float32),
        "awo": np.ascontiguousarray(inputs["L0_attn_wo"], dtype=np.float32),
        "abo": np.ascontiguousarray(inputs["L0_attn_bo"], dtype=np.float32).reshape(1, D),
        "bwg": np.ascontiguousarray(inputs["L0_ffn_w_gate"], dtype=np.float32),
        "bwu": np.ascontiguousarray(inputs["L0_ffn_w_up"], dtype=np.float32),
        "bwd": np.ascontiguousarray(inputs["L0_ffn_w_down"], dtype=np.float32),
        "dwg": np.ascontiguousarray(inputs["L1_ffn_w_gate"], dtype=np.int32),
        "dwu": np.ascontiguousarray(inputs["L1_ffn_w_up"], dtype=np.int32),
        "dwd": np.ascontiguousarray(inputs["L1_ffn_w_down"], dtype=np.int32),
    }
    for k in range(1, 5):
        com[f"cw{k}"] = np.ascontiguousarray(inputs[f"L1_attn_W{k}"], dtype=np.float32)
        com[f"cb{k}"] = np.ascontiguousarray(inputs[f"L1_attn_b{k}"], dtype=np.float32).reshape(1, D)
    in_maps = []
    for c in range(8):
        m = dict(com)
        m["x_in"] = xs[c]
        m["x_oth"] = xs[c ^ 1]
        in_maps.append(m)
    res = run_bass_kernel_spmd(nc, in_maps, core_ids=list(range(8)))
    return _gather_x(res)



_NC_CACHE = {}


def _get_nc(name, builder):
    if name not in _NC_CACHE:
        _NC_CACHE[name] = builder()
    return _NC_CACHE[name]


def _scatter_x(x):
    out = []
    for core in range(8):
        b, half = core // 2, core % 2
        out.append(np.ascontiguousarray(
            x[b, half * S_CORE:(half + 1) * S_CORE, :], dtype=np.float32))
    return out


def _gather_x(res, key="x_out"):
    out = np.empty((B, S, D), dtype=np.float32)
    for core in range(8):
        b, half = core // 2, core % 2
        out[b, half * S_CORE:(half + 1) * S_CORE, :] = res.results[core][key]
    return out


def _run_l1_ffn(x, wg, wu, wd):
    nc = _get_nc("l1_ffn", build_l1_ffn)
    wg = np.ascontiguousarray(wg, dtype=np.int32)
    wu = np.ascontiguousarray(wu, dtype=np.int32)
    wd = np.ascontiguousarray(wd, dtype=np.int32)
    xs = _scatter_x(x)
    in_maps = [{"x_in": xs[c], "wg": wg, "wu": wu, "wd": wd} for c in range(8)]
    res = run_bass_kernel_spmd(nc, in_maps, core_ids=list(range(8)))
    return _gather_x(res)


def _run_l0_ffn(x, wg, wu, wd):
    nc = _get_nc("l0_ffn", build_l0_ffn)
    wg = np.ascontiguousarray(wg, dtype=np.float32)
    wu = np.ascontiguousarray(wu, dtype=np.float32)
    wd = np.ascontiguousarray(wd, dtype=np.float32)
    xs = _scatter_x(x)
    in_maps = [{"x_in": xs[c], "wg": wg, "wu": wu, "wd": wd} for c in range(8)]
    res = run_bass_kernel_spmd(nc, in_maps, core_ids=list(range(8)))
    return _gather_x(res)


def _run_l0_attn(x, wq, wk, wv, wo, bo):
    nc = _get_nc("l0_attn", build_l0_attn)
    wq = np.ascontiguousarray(wq, dtype=np.float32)
    wk = np.ascontiguousarray(wk, dtype=np.float32)
    wv = np.ascontiguousarray(wv, dtype=np.float32)
    wo = np.ascontiguousarray(wo, dtype=np.float32)
    bo = np.ascontiguousarray(bo, dtype=np.float32).reshape(1, D)
    xs = _scatter_x(x)
    in_maps = []
    for c in range(8):
        oth = c ^ 1
        in_maps.append({"x_in": xs[c], "x_oth": xs[oth], "wq": wq, "wk": wk,
                        "wv": wv, "wo": wo, "bo": bo})
    res = run_bass_kernel_spmd(nc, in_maps, core_ids=list(range(8)))
    return _gather_x(res)


def _run_l1_attn(x, w1, b1, w2, b2, w3, b3, w4, b4):
    nc = _get_nc("l1_attn", build_l1_attn)
    ws = [np.ascontiguousarray(w, dtype=np.float32) for w in (w1, w2, w3, w4)]
    bs = [np.ascontiguousarray(b, dtype=np.float32).reshape(1, D) for b in (b1, b2, b3, b4)]
    xs = _scatter_x(x)
    in_maps = []
    for c in range(8):
        m = {"x_in": xs[c], "x_oth": xs[c ^ 1]}
        for k in range(4):
            m[f"w{k+1}"] = ws[k]
            m[f"b{k+1}"] = bs[k]
        in_maps.append(m)
    res = run_bass_kernel_spmd(nc, in_maps, core_ids=list(range(8)))
    return _gather_x(res)


def kernel(**inputs):
    return _run_fused(inputs)


# version 12
# speedup vs baseline: 1.0316x; 1.0223x over previous
import contextlib

import numpy as np
import ml_dtypes
import concourse.bass as bass
import concourse.mybir as mybir
from concourse.bass_utils import run_bass_kernel_spmd

F32 = mybir.dt.float32
BF16 = mybir.dt.bfloat16
I32 = mybir.dt.int32
AF = mybir.ActivationFunctionType
ALU = mybir.AluOpType
AX = mybir.AxisListType.X

B = 4
S = 4096
D = 1024
S_CORE = 2048
NT = S_CORE // 128
NDC = D // 128
MAGIC = float(np.float32(1.5 * 2 ** 23))
LN_EPS = 1e-5



_DRAIN_OPS = frozenset("""
tensor_scalar tensor_single_scalar tensor_scalar_add tensor_scalar_sub
tensor_scalar_mul tensor_scalar_max tensor_scalar_min tensor_relu
tensor_tensor tensor_add tensor_sub tensor_mul tensor_max tensor_copy
tensor_reduce reduce_max reduce_sum scalar_tensor_tensor tensor_tensor_scan
memset memzero activation sign square sqrt mul copy add reciprocal select
bn_stats bn_aggr transpose iota
""".split())


class DrainingEngine:

    def __init__(self, eng):
        self.raw = eng
        self._written = set()
        self._read = set()

    def drain(self, *a, **k):
        self._written.clear()
        self._read.clear()
        return self.raw.drain(*a, **k)

    @staticmethod
    def _nm(x):
        t = getattr(x, "tensor", None)
        if t is not None:
            return getattr(t, "name", None)
        if hasattr(x, "ap"):
            return getattr(x, "name", None)
        return None

    def __getattr__(self, name):
        attr = getattr(self.raw, name)
        if name in _DRAIN_OPS:
            def wrapped(*a, **k):
                outs, ins = [], []
                for idx, x in enumerate(a):
                    n_ = self._nm(x)
                    if n_:
                        (outs if idx == 0 else ins).append(n_)
                for kname, x in k.items():
                    n_ = self._nm(x)
                    if n_:
                        (outs if kname == "accum_out" else ins).append(n_)
                need = (any(n_ in self._written for n_ in ins)
                        or any(n_ in self._written or n_ in self._read
                               for n_ in outs))
                if need:
                    self.raw.drain()
                    self._written.clear()
                    self._read.clear()
                attr_r = attr(*a, **k)
                self._written.update(outs)
                self._read.update(ins)
                return attr_r
            return wrapped
        return attr


class NcView:

    def __init__(self, nc, pre):
        self._nc = nc
        self._pre = pre

    def sbuf_tensor(self, name, *a, **k):
        return self._nc.sbuf_tensor(self._pre + name, *a, **k)

    def psum_tensor(self, name, *a, **k):
        return self._nc.psum_tensor(self._pre + name, *a, **k)

    def semaphore(self, name, *a, **k):
        return self._nc.alloc_semaphore(name=self._pre + name)

    def dram_tensor(self, name, *a, **k):
        return self._nc.dram_tensor(self._pre + name, *a, **k)

    def __getattr__(self, name):
        return getattr(self._nc, name)


class DmaSlots:

    def __init__(self, nc, A, name, n):
        self.sems = [A(nc.semaphore(f"{name}{k}")) for k in range(n)]
        self.n = n

    def inc(self, inst, i):
        return inst.then_inc(self.sems[i % self.n], 16)

    def wait(self, eng, i):
        eng.wait_ge(self.sems[i % self.n], 16 * (i // self.n + 1))


def emit_weight_transpose(sync, dst, src, dsem, dcnt):
    ca = src.shape[1]
    cb = dst.shape[1]
    assert src.shape[2] == cb * 128 and dst.shape[2] == ca * 128
    for icb in range(cb):
        sync.dma_start(
            dst.ap()[:, icb, :].rearrange("p (ca q) -> p ca q", ca=ca),
            src.ap()[:, :, icb * 128:(icb + 1) * 128],
            transpose=True,
        ).then_inc(dsem, 16)
        dcnt += 16
    return dcnt


def emit_round_clip(v, out_ap, t_ap, ni_ap, nf_ap, lo, hi):
    v.tensor_copy(ni_ap, t_ap)
    v.tensor_copy(nf_ap, ni_ap)
    v.tensor_tensor(t_ap, t_ap, nf_ap, ALU.subtract)
    v.scalar_tensor_tensor(nf_ap, t_ap, 0.5, nf_ap, ALU.is_ge, ALU.add)
    v.scalar_tensor_tensor(nf_ap, t_ap, -0.5, nf_ap, ALU.is_ge, ALU.add)
    v.tensor_scalar(nf_ap, nf_ap, -1.0, hi, ALU.add, ALU.min)
    return v.tensor_scalar(out_ap, nf_ap, lo, None, ALU.max)


def emit_recip_newton(v, out_ap, x_ap, scrA, scrB):
    v.reciprocal(scrA, x_ap)
    v.tensor_tensor(scrB, x_ap, scrA, ALU.mult)
    v.tensor_scalar(scrB, scrB, -1.0, 2.0, ALU.mult, ALU.add)
    return v.tensor_tensor(out_ap, scrA, scrB, ALU.mult)


def emit_round_correct(v, out_ap, t_ap, nf_ap, lo, hi):
    v.tensor_tensor(t_ap, t_ap, nf_ap, ALU.subtract)
    v.scalar_tensor_tensor(nf_ap, t_ap, 0.5, nf_ap, ALU.is_ge, ALU.add)
    v.scalar_tensor_tensor(nf_ap, t_ap, -0.5, nf_ap, ALU.is_ge, ALU.add)
    v.tensor_scalar(nf_ap, nf_ap, -1.0, hi, ALU.add, ALU.min)
    return v.tensor_scalar(out_ap, nf_ap, lo, None, ALU.max)


def emit_round_correct2(v, outs, ts_, nfs, lo, hi):
    for j in range(2):
        v.tensor_tensor(ts_[j], ts_[j], nfs[j], ALU.subtract)
    for j in range(2):
        v.scalar_tensor_tensor(nfs[j], ts_[j], 0.5, nfs[j], ALU.is_ge, ALU.add)
    for j in range(2):
        v.scalar_tensor_tensor(nfs[j], ts_[j], -0.5, nfs[j], ALU.is_ge, ALU.add)
    for j in range(2):
        v.tensor_scalar(nfs[j], nfs[j], -1.0, hi, ALU.add, ALU.min)
    for j in range(2):
        v.tensor_scalar(outs[j], nfs[j], lo, None, ALU.max)


def emit_round_clip2(v, outs, ts_, nis, nfs, lo, hi):
    for j in range(2):
        v.tensor_copy(nis[j], ts_[j])
    for j in range(2):
        v.tensor_copy(nfs[j], nis[j])
    for j in range(2):
        v.tensor_tensor(ts_[j], ts_[j], nfs[j], ALU.subtract)
    for j in range(2):
        v.scalar_tensor_tensor(nfs[j], ts_[j], 0.5, nfs[j], ALU.is_ge, ALU.add)
    for j in range(2):
        v.scalar_tensor_tensor(nfs[j], ts_[j], -0.5, nfs[j], ALU.is_ge, ALU.add)
    for j in range(2):
        v.tensor_scalar(nfs[j], nfs[j], -1.0, hi, ALU.add, ALU.min)
    for j in range(2):
        v.tensor_scalar(outs[j], nfs[j], lo, None, ALU.max)


def emit_rstd(scalar, var_ap, rstd_ap):
    scalar.activation(rstd_ap, var_ap, AF.Ln)
    return scalar.activation(rstd_ap, rstd_ap, AF.Exp, scale=-0.5)


def emit_ln_stats_phase(blk, x_src_tiles, n_tiles, xt, sx_slots, ssq, ssum,
                        s_sumsq, s_sum, scr_sq, scr_cp, col0=0, cnt0=0):
    n_buf = xt.shape[1]

    @blk.gpsimd
    def _(g):
        for i in range(n_tiles):
            if i >= n_buf:
                g.wait_ge(ssq, cnt0 + i - n_buf + 1)
                g.wait_ge(ssum, cnt0 + i - n_buf + 1)
            sx_slots.inc(g.dma_start(xt[:, i % n_buf, :], x_src_tiles(i)), i)

    @blk.scalar
    def _(sc):
        sc = DrainingEngine(sc)
        for i in range(n_tiles):
            sx_slots.wait(sc, i)
            sc.activation(scr_sq[:, i % scr_sq.shape[1], :], xt[:, i % n_buf, :],
                          AF.Square,
                          accum_out=s_sumsq[:, col0 + i:col0 + i + 1]).then_inc(ssq, 1)

    @blk.vector
    def _(v):
        v = DrainingEngine(v)
        for i in range(n_tiles):
            sx_slots.wait(v, i)
            v.tensor_scalar(scr_cp[:, i % scr_cp.shape[1], :], xt[:, i % n_buf, :],
                            1.0, None, ALU.mult, ALU.add,
                            accum_out=s_sum[:, col0 + i:col0 + i + 1]).then_inc(ssum, 1)


def emit_stat_math_phase(blk, s_sum, s_sumsq, s_negm, s_var, s_rstd, width, dim):
    @blk.vector
    def _(v):
        v = DrainingEngine(v)
        v.tensor_scalar_mul(s_negm[:, 0:width], s_sum[:, 0:width], -1.0 / dim)
        v.tensor_tensor(s_var[:, 0:width], s_negm[:, 0:width], s_negm[:, 0:width], ALU.mult)
        v.tensor_scalar(s_sumsq[:, 0:width], s_sumsq[:, 0:width], 1.0 / dim, LN_EPS,
                        ALU.mult, ALU.add)
        v.tensor_tensor(s_var[:, 0:width], s_sumsq[:, 0:width], s_var[:, 0:width],
                        ALU.subtract)

    @blk.scalar
    def _(sc):
        pass


def emit_rstd_phase(blk, s_var, s_rstd, width):
    @blk.scalar
    def _(sc):
        emit_rstd(DrainingEngine(sc), s_var[:, 0:width], s_rstd[:, 0:width])




def emit_residual_phase(nc, A4, x_src_tiles, y_src_tiles, out_tiles, n_tiles,
                        xt, p_negm, p_rstd, stat0=0, SA=None):
    SA = SA or (lambda h: h)
    yt = A4(nc.sbuf_tensor("res_yt", [128, 2, D], BF16))
    t1 = A4(nc.sbuf_tensor("res_t1", [128, 2, D], F32))
    xnew = A4(nc.sbuf_tensor("res_xnew", [128, 2, D], F32))
    sx4 = DmaSlots(nc, SA, "res_sx", 2)
    sy4 = DmaSlots(nc, SA, "res_sy", 2)
    sres = SA(nc.semaphore("res_done"))
    sst = DmaSlots(nc, SA, "res_st", 2)
    blk = A4(nc.Block())

    @blk.gpsimd
    def _(g):
        for i in range(n_tiles):
            if i >= 2:
                g.wait_ge(sres, i - 1)
            sx4.inc(g.dma_start(xt[:, i % 2, :], x_src_tiles(i)), i)
            sy4.inc(g.dma_start(yt[:, i % 2, :], y_src_tiles(i)), i)

    @blk.vector
    def _(v):
        v = DrainingEngine(v)
        for i in range(n_tiles):
            sx4.wait(v, i)
            sy4.wait(v, i)
            if i >= 2:
                sst.wait(v, i - 2)
            k = stat0 + i
            v.tensor_scalar(t1[:, i % 2, :], yt[:, i % 2, :],
                            p_negm[:, k:k + 1], p_rstd[:, k:k + 1],
                            ALU.add, ALU.mult)
            v.tensor_tensor(xnew[:, i % 2, :], xt[:, i % 2, :],
                            t1[:, i % 2, :], ALU.add).then_inc(sres, 1)

    @blk.sync
    def _(sy_):
        for i in range(n_tiles):
            sy_.wait_ge(sres, i + 1)
            sst.inc(sy_.dma_start(out_tiles(i), xnew[:, i % 2, :]), i)
        for k in range(2):
            sy_.wait_ge(sst.sems[k], 16 * ((n_tiles + 1 - k) // 2))




def emit_l1_ffn(nc, SA, x_in, wg, wu, wd, x_out):
    SA2 = SA
    H = 3072
    NHC = H // 128
    NSL = H // 512
    c127 = float(np.float32(1.0 / 127.0))

    y_scr = nc.dram_tensor("y_scr", [S_CORE, D], BF16)

    with contextlib.ExitStack() as es:
        A = es.enter_context
        wgT = A(nc.sbuf_tensor("wgT", [128, NHC, NDC, 128], BF16))
        wuT = A(nc.sbuf_tensor("wuT", [128, NHC, NDC, 128], BF16))
        wdT = A(nc.sbuf_tensor("wdT", [128, 2, NDC, NHC // 2, 128], BF16))
        xt = A(nc.sbuf_tensor("xt", [128, 2, D], F32))
        scr_sq = A(nc.sbuf_tensor("scr_sq", [128, 2, D], F32))
        scr_cp = A(nc.sbuf_tensor("scr_cp", [128, 2, D], BF16))
        s_sum = A(nc.sbuf_tensor("s_sum", [128, NT], F32))
        s_sumsq = A(nc.sbuf_tensor("s_sumsq", [128, NT], F32))
        s_negm = A(nc.sbuf_tensor("s_negm", [128, NT], F32))
        s_var = A(nc.sbuf_tensor("s_var", [128, NT], F32))
        s_rstd = A(nc.sbuf_tensor("s_rstd", [128, NT], F32))
        p_sum = A(nc.sbuf_tensor("p_sum", [128, NT], F32))
        p_sumsq = A(nc.sbuf_tensor("p_sumsq", [128, NT], F32))
        p_negm = A(nc.sbuf_tensor("p_negm", [128, NT], F32))
        p_var = A(nc.sbuf_tensor("p_var", [128, NT], F32))
        p_rstd = A(nc.sbuf_tensor("p_rstd", [128, NT], F32))
        dwg = SA(nc.semaphore("dwg"))
        dws = SA(nc.semaphore("dws"))
        sx = DmaSlots(nc, SA, "sx", 2)
        ssq = SA(nc.semaphore("ssq"))
        ssum = SA(nc.semaphore("ssum"))
        gcnt = 0
        tcnt = 0
        with nc.sbuf_tensor("wraw", [128, NHC // 2, D], BF16) as wraw:
            for (w_dram, w_t) in ((wg, wgT), (wu, wuT)):
                for half in range(2):
                    hh = NHC // 2
                    with nc.Block() as blk:
                        @blk.gpsimd
                        def _(g, w_dram=w_dram, half=half, base=gcnt):
                            g.dma_start(
                                wraw[:],
                                w_dram.ap()[half * (H // 2):(half + 1) * (H // 2), :]
                                .rearrange("(c p) d -> p c d", p=128),
                            ).then_inc(dwg, 16)
                            g.wait_ge(dwg, base + 16)
                    gcnt += 16
                    with nc.Block() as blk:
                        @blk.sync
                        def _(sync, w_t=w_t, half=half, base=tcnt):
                            sync.dma_start(
                                w_t.ap()[:, half * hh:(half + 1) * hh, :, :],
                                wraw[:],
                                transpose=True,
                            ).then_inc(dws, 16)
                            sync.wait_ge(dws, base + 16)
                    tcnt += 16
        with nc.sbuf_tensor("wdraw", [128, NDC, H // 2], BF16) as wdraw:
            for half in range(2):
                with nc.Block() as blk:
                    @blk.gpsimd
                    def _(g, half=half, base=gcnt):
                        g.dma_start(
                            wdraw[:],
                            wd.ap()[:, half * (H // 2):(half + 1) * (H // 2)]
                            .rearrange("(c p) h -> p c h", p=128),
                        ).then_inc(dwg, 16)
                        g.wait_ge(dwg, base + 16)
                gcnt += 16
                with nc.Block() as blk:
                    @blk.sync
                    def _(sync, half=half, base=tcnt):
                        sync.dma_start(
                            wdT.ap()[:, half, :, :, :],
                            wdraw[:],
                            transpose=True,
                        ).then_inc(dws, 16)
                        sync.wait_ge(dws, base + 16)
                tcnt += 16

        with nc.Block() as blk:
            emit_ln_stats_phase(blk, lambda i: x_in.ap()[i * 128:(i + 1) * 128, :],
                                NT, xt, sx, ssq, ssum, s_sumsq, s_sum, scr_sq, scr_cp)
        with nc.Block() as blk:
            emit_stat_math_phase(blk, s_sum, s_sumsq, s_negm, s_var, s_rstd, NT, D)
        with nc.Block() as blk:
            emit_rstd_phase(blk, s_var, s_rstd, NT)

        with contextlib.ExitStack() as es3:
            A3 = es3.enter_context
            xln = A3(nc.sbuf_tensor("xln", [128, 2, D], BF16))
            xf = A3(nc.sbuf_tensor("xf", [128, 2, NDC, 128], BF16))
            gsil = A3(nc.sbuf_tensor("gsil", [128, 2, 512], BF16))
            usc = A3(nc.sbuf_tensor("usc", [128, 2, 512], BF16))
            h_tm = A3(nc.sbuf_tensor("h_tm", [128, 2, H], BF16))
            h_fm = A3(nc.sbuf_tensor("h_fm", [128, 2, NHC, 128], BF16))
            y_sb = A3(nc.sbuf_tensor("y_sb", [128, 2, D], BF16))
            sx3 = DmaSlots(nc, SA, "sx3", 2)
            sxln = SA(nc.semaphore("sxln"))
            sxf = DmaSlots(nc, SA, "sxf", 2)
            sgmm = SA(nc.semaphore("sgmm"))
            summ = SA(nc.semaphore("summ"))
            ssil = SA(nc.semaphore("ssil"))
            suev = SA(nc.semaphore("suev"))
            sh = SA(nc.semaphore("sh"))
            shf = DmaSlots(nc, SA, "shf", 2)
            sdmm = SA(nc.semaphore("sdmm"))
            spsq = SA(nc.semaphore("spsq"))
            sy2 = SA(nc.semaphore("sy2"))
            syst = DmaSlots(nc, SA, "syst", 2)
            gps = A3(nc.psum_tensor("gps", [128, 2, 512], F32))
            ups = A3(nc.psum_tensor("ups", [128, 2, 512], F32))
            dps = A3(nc.psum_tensor("dps", [128, 2, D], F32))
            blk = A3(nc.Block())
            @blk.gpsimd
            def _(g):
                for i in range(NT):
                    if i >= 2:
                        g.wait_ge(sxln, i - 1)
                    sx3.inc(g.dma_start(xt[:, i % 2, :],
                                        x_in.ap()[i * 128:(i + 1) * 128, :]), i)

            @blk.vector
            def _(v):
                v = DrainingEngine(v)

                def front(i):
                    sx3.wait(v, i)
                    if i >= 2:
                        sxf.wait(v, i - 2)
                    v.tensor_scalar(xln[:, i % 2, :], xt[:, i % 2, :],
                                    s_negm[:, i:i + 1], s_rstd[:, i:i + 1],
                                    ALU.add, ALU.mult).then_inc(sxln, 1)
                front(0)
                for i in range(NT):
                    if i + 1 < NT:
                        front(i + 1)
                    for j in range(NSL):
                        gidx = i * NSL + j
                        v.wait_ge(summ, gidx + 1)
                        v.tensor_scalar_mul(usc[:, j % 2, :], ups[:, j % 2, :],
                                            c127).then_inc(suev, 1)
                        v.wait_ge(ssil, gidx + 1)
                        if i >= 2 and j == 0:
                            shf.wait(v, i - 2)
                        v.tensor_tensor(h_tm[:, i % 2, j * 512:(j + 1) * 512],
                                        gsil[:, j % 2, :], usc[:, j % 2, :],
                                        ALU.mult).then_inc(sh, 1)
                    v.wait_ge(sdmm, i + 1)
                    if i >= 2:
                        syst.wait(v, i - 2)
                    v.tensor_scalar(y_sb[:, i % 2, :], dps[:, i % 2, :], c127, None,
                                    ALU.mult, ALU.add,
                                    accum_out=p_sum[:, i:i + 1]).then_inc(sy2, 1)

            @blk.scalar
            def _(sc):
                sc = DrainingEngine(sc)
                for i in range(NT):
                    for j in range(NSL):
                        gidx = i * NSL + j
                        sc.wait_ge(sgmm, gidx + 1)
                        if gidx >= 2:
                            sc.wait_ge(sh, gidx - 1)
                        sc.activation(gsil[:, j % 2, :], gps[:, j % 2, :], AF.Silu,
                                      scale=c127).then_inc(ssil, 1)
                    sc.wait_ge(sy2, i + 1)
                    sc.activation(scr_sq[:, i % 2, :], y_sb[:, i % 2, :], AF.Square,
                                  accum_out=p_sumsq[:, i:i + 1]).then_inc(spsq, 1)

            @blk.sync
            def _(sy_):
                for i in range(NT):
                    sy_.wait_ge(sxln, i + 1)
                    sxf.inc(sy_.dma_start(
                        xf.ap()[:, i % 2, :, :],
                        xln.ap()[:, i % 2, :].rearrange("p (c q) -> p c q", c=NDC),
                        transpose=True,
                    ), i)
                    sy_.wait_ge(sh, (i + 1) * NSL)
                    shf.inc(sy_.dma_start(
                        h_fm.ap()[:, i % 2, :, :],
                        h_tm.ap()[:, i % 2, :].rearrange("p (c q) -> p c q", c=NHC),
                        transpose=True,
                    ), i)
                    sy_.wait_ge(sy2, i + 1)
                    syst.inc(sy_.dma_start(y_scr.ap()[i * 128:(i + 1) * 128, :],
                                           y_sb[:, i % 2, :]), i)
                for k in range(2):
                    sy_.wait_ge(syst.sems[k], 16 * (NT // 2))

            @blk.tensor
            def _(t):
                for i in range(NT):
                    sxf.wait(t, i)
                    for j in range(NSL):
                        gidx = i * NSL + j
                        if gidx >= 2:
                            t.wait_ge(ssil, gidx - 1)
                        for c in range(NDC):
                            mm_ = t.matmul(gps[:, j % 2, :],
                                     xf[:, i % 2, c, :],
                                     wgT.ap()[:, 4 * j:4 * j + 4, c, :],
                                     start=(c == 0), stop=(c == NDC - 1))
                        mm_.then_inc(sgmm, 1)
                        if gidx >= 2:
                            t.wait_ge(suev, gidx - 1)
                        for c in range(NDC):
                            mm_ = t.matmul(ups[:, j % 2, :],
                                     xf[:, i % 2, c, :],
                                     wuT.ap()[:, 4 * j:4 * j + 4, c, :],
                                     start=(c == 0), stop=(c == NDC - 1))
                        mm_.then_inc(summ, 1)
                    shf.wait(t, i)
                    if i >= 2:
                        t.wait_ge(sy2, i - 1)
                        t.wait_ge(spsq, i - 1)
                    for n2 in range(2):
                        for c in range(NHC):
                            mm_ = t.matmul(dps[:, i % 2, n2 * 512:(n2 + 1) * 512],
                                     h_fm[:, i % 2, c, :],
                                     wdT.ap()[:, c // (NHC // 2), 4 * n2:4 * n2 + 4,
                                              c % (NHC // 2), :],
                                     start=(c == 0), stop=(c == NHC - 1))
                    mm_.then_inc(sdmm, 1)

        with nc.Block() as blk:
            emit_stat_math_phase(blk, p_sum, p_sumsq, p_negm, p_var, p_rstd, NT, D)
        with nc.Block() as blk:
            emit_rstd_phase(blk, p_var, p_rstd, NT)
        with contextlib.ExitStack() as es4:
            A4 = es4.enter_context
            yt = A4(nc.sbuf_tensor("yt", [128, 2, D], BF16))
            t1 = A4(nc.sbuf_tensor("t1", [128, 2, D], F32))
            xnew = A4(nc.sbuf_tensor("xnew", [128, 2, D], F32))
            sx4 = DmaSlots(nc, SA, "sx4", 2)
            sy4 = DmaSlots(nc, SA, "sy4", 2)
            sres = SA(nc.semaphore("sres"))
            sst = DmaSlots(nc, SA, "sst", 2)
            blk = A4(nc.Block())
            @blk.gpsimd
            def _(g):
                for i in range(NT):
                    if i >= 2:
                        g.wait_ge(sres, i - 1)
                    sx4.inc(g.dma_start(xt[:, i % 2, :],
                                        x_in.ap()[i * 128:(i + 1) * 128, :]), i)
                    sy4.inc(g.dma_start(yt[:, i % 2, :],
                                        y_scr.ap()[i * 128:(i + 1) * 128, :]), i)

            @blk.vector
            def _(v):
                v = DrainingEngine(v)
                for i in range(NT):
                    sx4.wait(v, i)
                    sy4.wait(v, i)
                    if i >= 2:
                        sst.wait(v, i - 2)
                    v.tensor_scalar(t1[:, i % 2, :], yt[:, i % 2, :],
                                    p_negm[:, i:i + 1], p_rstd[:, i:i + 1],
                                    ALU.add, ALU.mult)
                    v.tensor_tensor(xnew[:, i % 2, :], xt[:, i % 2, :],
                                    t1[:, i % 2, :], ALU.add).then_inc(sres, 1)

            @blk.sync
            def _(sy_):
                for i in range(NT):
                    sy_.wait_ge(sres, i + 1)
                    sst.inc(sy_.dma_start(x_out.ap()[i * 128:(i + 1) * 128, :],
                                          xnew[:, i % 2, :]), i)
                for k in range(2):
                    sy_.wait_ge(sst.sems[k], 16 * (NT // 2))


def build_l1_ffn():
    nc0 = bass.Bass("TRN2", target_bir_lowering=False)
    nc = NcView(nc0, "d_")
    x_in = nc0.dram_tensor("x_in", [S_CORE, D], F32, kind="ExternalInput")
    wg = nc0.dram_tensor("wg", [3072, D], I32, kind="ExternalInput")
    wu = nc0.dram_tensor("wu", [3072, D], I32, kind="ExternalInput")
    wd = nc0.dram_tensor("wd", [D, 3072], I32, kind="ExternalInput")
    x_out = nc0.dram_tensor("x_out", [S_CORE, D], F32, kind="ExternalOutput")
    with contextlib.ExitStack() as es:
        emit_l1_ffn(nc, lambda h: h, x_in, wg, wu, wd, x_out)
    return nc0






def emit_l0_ffn(nc, SA, x_in, wg, wu, wd, x_out):
    SA2 = SA
    H = 4096
    NHC = H // 128
    NSL = H // 512
    NQ = 4
    c127 = float(np.float32(1.0 / 127.0))

    y_scr = nc.dram_tensor("y_scr", [S_CORE, D], BF16)
    inth_scr = nc.dram_tensor("inth_scr", [S_CORE, H], BF16)

    QF = H * D // NQ // 128

    with contextlib.ExitStack() as es:
        A = es.enter_context
        xt = A(nc.sbuf_tensor("xt", [128, 2, D], F32))
        scr_sq = A(nc.sbuf_tensor("scr_sq", [128, 1, D], F32))
        scr_cp = A(nc.sbuf_tensor("scr_cp", [128, 1, D], BF16))
        s_sum = A(nc.sbuf_tensor("s_sum", [128, NT], F32))
        s_sumsq = A(nc.sbuf_tensor("s_sumsq", [128, NT], F32))
        s_negm = A(nc.sbuf_tensor("s_negm", [128, NT], F32))
        s_var = A(nc.sbuf_tensor("s_var", [128, NT], F32))
        s_rstd = A(nc.sbuf_tensor("s_rstd", [128, NT], F32))
        p_sum = A(nc.sbuf_tensor("p_sum", [128, NT], F32))
        p_sumsq = A(nc.sbuf_tensor("p_sumsq", [128, NT], F32))
        p_negm = A(nc.sbuf_tensor("p_negm", [128, NT], F32))
        p_var = A(nc.sbuf_tensor("p_var", [128, NT], F32))
        p_rstd = A(nc.sbuf_tensor("p_rstd", [128, NT], F32))
        s_a = A(nc.sbuf_tensor("s_a", [128, NT], F32))
        s_cg = A(nc.sbuf_tensor("s_cg", [128, NT], F32))
        s_cu = A(nc.sbuf_tensor("s_cu", [128, NT], F32))
        s_a2 = A(nc.sbuf_tensor("s_a2", [128, NT], F32))
        s_c2 = A(nc.sbuf_tensor("s_c2", [128, NT], F32))
        wm_b = A(nc.sbuf_tensor("wm_b", [128, 6], F32))
        wsum = A(nc.sbuf_tensor("wsum", [128, 3 * NQ], F32))
        ones_c = A(nc.sbuf_tensor("ones_c", [128, 1], F32))
        ones_r = A(nc.sbuf_tensor("ones_r", [1, 128], F32))
        row6 = A(nc.sbuf_tensor("row6", [1, 6], F32))
        rowq = A(nc.sbuf_tensor("rowq", [1, 3 * NQ], F32))

        with contextlib.ExitStack() as es0:
            A0 = es0.enter_context
            wst = A0(nc.sbuf_tensor("wst", [128, 2, QF], F32))
            sld = DmaSlots(nc, SA, "b_ld", 2)
            sred = SA(nc.semaphore("b_red"))
            blk = A0(nc.Block())
            srcs = []
            for widx, w_dram in enumerate((wg, wu)):
                for q in range(NQ):
                    srcs.append(w_dram.ap()[q * (H // NQ):(q + 1) * (H // NQ), :]
                                .rearrange("(c p) d -> p c d", p=128))
            for q in range(NQ):
                srcs.append(wd.ap()[q * (D // NQ):(q + 1) * (D // NQ), :]
                            .rearrange("(c p) h -> p c h", p=128))

            @blk.gpsimd
            def _(g):
                for k in range(3 * NQ):
                    if k >= 2:
                        g.wait_ge(sred, k - 1)
                    sld.inc(g.dma_start(
                        wst.ap()[:, k % 2, :].rearrange("p (c d) -> p c d", d=D if k < 2 * NQ else H),
                        srcs[k]), k)

            @blk.vector
            def _(v):
                v = DrainingEngine(v)
                for k in range(3 * NQ):
                    sld.wait(v, k)
                    v.reduce_sum(wsum[:, k:k + 1], wst[:, k % 2, :], axis=AX,
                                 apply_absolute_value=True).then_inc(sred, 1)

        with contextlib.ExitStack() as es0:
            A0 = es0.enter_context
            pst = A0(nc.psum_tensor("b_pst", [128, 16], F32))
            with nc.Block() as blk:
                @blk.vector
                def _(v):
                    v.memset(ones_c[:], 1.0)
                    v.memset(ones_r[:], 1.0)
            with nc.Block() as blk:
                @blk.tensor
                def _(t):
                    mm_ = t.matmul(pst[0:1, 0:3 * NQ], ones_c[:], wsum[:], start=True, stop=True)
                    t.drain()
            with nc.Block() as blk:
                @blk.vector
                def _(v):
                    v = DrainingEngine(v)
                    v.tensor_copy(rowq[:], pst[0:1, 0:3 * NQ])
                    for w in range(3):
                        v.reduce_sum(row6[:, w:w + 1], rowq[:, w * NQ:(w + 1) * NQ], axis=AX)
                    v.tensor_scalar(row6[:, 0:3], row6[:, 0:3], 1.0 / (H * D), 1e-5,
                                    ALU.mult, ALU.max)
                    emit_recip_newton(v, row6[:, 3:6], row6[:, 0:3],
                                      rowq[:, 0:3], rowq[:, 3:6])
            with nc.Block() as blk:
                @blk.tensor
                def _(t):
                    mm_ = t.matmul(pst[:, 0:6], ones_r[:], row6[:], start=True, stop=True)
                    t.drain()
            with nc.Block() as blk:
                @blk.vector
                def _(v):
                    v.tensor_copy(wm_b[:], pst[:, 0:6])

        def emit_quant_weights(A0, pairs, stp_name):
            wstq = A0(nc.sbuf_tensor(stp_name + "_f", [128, QF], F32))
            wqb = A0(nc.sbuf_tensor(stp_name + "_b", [128, QF], BF16))
            wnf = A0(nc.sbuf_tensor(stp_name + "_nf", [128, QF // 2], F32))
            sld = SA(nc.semaphore(stp_name + "_ld"))
            sqn = SA(nc.semaphore(stp_name + "_qn"))
            stp = SA(nc.semaphore(stp_name + "_tp"))
            blk = A0(nc.Block())

            @blk.gpsimd
            def _(g):
                for k, (src, dst, widx) in enumerate(pairs):
                    if k >= 1:
                        g.wait_ge(sqn, k)
                    g.dma_start(wstq.ap().rearrange("p (c d) -> p c d", c=src.shape[1]),
                                src).then_inc(sld, 16)

            @blk.vector
            def _(v):
                v = DrainingEngine(v)
                for k, (src, dst, widx) in enumerate(pairs):
                    v.wait_ge(sld, 16 * (k + 1))
                    if k >= 1:
                        v.wait_ge(stp, 16 * k)
                    v.tensor_scalar_mul(wstq[:], wstq[:], wm_b[:, 3 + widx:4 + widx])
                    for q4 in range(2):
                        sl = slice(q4 * (QF // 2), (q4 + 1) * (QF // 2))
                        v.tensor_scalar(wnf[:], wstq[:, sl], 0.5, None, ALU.is_ge)
                        v.scalar_tensor_tensor(wnf[:], wstq[:, sl], -0.5,
                                               wnf[:], ALU.is_ge, ALU.add)
                        v.tensor_scalar(wqb[:, sl], wnf[:], -1.0, 1.0,
                                        ALU.mult, ALU.add)
                    v.drain().then_inc(sqn, 1)

            @blk.sync
            def _(sy_):
                for k, (src, dst, widx) in enumerate(pairs):
                    sy_.wait_ge(sqn, k + 1)
                    sy_.dma_start(dst, wstq if False else wqb[:],
                                  transpose=True).then_inc(stp, 16)
                sy_.wait_ge(stp, 16 * len(pairs))

        sx = DmaSlots(nc, SA, "sx", 2)
        ssq = SA(nc.semaphore("ssq"))
        ssum = SA(nc.semaphore("ssum"))
        with nc.Block() as blk:
            emit_ln_stats_phase(blk, lambda i: x_in.ap()[i * 128:(i + 1) * 128, :],
                                NT, xt, sx, ssq, ssum, s_sumsq, s_sum, scr_sq, scr_cp)
        with nc.Block() as blk:
            emit_stat_math_phase(blk, s_sum, s_sumsq, s_negm, s_var, s_rstd, NT, D)
        with nc.Block() as blk:
            emit_rstd_phase(blk, s_var, s_rstd, NT)

        with contextlib.ExitStack() as es3:
            A3 = es3.enter_context
            wgT = A3(nc.sbuf_tensor("wgT", [128, NHC, NDC, 128], BF16))
            wuT = A3(nc.sbuf_tensor("wuT", [128, NHC, NDC, 128], BF16))
            with contextlib.ExitStack() as esq:
                pairs = []
                for widx, (w_dram, w_t) in enumerate(((wg, wgT), (wu, wuT))):
                    for q in range(NQ):
                        src = (w_dram.ap()[q * (H // NQ):(q + 1) * (H // NQ), :]
                               .rearrange("(c p) d -> p c d", p=128))
                        dst = w_t.ap()[:, q * (NHC // NQ):(q + 1) * (NHC // NQ), :, :]
                        pairs.append((src, dst, widx))
                emit_quant_weights(esq.enter_context, pairs, "qgu")

            xln = A3(nc.sbuf_tensor("xln", [128, D], F32))
            scrq = scr_sq
            intx = A3(nc.sbuf_tensor("intx", [128, 2, D], BF16))
            ixf = A3(nc.sbuf_tensor("ixf", [128, 2, NDC, 128], BF16))
            gsil = A3(nc.sbuf_tensor("gsil", [128, 2, 512], F32))
            usc = A3(nc.sbuf_tensor("usc", [128, 2, 512], F32))
            h_tm = A3(nc.sbuf_tensor("h_tm", [128, H], F32))
            inth = A3(nc.sbuf_tensor("inth", [128, 1, H], BF16))
            b_ni0 = A3(nc.sbuf_tensor("b_ni0", [128, H // 4], I32))
            b_nf0 = A3(nc.sbuf_tensor("b_nf0", [128, H // 4], F32))
            b_ni1 = A3(nc.sbuf_tensor("b_ni1", [128, H // 4], I32))
            b_nf1 = A3(nc.sbuf_tensor("b_nf1", [128, H // 4], F32))
            sx3 = DmaSlots(nc, SA, "sx3", 2)
            sxln = SA(nc.semaphore("sxln"))
            sixf = DmaSlots(nc, SA, "sixf", 2)
            sgmm = SA(nc.semaphore("sgmm"))
            summ = SA(nc.semaphore("summ"))
            ssil = SA(nc.semaphore("ssil"))
            suev = SA(nc.semaphore("suev"))
            sh = SA(nc.semaphore("sh"))
            sqh = SA(nc.semaphore("sqh"))
            sqm = SA(nc.semaphore("sqm"))
            sqc = SA(nc.semaphore("sqc"))
            shp = SA(nc.semaphore("shp"))
            shst = DmaSlots(nc, SA, "shst", 2)
            gps = A3(nc.psum_tensor("gps", [128, 2, 512], F32))
            ups = A3(nc.psum_tensor("ups", [128, 2, 512], F32))
            blk = A3(nc.Block())

            @blk.gpsimd
            def _(g):
                gp = DrainingEngine(g)
                for i in range(NT):
                    if i >= 2:
                        g.wait_ge(sxln, 2 * (i - 1))
                    sx3.inc(g.dma_start(xt[:, i % 2, :],
                                        x_in.ap()[i * 128:(i + 1) * 128, :]), i)
                    for j in range(NSL):
                        gidx = i * NSL + j
                        g.wait_ge(ssil, gidx + 1)
                        g.wait_ge(suev, gidx + 1)
                        if i >= 1 and j == 0:
                            g.wait_ge(sqh, i)
                        gp.tensor_tensor(h_tm[:, j * 512:(j + 1) * 512],
                                         gsil[:, j % 2, :], usc[:, j % 2, :],
                                         ALU.mult).then_inc(sh, 1)

            @blk.vector
            def _(v):
                v = DrainingEngine(v)
                for i in range(NT):
                    sx3.wait(v, i)
                    if i >= 2:
                        sixf.wait(v, i - 2)
                    v.tensor_scalar(xln[:], xt[:, i % 2, :],
                                    s_negm[:, i:i + 1], s_rstd[:, i:i + 1],
                                    ALU.add, ALU.mult).then_inc(sxln, 1)
                    v.reduce_max(s_a[:, i:i + 1], xln[:], axis=AX,
                                 apply_absolute_value=True)
                    v.tensor_scalar_max(s_a[:, i:i + 1], s_a[:, i:i + 1], 1e-5)
                    v.tensor_scalar(s_cg[:, i:i + 1], s_a[:, i:i + 1], -c127, None,
                                    ALU.mult, ALU.bypass)
                    v.tensor_tensor(s_cu[:, i:i + 1], s_cg[:, i:i + 1],
                                    wm_b[:, 1:2], ALU.mult)
                    v.tensor_tensor(s_cg[:, i:i + 1], s_cg[:, i:i + 1],
                                    wm_b[:, 0:1], ALU.mult)
                    emit_recip_newton(v, s_a[:, i:i + 1], s_a[:, i:i + 1],
                                      s_sum[:, i:i + 1], s_sumsq[:, i:i + 1])
                    v.tensor_scalar_mul(s_a[:, i:i + 1], s_a[:, i:i + 1], 127.0)
                    v.tensor_scalar_mul(xln[:], xln[:], s_a[:, i:i + 1])
                    v.drain().then_inc(sqm, 1)
                    v.wait_ge(sqc, 5 * i + 1)
                    emit_round_correct(v, intx[:, i % 2, :], xln[:],
                                       b_nf0[:], -128.0, 127.0)
                    v.drain().then_inc(sxln, 1)

                    v.wait_ge(sh, (i + 1) * NSL)
                    v.reduce_max(s_a2[:, i:i + 1], h_tm[:], axis=AX,
                                 apply_absolute_value=True)
                    v.tensor_scalar_max(s_a2[:, i:i + 1], s_a2[:, i:i + 1], 1e-5)
                    v.tensor_scalar(s_c2[:, i:i + 1], s_a2[:, i:i + 1], -c127, None,
                                    ALU.mult, ALU.bypass)
                    v.tensor_tensor(s_c2[:, i:i + 1], s_c2[:, i:i + 1],
                                    wm_b[:, 2:3], ALU.mult)
                    emit_recip_newton(v, s_a2[:, i:i + 1], s_a2[:, i:i + 1],
                                      s_sum[:, i:i + 1], s_sumsq[:, i:i + 1])
                    v.tensor_scalar_mul(s_a2[:, i:i + 1], s_a2[:, i:i + 1], 127.0)
                    if i >= 1:
                        shst.wait(v, i - 1)
                    for pairq in range(2):
                        h0 = slice((2 * pairq) * (H // 4), (2 * pairq + 1) * (H // 4))
                        h1 = slice((2 * pairq + 1) * (H // 4), (2 * pairq + 2) * (H // 4))
                        v.tensor_scalar_mul(h_tm[:, h0], h_tm[:, h0], s_a2[:, i:i + 1])
                        v.tensor_scalar_mul(h_tm[:, h1], h_tm[:, h1], s_a2[:, i:i + 1])
                        v.drain().then_inc(sqm, 1)
                        v.wait_ge(sqc, 5 * i + 3 + 2 * pairq)
                        emit_round_correct2(v, (inth[:, 0, h0], inth[:, 0, h1]),
                                            (h_tm[:, h0], h_tm[:, h1]),
                                            (b_nf0[:], b_nf1[:]), -128.0, 127.0)
                    v.drain().then_inc(sqh, 1)

            @blk.scalar
            def _(sc):
                sc = DrainingEngine(sc)
                for i in range(NT):
                    sc.wait_ge(sqm, 3 * i + 1)
                    sc.activation(b_ni0[:, 0:D], xln[:], AF.Copy)
                    sc.activation(b_nf0[:, 0:D], b_ni0[:, 0:D],
                                  AF.Copy).then_inc(sqc, 1)
                    for j in range(NSL):
                        gidx = i * NSL + j
                        sc.wait_ge(sgmm, gidx + 1)
                        if gidx >= 2:
                            sc.wait_ge(sh, gidx - 1)
                        sc.activation(gsil[:, j % 2, :], gps[:, j % 2, :], AF.Silu,
                                      scale=s_cg[:, i:i + 1]).then_inc(ssil, 1)
                        sc.wait_ge(summ, gidx + 1)
                        if gidx >= 2:
                            sc.wait_ge(sh, gidx - 1)
                        sc.activation(usc[:, j % 2, :], ups[:, j % 2, :], AF.Copy,
                                      scale=s_cu[:, i:i + 1]).then_inc(suev, 1)
                    for pairq in range(2):
                        h0 = slice((2 * pairq) * (H // 4), (2 * pairq + 1) * (H // 4))
                        h1 = slice((2 * pairq + 1) * (H // 4), (2 * pairq + 2) * (H // 4))
                        sc.wait_ge(sqm, 3 * i + 2 + pairq)
                        sc.activation(b_ni0[:], h_tm[:, h0], AF.Copy)
                        sc.activation(b_nf0[:], b_ni0[:], AF.Copy).then_inc(sqc, 1)
                        sc.activation(b_ni1[:], h_tm[:, h1], AF.Copy)
                        sc.activation(b_nf1[:], b_ni1[:], AF.Copy).then_inc(sqc, 1)


            @blk.sync
            def _(sy_):
                for i in range(NT):
                    sy_.wait_ge(sxln, 2 * (i + 1))
                    sixf.inc(sy_.dma_start(
                        ixf.ap()[:, i % 2, :, :],
                        intx.ap()[:, i % 2, :].rearrange("p (c q) -> p c q", c=NDC),
                        transpose=True,
                    ), i)
                    sy_.wait_ge(sqh, i + 1)
                    shst.inc(sy_.dma_start(inth_scr.ap()[i * 128:(i + 1) * 128, :],
                                           inth[:, 0, :]), i)
                for k in range(2):
                    sy_.wait_ge(shst.sems[k], 16 * ((NT + 1 - k) // 2))

            @blk.tensor
            def _(t):
                for i in range(NT):
                    sixf.wait(t, i)
                    for j in range(NSL):
                        gidx = i * NSL + j
                        if gidx >= 2:
                            t.wait_ge(ssil, gidx - 1)
                        for c in range(NDC):
                            mm_ = t.matmul(gps[:, j % 2, :],
                                     ixf[:, i % 2, c, :],
                                     wgT.ap()[:, 4 * j:4 * j + 4, c, :],
                                     start=(c == 0), stop=(c == NDC - 1))
                        mm_.then_inc(sgmm, 1)
                        if gidx >= 2:
                            t.wait_ge(suev, gidx - 1)
                        for c in range(NDC):
                            mm_ = t.matmul(ups[:, j % 2, :],
                                     ixf[:, i % 2, c, :],
                                     wuT.ap()[:, 4 * j:4 * j + 4, c, :],
                                     start=(c == 0), stop=(c == NDC - 1))
                        mm_.then_inc(summ, 1)

        with contextlib.ExitStack() as es3:
            A3 = es3.enter_context
            wdT = A3(nc.sbuf_tensor("wdT", [128, NQ, D // NQ // 128, NHC, 128], BF16))
            with contextlib.ExitStack() as esq:
                pairs = []
                for q in range(NQ):
                    src = (wd.ap()[q * (D // NQ):(q + 1) * (D // NQ), :]
                           .rearrange("(c p) h -> p c h", p=128))
                    dst = wdT.ap()[:, q, :, :, :]
                    pairs.append((src, dst, 2))
                emit_quant_weights(esq.enter_context, pairs, "qd")

            ihf = A3(nc.sbuf_tensor("ihf", [128, 2, NHC, 128], BF16))
            y_sb = A3(nc.sbuf_tensor("y_sb", [128, 2, D], BF16))
            sihf = DmaSlots(nc, SA, "sihf", 2)
            sdmm = SA(nc.semaphore("sdmm"))
            sy2 = SA(nc.semaphore("sy2"))
            spsq = SA(nc.semaphore("spsq"))
            syst = DmaSlots(nc, SA, "syst", 2)
            dps = A3(nc.psum_tensor("dps", [128, 2, D], F32))
            blk = A3(nc.Block())

            @blk.sync
            def _(sy_):
                for i in range(NT):
                    if i >= 2:
                        sy_.wait_ge(sdmm, i - 1)
                    sihf.inc(sy_.dma_start(
                        ihf.ap()[:, i % 2, :, :],
                        inth_scr.ap()[i * 128:(i + 1) * 128, :]
                        .rearrange("p (c q) -> p c q", c=NHC),
                        transpose=True,
                    ), i)
                    sy_.wait_ge(sy2, i + 1)
                    syst.inc(sy_.dma_start(y_scr.ap()[i * 128:(i + 1) * 128, :],
                                           y_sb[:, i % 2, :]), i)
                for k in range(2):
                    sy_.wait_ge(syst.sems[k], 16 * ((NT + 1 - k) // 2))

            @blk.tensor
            def _(t):
                for i in range(NT):
                    sihf.wait(t, i)
                    if i >= 2:
                        t.wait_ge(sy2, i - 1)
                    for n2 in range(2):
                        for c in range(NHC):
                            mm_ = t.matmul(dps[:, i % 2, n2 * 512:(n2 + 1) * 512],
                                     ihf[:, i % 2, c, :],
                                     wdT.ap()[:, 2 * n2:2 * n2 + 2, :, c, :]
                                     .rearrange("p a b q -> p (a b) q"),
                                     start=(c == 0), stop=(c == NHC - 1))
                    mm_.then_inc(sdmm, 1)

            @blk.scalar
            def _(sc):
                sc = DrainingEngine(sc)
                for i in range(NT):
                    sc.wait_ge(sdmm, i + 1)
                    if i >= 2:
                        syst.wait(sc, i - 2)
                    sc.activation(y_sb[:, i % 2, :], dps[:, i % 2, :], AF.Copy,
                                  scale=s_c2[:, i:i + 1],
                                  accum_out=p_sum[:, i:i + 1]).then_inc(sy2, 1)
                    sc.activation(scr_sq[:, 0, :], y_sb[:, i % 2, :], AF.Square,
                                  accum_out=p_sumsq[:, i:i + 1]).then_inc(spsq, 1)

        with nc.Block() as blk:
            emit_stat_math_phase(blk, p_sum, p_sumsq, p_negm, p_var, p_rstd, NT, D)
        with nc.Block() as blk:
            emit_rstd_phase(blk, p_var, p_rstd, NT)
        with contextlib.ExitStack() as es4:
            emit_residual_phase(nc, es4.enter_context,
                                lambda i: x_in.ap()[i * 128:(i + 1) * 128, :],
                                lambda i: y_scr.ap()[i * 128:(i + 1) * 128, :],
                                lambda i: x_out.ap()[i * 128:(i + 1) * 128, :],
                                NT, xt, p_negm, p_rstd, SA=SA2)


def build_l0_ffn():
    nc0 = bass.Bass("TRN2", target_bir_lowering=False)
    nc = NcView(nc0, "b_")
    x_in = nc0.dram_tensor("x_in", [S_CORE, D], F32, kind="ExternalInput")
    wg = nc0.dram_tensor("wg", [4096, D], F32, kind="ExternalInput")
    wu = nc0.dram_tensor("wu", [4096, D], F32, kind="ExternalInput")
    wd = nc0.dram_tensor("wd", [D, 4096], F32, kind="ExternalInput")
    x_out = nc0.dram_tensor("x_out", [S_CORE, D], F32, kind="ExternalOutput")
    with contextlib.ExitStack() as es:
        emit_l0_ffn(nc, lambda h: h, x_in, wg, wu, wd, x_out)
    return nc0




def emit_l0_attn(nc, SA, x_in, x_oth, wqd, wkd, wvd, wod, bod, x_out):
    NH = 16
    NP = 8
    NTF = 2 * NT
    EPS = 1e-6

    y_scr = nc.dram_tensor("y_scr", [S_CORE, D], BF16)

    def tile_src(t):
        if t < NT:
            return x_in.ap()[t * 128:(t + 1) * 128, :]
        return x_oth.ap()[(t - NT) * 128:(t - NT + 1) * 128, :]

    with contextlib.ExitStack() as es:
        A = es.enter_context
        wqT = A(nc.sbuf_tensor("wqT", [128, NDC, NDC, 128], BF16))
        wkT = A(nc.sbuf_tensor("wkT", [128, NDC, NDC, 128], BF16))
        wvT = A(nc.sbuf_tensor("wvT", [128, NDC, NDC, 128], BF16))
        woT = A(nc.sbuf_tensor("woT", [128, NDC, NDC, 128], BF16))
        bo_bf = A(nc.sbuf_tensor("bo_bf", [1, D], BF16))
        ones_r = A(nc.sbuf_tensor("ones_r", [1, 128], BF16))
        ones_c = A(nc.sbuf_tensor("ones_c", [128, 1], BF16))
        xt = A(nc.sbuf_tensor("xt", [128, 2, D], F32))
        scr_sq = A(nc.sbuf_tensor("scr_sq", [128, 2, D], F32))
        scr_cp = A(nc.sbuf_tensor("scr_cp", [128, 2, D], BF16))
        s_sum = A(nc.sbuf_tensor("s_sum", [128, NTF], F32))
        s_sumsq = A(nc.sbuf_tensor("s_sumsq", [128, NTF], F32))
        s_negm = A(nc.sbuf_tensor("s_negm", [128, NTF], F32))
        s_var = A(nc.sbuf_tensor("s_var", [128, NTF], F32))
        s_rstd = A(nc.sbuf_tensor("s_rstd", [128, NTF], F32))
        p_sum = A(nc.sbuf_tensor("p_sum", [128, NT], F32))
        p_sumsq = A(nc.sbuf_tensor("p_sumsq", [128, NT], F32))
        p_negm = A(nc.sbuf_tensor("p_negm", [128, NT], F32))
        p_var = A(nc.sbuf_tensor("p_var", [128, NT], F32))
        p_rstd = A(nc.sbuf_tensor("p_rstd", [128, NT], F32))
        dwg = SA(nc.semaphore("dwg"))
        dws = SA(nc.semaphore("dws"))

        gcnt = 0
        tcnt = 0
        with nc.sbuf_tensor("wraw", [128, NDC, D], BF16) as wraw:
            for wi, (w_dram, w_t) in enumerate(
                    ((wqd, wqT), (wkd, wkT), (wvd, wvT), (wod, woT))):
                with nc.Block() as blk:
                    @blk.gpsimd
                    def _(g, w_dram=w_dram, base=gcnt):
                        g.dma_start(
                            wraw[:],
                            w_dram.ap().rearrange("(c p) d -> p c d", p=128),
                        ).then_inc(dwg, 16)
                        g.wait_ge(dwg, base + 16)
                gcnt += 16
                with nc.Block() as blk:
                    @blk.sync
                    def _(sync, w_t=w_t, base=tcnt):
                        sync.dma_start(w_t[:], wraw[:], transpose=True).then_inc(dws, 16)
                        sync.wait_ge(dws, base + 16)
                tcnt += 16
        with nc.Block() as blk:
            @blk.gpsimd
            def _(g, base=gcnt):
                g.dma_start(bo_bf[:], bod.ap()).then_inc(dwg, 16)
                g.wait_ge(dwg, base + 16)
            @blk.vector
            def _(v):
                v.memset(ones_r[:], 1.0)
                v.memset(ones_c[:], 1.0)
        gcnt += 16

        sx = DmaSlots(nc, SA, "sx", 2)
        ssq = SA(nc.semaphore("ssq"))
        ssum = SA(nc.semaphore("ssum"))
        with nc.Block() as blk:
            emit_ln_stats_phase(blk, tile_src, NTF, xt, sx, ssq, ssum,
                                s_sumsq, s_sum, scr_sq, scr_cp)
        with nc.Block() as blk:
            emit_stat_math_phase(blk, s_sum, s_sumsq, s_negm, s_var, s_rstd, NTF, D)
        with nc.Block() as blk:
            emit_rstd_phase(blk, s_var, s_rstd, NTF)

        xf_own = A(nc.sbuf_tensor("xf_own", [128, NT, NDC, 128], BF16))
        qf_fm = A(nc.sbuf_tensor("qf_fm", [128, NDC, S_CORE], BF16))
        kv2 = A(nc.sbuf_tensor("kv2", [128, NP, 128], BF16))
        ks_bd = A(nc.sbuf_tensor("ks_bd", [128, NH], BF16))

        with contextlib.ExitStack() as e3:
            A3 = e3.enter_context
            xln = A3(nc.sbuf_tensor("xln", [128, 2, D], BF16))
            xf = A3(nc.sbuf_tensor("xf", [128, 2, NDC, 128], BF16))
            scrA = A3(nc.sbuf_tensor("scrA", [128, 2, D], F32))
            scrB = A3(nc.sbuf_tensor("scrB", [128, 2, D], F32))
            scrC = A3(nc.sbuf_tensor("scrC", [128, 2, D], F32))
            kf_t = A3(nc.sbuf_tensor("kf_t", [128, 2, D], BF16))
            v_t = A3(nc.sbuf_tensor("v_t", [128, 2, D], BF16))
            sxt = DmaSlots(nc, SA, "sxt", 2)
            sxln = SA(nc.semaphore("sxln"))
            sxf = DmaSlots(nc, SA, "sxf", 2)
            skmm = SA(nc.semaphore("skmm"))
            svmm = SA(nc.semaphore("svmm"))
            sminev = SA(nc.semaphore("sminev"))
            skev = SA(nc.semaphore("skev"))
            sExp = SA(nc.semaphore("sExp"))
            skf = SA(nc.semaphore("skf"))
            svev = SA(nc.semaphore("svev"))
            skvmm = SA(nc.semaphore("skvmm"))
            kps = A3(nc.psum_tensor("kps", [128, D], F32))
            vps = A3(nc.psum_tensor("vps", [128, D], F32))
            kvps = A3(nc.psum_tensor("kvps", [128, NP, 128], F32))
            ksps = A3(nc.psum_tensor("ksps", [128, NDC], F32))
            bstk = contextlib.ExitStack()
            blk = bstk.enter_context(nc.Block())

            @blk.gpsimd
            def _(g):
                for t in range(NTF):
                    if t >= 2:
                        g.wait_ge(sxln, t - 1)
                    sxt.inc(g.dma_start(xt[:, t % 2, :], tile_src(t)), t)

            @blk.vector
            def _(v):
                v = DrainingEngine(v)
                for t in range(NTF):
                    sxt.wait(v, t)
                    if t >= 2:
                        sxf.wait(v, t - 2)
                    v.tensor_scalar(xln[:, t % 2, :], xt[:, t % 2, :],
                                    s_negm[:, t:t + 1], s_rstd[:, t:t + 1],
                                    ALU.add, ALU.mult).then_inc(sxln, 1)
                    v.wait_ge(skmm, t + 1)
                    if t >= 2:
                        v.wait_ge(sExp, t - 1)
                    v.tensor_scalar_min(scrA[:, t % 2, :], kps[:], 0.0).then_inc(sminev, 1)
                    v.tensor_scalar_max(scrC[:, t % 2, :], kps[:], 0.0).then_inc(skev, 1)
                    v.wait_ge(sExp, t + 1)
                    if t >= 2:
                        v.wait_ge(skvmm, t - 1)
                    v.tensor_tensor(kf_t[:, t % 2, :], scrB[:, t % 2, :],
                                    scrC[:, t % 2, :], ALU.add).then_inc(skf, 1)
                    v.wait_ge(svmm, t + 1)
                    v.tensor_copy(v_t[:, t % 2, :], vps[:]).then_inc(svev, 1)

            @blk.scalar
            def _(sc):
                sc = DrainingEngine(sc)
                for t in range(NTF):
                    sc.wait_ge(sminev, t + 1)
                    if t >= 2:
                        sc.wait_ge(skf, t - 1)
                    sc.activation(scrB[:, t % 2, :], scrA[:, t % 2, :],
                                  AF.Exp).then_inc(sExp, 1)

            @blk.sync
            def _(sy_):
                for t in range(NTF):
                    sy_.wait_ge(sxln, t + 1)
                    dst = (xf_own.ap()[:, t, :, :] if t < NT
                           else xf.ap()[:, t % 2, :, :])
                    sxf.inc(sy_.dma_start(
                        dst,
                        xln.ap()[:, t % 2, :].rearrange("p (c q) -> p c q", c=NDC),
                        transpose=True,
                    ), t)

            @blk.tensor
            def _(t_):
                for t in range(NTF):
                    sxf.wait(t_, t)
                    xfv = (xf_own.ap()[:, t, :, :] if t < NT
                           else xf.ap()[:, t % 2, :, :])
                    if t >= 1:
                        t_.wait_ge(skev, t)
                    for j in range(2):
                        for c in range(NDC):
                            mm_ = t_.matmul(kps[:, j * 512:(j + 1) * 512],
                                      xfv[:, c, :],
                                      wkT.ap()[:, 4 * j:4 * j + 4, c, :],
                                      start=(c == 0), stop=(c == NDC - 1))
                    mm_.then_inc(skmm, 1)
                    if t >= 1:
                        t_.wait_ge(svev, t)
                    for j in range(2):
                        for c in range(NDC):
                            mm_ = t_.matmul(vps[:, j * 512:(j + 1) * 512],
                                      xfv[:, c, :],
                                      wvT.ap()[:, 4 * j:4 * j + 4, c, :],
                                      start=(c == 0), stop=(c == NDC - 1))
                    mm_.then_inc(svmm, 1)
                    if t >= 1:
                        tp = t - 1
                        t_.wait_ge(skf, tp + 1)
                        t_.wait_ge(svev, tp + 1)
                        for pp in range(NP):
                            mm_ = t_.matmul(kvps[:, pp, :],
                                      kf_t[:, tp % 2, pp * 128:(pp + 1) * 128],
                                      v_t[:, tp % 2, pp * 128:(pp + 1) * 128],
                                      start=(tp == 0 and pp % 4 == 0),
                                      stop=(tp == NTF - 1),
                                      skip_group_check=True)
                        for c in range(NDC):
                            mm_ = t_.matmul(ksps[:, c:c + 1],
                                      kf_t[:, tp % 2, c * 128:(c + 1) * 128],
                                      ones_c[:],
                                      start=(tp == 0 and c == 0),
                                      stop=(tp == NTF - 1),
                                      skip_group_check=True)
                        mm_.then_inc(skvmm, 1)
                tp = NTF - 1
                t_.wait_ge(skf, tp + 1)
                t_.wait_ge(svev, tp + 1)
                for pp in range(NP):
                    mm_ = t_.matmul(kvps[:, pp, :],
                              kf_t[:, tp % 2, pp * 128:(pp + 1) * 128],
                              v_t[:, tp % 2, pp * 128:(pp + 1) * 128],
                              start=False, stop=True, skip_group_check=True)
                for c in range(NDC):
                    mm_ = t_.matmul(ksps[:, c:c + 1],
                              kf_t[:, tp % 2, c * 128:(c + 1) * 128],
                              ones_c[:],
                              start=False, stop=True, skip_group_check=True)
                mm_.then_inc(skvmm, 1)

            bstk.close()
            blk2 = bstk.enter_context(nc.Block())

            @blk2.vector
            def _(v):
                v = DrainingEngine(v)
                v.memset(kv2[:], 0.0)
                v.memset(ks_bd[:], 0.0)
                for pp in range(NP):
                    v.tensor_copy(kv2[0:64, pp, 0:64], kvps[0:64, pp, 0:64])
                    v.tensor_copy(kv2[64:128, pp, 64:128], kvps[64:128, pp, 64:128])
                    v.tensor_copy(ks_bd[0:64, 2 * pp:2 * pp + 1], ksps[0:64, pp:pp + 1])
                    v.tensor_copy(ks_bd[64:128, 2 * pp + 1:2 * pp + 2],
                                  ksps[64:128, pp:pp + 1])
            bstk.close()

        with contextlib.ExitStack() as e4:
            A4 = e4.enter_context
            scrA = A4(nc.sbuf_tensor("q_scrA", [128, 2, 512], F32))
            scrB = A4(nc.sbuf_tensor("q_scrB", [128, 2, 512], F32))
            scrC = A4(nc.sbuf_tensor("q_scrC", [128, 2, 512], F32))
            sqmm = SA(nc.semaphore("sqmm"))
            sminev = SA(nc.semaphore("q_sminev"))
            sqev = SA(nc.semaphore("q_sqev"))
            sExp = SA(nc.semaphore("q_sExp"))
            sqf = SA(nc.semaphore("q_sqf"))
            qps = A4(nc.psum_tensor("qps", [128, 2, 512], F32))
            blk = A4(nc.Block())
            NSLC = S_CORE // 512

            @blk.tensor
            def _(t_):
                k = 0
                for co in range(NDC):
                    for st in range(NSLC):
                        if k >= 2:
                            t_.wait_ge(sqev, k - 1)
                        for ci in range(NDC):
                            mm_ = t_.matmul(qps[:, k % 2, :],
                                      wqT.ap()[:, co, ci, :],
                                      xf_own.ap()[:, 4 * st:4 * st + 4, ci, :],
                                      start=(ci == 0), stop=(ci == NDC - 1))
                        mm_.then_inc(sqmm, 1)
                        k += 1

            @blk.vector
            def _(v):
                v = DrainingEngine(v)
                k = 0
                for co in range(NDC):
                    for st in range(NSLC):
                        v.wait_ge(sqmm, k + 1)
                        if k >= 2:
                            v.wait_ge(sExp, k - 1)
                        v.tensor_scalar_min(scrA[:, k % 2, :], qps[:, k % 2, :],
                                            0.0).then_inc(sminev, 1)
                        v.tensor_scalar_max(scrC[:, k % 2, :], qps[:, k % 2, :],
                                            0.0).then_inc(sqev, 1)
                        v.wait_ge(sExp, k + 1)
                        v.tensor_tensor(qf_fm[:, co, st * 512:(st + 1) * 512],
                                        scrB[:, k % 2, :], scrC[:, k % 2, :],
                                        ALU.add).then_inc(sqf, 1)
                        k += 1

            @blk.scalar
            def _(sc):
                sc = DrainingEngine(sc)
                for k in range(NDC * NSLC):
                    sc.wait_ge(sminev, k + 1)
                    if k >= 2:
                        sc.wait_ge(sqf, k - 1)
                    sc.activation(scrB[:, k % 2, :], scrA[:, k % 2, :],
                                  AF.Exp).then_inc(sExp, 1)

        with contextlib.ExitStack() as e5:
            A5 = e5.enter_context
            att = A5(nc.sbuf_tensor("att", [128, 2, D], BF16))
            af = A5(nc.sbuf_tensor("af", [128, 2, NDC, 128], BF16))
            dnv = A5(nc.sbuf_tensor("dnv", [128, 2, NH], F32))
            dscr = A5(nc.sbuf_tensor("dscr", [128, 2 * NH], F32))
            y_sb = A5(nc.sbuf_tensor("y_sb", [128, 2, D], BF16))
            sden = SA(nc.semaphore("sden"))
            sdev = SA(nc.semaphore("sdev"))
            satt = SA(nc.semaphore("satt"))
            saf = DmaSlots(nc, SA, "saf", 2)
            sop = SA(nc.semaphore("sop"))
            sy2 = SA(nc.semaphore("sy2"))
            spsq = SA(nc.semaphore("spsq"))
            syst = DmaSlots(nc, SA, "syst", 2)
            dps_d = A5(nc.psum_tensor("dps_d", [128, 2, 512], F32))
            qk_ps = A5(nc.psum_tensor("qk_ps", [128, 2, NP, 128], F32))
            ops = A5(nc.psum_tensor("ops", [128, D], F32))
            blk = A5(nc.Block())

            @blk.tensor
            def _(t_):
                for i in range(NT):
                    sl = slice(i * 128, (i + 1) * 128)
                    if i >= 2:
                        t_.wait_ge(sdev, i - 1)
                        t_.wait_ge(satt, i - 1)
                    for c in range(NDC):
                        mm_ = t_.matmul(dps_d[:, i % 2, 2 * c:2 * c + 2],
                                  qf_fm[:, c, sl],
                                  ks_bd[:, 2 * c:2 * c + 2],
                                  start=(c == 0), stop=True,
                                  skip_group_check=True)
                        mm_ = t_.matmul(qk_ps[:, i % 2, c, :],
                                  qf_fm[:, c, sl],
                                  kv2[:, c, :],
                                  start=True, stop=True)
                    mm_.then_inc(sden, 1)
                    if i >= 1:
                        ip = i - 1
                        saf.wait(t_, ip)
                        if ip >= 1:
                            t_.wait_ge(sy2, ip)
                        for j in range(2):
                            mm_ = t_.matmul(ops[:, j * 512:(j + 1) * 512],
                                      ones_r[:], bo_bf.ap()[0:1, j * 512:(j + 1) * 512],
                                      start=True, stop=False)
                            for c in range(NDC):
                                mm_ = t_.matmul(ops[:, j * 512:(j + 1) * 512],
                                          af[:, ip % 2, c, :],
                                          woT.ap()[:, 4 * j:4 * j + 4, c, :],
                                          start=False, stop=(c == NDC - 1))
                        mm_.then_inc(sop, 1)
                ip = NT - 1
                saf.wait(t_, ip)
                t_.wait_ge(sy2, ip)
                for j in range(2):
                    mm_ = t_.matmul(ops[:, j * 512:(j + 1) * 512],
                              ones_r[:], bo_bf.ap()[0:1, j * 512:(j + 1) * 512],
                              start=True, stop=False)
                    for c in range(NDC):
                        mm_ = t_.matmul(ops[:, j * 512:(j + 1) * 512],
                                  af[:, ip % 2, c, :],
                                  woT.ap()[:, 4 * j:4 * j + 4, c, :],
                                  start=False, stop=(c == NDC - 1))
                mm_.then_inc(sop, 1)

            @blk.vector
            def _(v):
                v = DrainingEngine(v)
                for i in range(NT):
                    v.wait_ge(sden, i + 1)
                    v.tensor_scalar_add(dnv[:, i % 2, :], dps_d[:, i % 2, 0:NH], EPS)
                    emit_recip_newton(v, dnv[:, i % 2, :], dnv[:, i % 2, :],
                                      dscr[:, 0:NH], dscr[:, NH:2 * NH]
                                      ).then_inc(sdev, 1)
                    if i >= 2:
                        saf.wait(v, i - 2)
                    v.tensor_tensor(
                        att.ap()[:, i % 2, :].rearrange("p (c t q) -> p c t q", c=NP, t=2),
                        qk_ps.ap()[:, i % 2, :, :].rearrange("p c (t q) -> p c t q", t=2),
                        dnv.ap()[:, i % 2, :].rearrange("p (c t) -> p c t", c=NP)
                        .unsqueeze(3).broadcast_to([128, NP, 2, 64]),
                        ALU.mult).then_inc(satt, 1)
                    v.wait_ge(sop, i + 1)
                    if i >= 2:
                        syst.wait(v, i - 2)
                        v.wait_ge(spsq, i - 1)
                    v.tensor_scalar(y_sb[:, i % 2, :], ops[:], 1.0, None,
                                    ALU.mult, ALU.add,
                                    accum_out=p_sum[:, i:i + 1]).then_inc(sy2, 1)

            @blk.scalar
            def _(sc):
                sc = DrainingEngine(sc)
                for i in range(NT):
                    sc.wait_ge(sy2, i + 1)
                    sc.activation(scr_sq[:, i % 2, :], y_sb[:, i % 2, :], AF.Square,
                                  accum_out=p_sumsq[:, i:i + 1]).then_inc(spsq, 1)

            @blk.sync
            def _(sy_):
                for i in range(NT):
                    sy_.wait_ge(satt, i + 1)
                    saf.inc(sy_.dma_start(
                        af.ap()[:, i % 2, :, :],
                        att.ap()[:, i % 2, :].rearrange("p (c q) -> p c q", c=NDC),
                        transpose=True,
                    ), i)
                    sy_.wait_ge(sy2, i + 1)
                    syst.inc(sy_.dma_start(y_scr.ap()[i * 128:(i + 1) * 128, :],
                                           y_sb[:, i % 2, :]), i)
                for k in range(2):
                    sy_.wait_ge(syst.sems[k], 16 * ((NT + 1 - k) // 2))

        with nc.Block() as blk:
            emit_stat_math_phase(blk, p_sum, p_sumsq, p_negm, p_var, p_rstd, NT, D)
        with nc.Block() as blk:
            emit_rstd_phase(blk, p_var, p_rstd, NT)
        with contextlib.ExitStack() as es6:
            emit_residual_phase(nc, es6.enter_context,
                                lambda i: x_in.ap()[i * 128:(i + 1) * 128, :],
                                lambda i: y_scr.ap()[i * 128:(i + 1) * 128, :],
                                lambda i: x_out.ap()[i * 128:(i + 1) * 128, :],
                                NT, xt, p_negm, p_rstd, SA=SA)


def build_l0_attn():
    nc0 = bass.Bass("TRN2", target_bir_lowering=False)
    nc = NcView(nc0, "a_")
    x_in = nc0.dram_tensor("x_in", [S_CORE, D], F32, kind="ExternalInput")
    x_oth = nc0.dram_tensor("x_oth", [S_CORE, D], F32, kind="ExternalInput")
    wqd = nc0.dram_tensor("wq", [D, D], F32, kind="ExternalInput")
    wkd = nc0.dram_tensor("wk", [D, D], F32, kind="ExternalInput")
    wvd = nc0.dram_tensor("wv", [D, D], F32, kind="ExternalInput")
    wod = nc0.dram_tensor("wo", [D, D], F32, kind="ExternalInput")
    bod = nc0.dram_tensor("bo", [1, D], F32, kind="ExternalInput")
    x_out = nc0.dram_tensor("x_out", [S_CORE, D], F32, kind="ExternalOutput")
    with contextlib.ExitStack() as es:
        emit_l0_attn(nc, lambda h: h, x_in, x_oth, wqd, wkd, wvd, wod, bod, x_out)
    return nc0





def emit_l1_attn(nc, SA, x_in, x_oth, wds, bds, x_out, fused=False,
                  replica_groups=None):
    SA2 = SA
    NTF = 2 * NT
    SM = 1.0 / 32.0
    w1d, w2d, w3d, w4d = wds
    b1d, b2d, b3d, b4d = bds

    y_scr = nc.dram_tensor("y_scr", [S_CORE, D], BF16)
    q_scr = nc.dram_tensor("q_scr", [NT, 128, NDC * 128], BF16)
    if fused:
        cc_in = nc.dram_tensor("cc_in", [128, 2 * NT * 1024], BF16)
        cc_out = nc.dram_tensor("cc_out", [256, 2 * NT * 1024], BF16)

    def tile_src(t):
        if t < NT:
            return x_in.ap()[t * 128:(t + 1) * 128, :]
        return x_oth.ap()[(t - NT) * 128:(t - NT + 1) * 128, :]

    NTP = NT if fused else NTF
    with contextlib.ExitStack() as es:
        A = es.enter_context
        K_fm = A(nc.sbuf_tensor("K_fm", [128, NTF, NDC, 128], BF16))
        V_tm = A(nc.sbuf_tensor("V_tm", [128, NTF, D], BF16))
        b_bf = A(nc.sbuf_tensor("b_bf", [1, 4, D], BF16))
        ones_r = A(nc.sbuf_tensor("ones_r", [1, 128], BF16))
        xt = A(nc.sbuf_tensor("xt", [128, 2, D], F32))
        scr_sq = A(nc.sbuf_tensor("scr_sq", [128, 1, D], F32))
        scr_cp = A(nc.sbuf_tensor("scr_cp", [128, 1, D], BF16))
        s_sum = A(nc.sbuf_tensor("s_sum", [128, NTF], F32))
        s_sumsq = A(nc.sbuf_tensor("s_sumsq", [128, NTF], F32))
        s_negm = A(nc.sbuf_tensor("s_negm", [128, NTF], F32))
        s_var = A(nc.sbuf_tensor("s_var", [128, NTF], F32))
        s_rstd = A(nc.sbuf_tensor("s_rstd", [128, NTF], F32))
        p_sum = A(nc.sbuf_tensor("p_sum", [128, NT], F32))
        p_sumsq = A(nc.sbuf_tensor("p_sumsq", [128, NT], F32))
        p_negm = A(nc.sbuf_tensor("p_negm", [128, NT], F32))
        p_var = A(nc.sbuf_tensor("p_var", [128, NT], F32))
        p_rstd = A(nc.sbuf_tensor("p_rstd", [128, NT], F32))
        dwg = SA(nc.semaphore("dwg"))

        with nc.Block() as blk:
            @blk.gpsimd
            def _(g):
                for bi, bd in enumerate((b1d, b2d, b3d, b4d)):
                    g.dma_start(b_bf[0:1, bi, :], bd.ap()).then_inc(dwg, 16)
                g.wait_ge(dwg, 64)
            @blk.vector
            def _(v):
                v.memset(ones_r[:], 1.0)

        sx = DmaSlots(nc, SA, "sx", 2)
        ssq = SA(nc.semaphore("ssq"))
        ssum = SA(nc.semaphore("ssum"))
        with nc.Block() as blk:
            emit_ln_stats_phase(blk, tile_src, NTP, xt, sx, ssq, ssum,
                                s_sumsq, s_sum, scr_sq, scr_cp)
        with nc.Block() as blk:
            emit_stat_math_phase(blk, s_sum, s_sumsq, s_negm, s_var, s_rstd, NTP, D)
        with nc.Block() as blk:
            emit_rstd_phase(blk, s_var, s_rstd, NTP)

        def emit_proj_phase(A3, tiles, q_mode):
            if q_mode:
                w1 = A3(nc.sbuf_tensor("w1b", [128, NDC, D], BF16))
                wlist = [(w1d, w1)]
            else:
                w2 = A3(nc.sbuf_tensor("w2b", [128, NDC, D], BF16))
                w3 = A3(nc.sbuf_tensor("w3b", [128, NDC, D], BF16))
                wlist = [(w2d, w2), (w3d, w3)]
            pref = "q" if q_mode else "kv"
            dwl = SA(nc.semaphore(pref + "_dwl"))
            with nc.Block() as blk:
                @blk.gpsimd
                def _(g):
                    for wi, (wd_, wt) in enumerate(wlist):
                        g.dma_start(wt[:],
                                    wd_.ap().rearrange("(c p) o -> p c o", p=128)
                                    ).then_inc(dwl, 16)
                    g.wait_ge(dwl, 16 * len(wlist))

            xln = A3(nc.sbuf_tensor(pref + "_xln", [128, 2, D], BF16))
            xf = A3(nc.sbuf_tensor(pref + "_xf", [128, 2, NDC, 128], BF16))
            sxt = DmaSlots(nc, SA, pref + "_sxt", 2)
            sxln = SA(nc.semaphore(pref + "_sxln"))
            sxf = DmaSlots(nc, SA, pref + "_sxf", 2)
            if q_mode:
                qtile = A3(nc.sbuf_tensor("qtile", [128, 2, NDC * 128], BF16))
                sqmm = SA(nc.semaphore("sqmm"))
                sqev = SA(nc.semaphore("sqev"))
                sqst = DmaSlots(nc, SA, "sqst", 2)
                qps = A3(nc.psum_tensor("qps", [128, 2, NDC, 128], F32))
            else:
                skmm = SA(nc.semaphore("skmm"))
                skev = SA(nc.semaphore("skev"))
                svmm = SA(nc.semaphore("svmm"))
                svev = SA(nc.semaphore("svev"))
                kps = A3(nc.psum_tensor("kps", [128, 2, NDC, 128], F32))
                vps = A3(nc.psum_tensor("vps", [128, D], F32))
            blk = A3(nc.Block())

            @blk.gpsimd
            def _(g):
                for t in tiles:
                    ti = tiles.index(t)
                    if ti >= 2:
                        g.wait_ge(sxln, ti - 1)
                    sxt.inc(g.dma_start(xt[:, ti % 2, :], tile_src(t)), ti)

            @blk.vector
            def _(v):
                v = DrainingEngine(v)
                for ti, t in enumerate(tiles):
                    sxt.wait(v, ti)
                    if ti >= 2:
                        sxf.wait(v, ti - 2)
                    v.tensor_scalar(xln[:, ti % 2, :], xt[:, ti % 2, :],
                                    s_negm[:, t:t + 1], s_rstd[:, t:t + 1],
                                    ALU.add, ALU.mult).then_inc(sxln, 1)
                    if q_mode:
                        v.wait_ge(sqmm, ti + 1)
                        if ti >= 2:
                            sqst.wait(v, ti - 2)
                        v.tensor_copy(qtile[:, ti % 2, :],
                                      qps.ap()[:, ti % 2, :, :]
                                      .rearrange("p a b -> p (a b)")).then_inc(sqev, 1)
                    else:
                        v.wait_ge(skmm, ti + 1)
                        v.tensor_copy(K_fm.ap()[:, t, :, :]
                                      .rearrange("p a b -> p (a b)"),
                                      kps.ap()[:, ti % 2, :, :]
                                      .rearrange("p a b -> p (a b)")).then_inc(skev, 1)
                        v.wait_ge(svmm, ti + 1)
                        v.tensor_copy(V_tm[:, t, :], vps[:]).then_inc(svev, 1)

            @blk.sync
            def _(sy_):
                for ti, t in enumerate(tiles):
                    sy_.wait_ge(sxln, ti + 1)
                    sxf.inc(sy_.dma_start(
                        xf.ap()[:, ti % 2, :, :],
                        xln.ap()[:, ti % 2, :].rearrange("p (c q) -> p c q", c=NDC),
                        transpose=True,
                    ), ti)
                    if q_mode:
                        sy_.wait_ge(sqev, ti + 1)
                        sqst.inc(sy_.dma_start(q_scr.ap()[t, :, :],
                                               qtile[:, ti % 2, :]), ti)
                if q_mode:
                    for k in range(2):
                        sy_.wait_ge(sqst.sems[k], 16 * ((len(tiles) + 1 - k) // 2))

            @blk.tensor
            def _(t_):
                for ti, t in enumerate(tiles):
                    sxf.wait(t_, ti)
                    if q_mode:
                        if ti >= 2:
                            t_.wait_ge(sqev, ti - 1)
                        for ce in range(NDC):
                            mm_ = t_.matmul(qps[:, ti % 2, ce, :],
                                      b_bf.ap()[0:1, 0, ce * 128:(ce + 1) * 128],
                                      ones_r[:], start=True, stop=False)
                            for ci in range(NDC):
                                mm_ = t_.matmul(qps[:, ti % 2, ce, :],
                                          w1[:, ci, ce * 128:(ce + 1) * 128],
                                          xf[:, ti % 2, ci, :],
                                          start=False, stop=(ci == NDC - 1))
                        mm_.then_inc(sqmm, 1)
                    else:
                        if ti >= 2:
                            t_.wait_ge(skev, ti - 1)
                        for ce in range(NDC):
                            mm_ = t_.matmul(kps[:, ti % 2, ce, :],
                                      b_bf.ap()[0:1, 1, ce * 128:(ce + 1) * 128],
                                      ones_r[:], start=True, stop=False)
                            for ci in range(NDC):
                                mm_ = t_.matmul(kps[:, ti % 2, ce, :],
                                          w2[:, ci, ce * 128:(ce + 1) * 128],
                                          xf[:, ti % 2, ci, :],
                                          start=False, stop=(ci == NDC - 1))
                        mm_.then_inc(skmm, 1)
                        if ti >= 1:
                            t_.wait_ge(svev, ti)
                        for j in range(2):
                            mm_ = t_.matmul(vps[:, j * 512:(j + 1) * 512], ones_r[:],
                                      b_bf.ap()[0:1, 2, j * 512:(j + 1) * 512],
                                      start=True, stop=False)
                            for ci in range(NDC):
                                mm_ = t_.matmul(vps[:, j * 512:(j + 1) * 512],
                                          xf[:, ti % 2, ci, :],
                                          w3[:, ci, j * 512:(j + 1) * 512],
                                          start=False, stop=(ci == NDC - 1))
                        mm_.then_inc(svmm, 1)

        with contextlib.ExitStack() as e1:
            emit_proj_phase(e1.enter_context, list(range(NTP)), False)
        if fused:
            ccst = SA(nc.semaphore("ccst"))
            ccld = SA(nc.semaphore("ccld"))
            ccdone = SA(nc.semaphore("ccdone"))
            KW = NT * 1024
            with nc.Block() as blk:
                @blk.sync
                def _(sy_):
                    sy_.dma_start(cc_in.ap()[:, 0:KW],
                                  K_fm.ap()[:, 0:NT, :, :]
                                  .rearrange("p a b c -> p (a b c)")).then_inc(ccst, 16)
                    sy_.dma_start(cc_in.ap()[:, KW:2 * KW],
                                  V_tm.ap()[:, 0:NT, :]
                                  .rearrange("p a b -> p (a b)")).then_inc(ccst, 16)
                    sy_.wait_ge(ccst, 32)
            with nc.Block() as blk:
                @blk.gpsimd
                def _(g):
                    g.collective_compute(
                        "AllGather", ALU.bypass,
                        replica_groups=replica_groups,
                        ins=[cc_in.ap().opt()],
                        outs=[cc_out.ap().opt()]).then_inc(ccdone, 1)
                    g.wait_ge(ccdone, 1)
            with nc.Block() as blk:
                @blk.sync
                def _(sy_):
                    for r in range(2):
                        sy_.dma_start(
                            K_fm.ap()[:, r * NT:(r + 1) * NT, :, :]
                            .rearrange("p a b c -> p (a b c)"),
                            cc_out.ap()[r * 128:(r + 1) * 128, 0:KW]).then_inc(ccld, 16)
                        sy_.dma_start(
                            V_tm.ap()[:, r * NT:(r + 1) * NT, :]
                            .rearrange("p a b -> p (a b)"),
                            cc_out.ap()[r * 128:(r + 1) * 128, KW:2 * KW]).then_inc(ccld, 16)
                    sy_.wait_ge(ccld, 64)
        with contextlib.ExitStack() as e1:
            emit_proj_phase(e1.enter_context, list(range(NT)), True)

        with contextlib.ExitStack() as e2:
            A2 = e2.enter_context
            w4 = A2(nc.sbuf_tensor("w4b", [128, NDC, D], BF16))
            dwl = SA(nc.semaphore("c2_dwl"))
            with nc.Block() as blk:
                @blk.gpsimd
                def _(g):
                    g.dma_start(w4[:], w4d.ap().rearrange("(c p) o -> p c o", p=128)
                                ).then_inc(dwl, 16)
                    g.wait_ge(dwl, 16)

            qt = A2(nc.sbuf_tensor("qt", [128, 2, NDC, 128], BF16))
            sc_sb = A2(nc.sbuf_tensor("sc_sb", [128, S], F32))
            mx8 = A2(nc.sbuf_tensor("mx8", [128, 8], F32))
            bias_c = A2(nc.sbuf_tensor("bias_c", [128, 1], F32))
            l4 = A2(nc.sbuf_tensor("l4", [128, 4], F32))
            rl = A2(nc.sbuf_tensor("rl", [128, 1], F32))
            rscr = A2(nc.sbuf_tensor("rscr", [128, 2], F32))
            P = A2(nc.sbuf_tensor("P", [128, 2, 1024], BF16))
            PT = A2(nc.sbuf_tensor("PT", [128, 2, 8, 128], BF16))
            att = A2(nc.sbuf_tensor("att", [128, 2, D], BF16))
            af = A2(nc.sbuf_tensor("af", [128, 2, NDC, 128], BF16))
            y_sb = A2(nc.sbuf_tensor("y_sb", [128, 2, D], BF16))
            sq = DmaSlots(nc, SA, "c2_sq", 2)
            ssc = SA(nc.semaphore("ssc"))
            ssce = SA(nc.semaphore("ssce"))
            sbias = SA(nc.semaphore("sbias"))
            sexp = SA(nc.semaphore("sexp"))
            spt = DmaSlots(nc, SA, "c2_spt", 2)
            samm = SA(nc.semaphore("samm"))
            sptd = SA(nc.semaphore("sptd"))
            srl = SA(nc.semaphore("srl"))
            satt = SA(nc.semaphore("satt"))
            saf = DmaSlots(nc, SA, "c2_saf", 2)
            sop = SA(nc.semaphore("sop"))
            sy2 = SA(nc.semaphore("sy2"))
            spsq = SA(nc.semaphore("spsq"))
            syst = DmaSlots(nc, SA, "c2_syst", 2)
            sc_ps = A2(nc.psum_tensor("sc_ps", [128, 2, 512], F32))
            aps = A2(nc.psum_tensor("aps", [128, D], F32))
            ops = A2(nc.psum_tensor("ops", [128, D], F32))
            blk = A2(nc.Block())

            @blk.gpsimd
            def _(g):
                for i in range(NT):
                    if i >= 2:
                        g.wait_ge(sptd, 4 * (i - 1))
                    sq.inc(g.dma_start(
                        qt.ap()[:, i % 2, :, :].rearrange("p a b -> p (a b)"),
                        q_scr.ap()[i, :, :]), i)

            @blk.tensor
            def _(t_):
                for i in range(NT):
                    sq.wait(t_, i)
                    for j in range(8):
                        gidx = i * 8 + j
                        if gidx >= 2:
                            t_.wait_ge(ssce, gidx - 1)
                        for ce in range(NDC):
                            mm_ = t_.matmul(sc_ps[:, j % 2, :],
                                      qt[:, i % 2, ce, :],
                                      K_fm.ap()[:, 4 * j:4 * j + 4, ce, :],
                                      start=(ce == 0), stop=(ce == NDC - 1))
                        mm_.then_inc(ssc, 1)
                    if i >= 1:
                        t_.wait_ge(satt, i)
                    for c in range(4):
                        spt.wait(t_, i * 4 + c)
                        for tt in range(8):
                            for j2 in range(2):
                                mm_ = t_.matmul(aps[:, j2 * 512:(j2 + 1) * 512],
                                          PT[:, c % 2, tt, :],
                                          V_tm[:, 8 * c + tt,
                                               j2 * 512:(j2 + 1) * 512],
                                          start=(c == 0 and tt == 0),
                                          stop=(c == 3 and tt == 7),
                                          skip_group_check=True)
                        mm_.then_inc(sptd, 1)
                    if i >= 1:
                        ip = i - 1
                        saf.wait(t_, ip)
                        if ip >= 1:
                            t_.wait_ge(sy2, ip)
                        for j in range(2):
                            mm_ = t_.matmul(ops[:, j * 512:(j + 1) * 512], ones_r[:],
                                      b_bf.ap()[0:1, 3, j * 512:(j + 1) * 512],
                                      start=True, stop=False)
                            for ce in range(NDC):
                                mm_ = t_.matmul(ops[:, j * 512:(j + 1) * 512],
                                          af[:, ip % 2, ce, :],
                                          w4[:, ce, j * 512:(j + 1) * 512],
                                          start=False, stop=(ce == NDC - 1))
                        mm_.then_inc(sop, 1)
                ip = NT - 1
                saf.wait(t_, ip)
                t_.wait_ge(sy2, ip)
                for j in range(2):
                    mm_ = t_.matmul(ops[:, j * 512:(j + 1) * 512], ones_r[:],
                              b_bf.ap()[0:1, 3, j * 512:(j + 1) * 512],
                              start=True, stop=False)
                    for ce in range(NDC):
                        mm_ = t_.matmul(ops[:, j * 512:(j + 1) * 512],
                                  af[:, ip % 2, ce, :],
                                  w4[:, ce, j * 512:(j + 1) * 512],
                                  start=False, stop=(ce == NDC - 1))
                mm_.then_inc(sop, 1)

            @blk.vector
            def _(v):
                v = DrainingEngine(v)

                def emit_yev(v, i):
                    v.wait_ge(sop, i + 1)
                    if i >= 2:
                        syst.wait(v, i - 2)
                        v.wait_ge(spsq, i - 1)
                    v.tensor_scalar(y_sb[:, i % 2, :], ops[:], 1.0, None,
                                    ALU.mult, ALU.add,
                                    accum_out=p_sum[:, i:i + 1]).then_inc(sy2, 1)
                for i in range(NT):
                    for j in range(8):
                        gidx = i * 8 + j
                        v.wait_ge(ssc, gidx + 1)
                        v.tensor_copy(sc_sb[:, j * 512:(j + 1) * 512],
                                      sc_ps[:, j % 2, :])
                        v.reduce_max(mx8[:, j:j + 1], sc_sb[:, j * 512:(j + 1) * 512],
                                     axis=AX).then_inc(ssce, 1)
                    v.reduce_max(bias_c[:], mx8[:], axis=AX)
                    v.tensor_scalar_mul(bias_c[:], bias_c[:], -SM)
                    v.drain().then_inc(sbias, 1)
                    v.wait_ge(sexp, 4 * (i + 1))
                    v.reduce_sum(rl[:], l4[:], axis=AX)
                    emit_recip_newton(v, rl[:], rl[:], rscr[:, 0:1],
                                      rscr[:, 1:2]).then_inc(srl, 1)
                    v.wait_ge(sptd, 4 * (i + 1))
                    if i >= 2:
                        saf.wait(v, i - 2)
                    v.tensor_scalar_mul(att[:, i % 2, :], aps[:], rl[:, 0:1]
                                        ).then_inc(satt, 1)
                    if i >= 1:
                        emit_yev(v, i - 1)
                if NT >= 1:
                    emit_yev(v, NT - 1)

            @blk.scalar
            def _(sc):
                sc = DrainingEngine(sc)
                def emit_sq(i):
                    sc.wait_ge(sy2, i + 1)
                    sc.activation(scr_sq[:, 0, :], y_sb[:, i % 2, :], AF.Square,
                                  accum_out=p_sumsq[:, i:i + 1]).then_inc(spsq, 1)
                for i in range(NT):
                    sc.wait_ge(sbias, i + 1)
                    for c in range(4):
                        if i * 4 + c >= 2:
                            spt.wait(sc, i * 4 + c - 2)
                        sc.activation(P[:, c % 2, :], sc_sb[:, c * 1024:(c + 1) * 1024],
                                      AF.Exp, bias=bias_c[:, 0:1], scale=SM,
                                      accum_out=l4[:, c:c + 1]).then_inc(sexp, 1)
                    if i >= 1:
                        emit_sq(i - 1)
                emit_sq(NT - 1)

            @blk.sync
            def _(sy_):
                def emit_af(i):
                    sy_.wait_ge(satt, i + 1)
                    saf.inc(sy_.dma_start(
                        af.ap()[:, i % 2, :, :],
                        att.ap()[:, i % 2, :].rearrange("p (c q) -> p c q", c=NDC),
                        transpose=True,
                    ), i)

                def emit_yst(i):
                    sy_.wait_ge(sy2, i + 1)
                    syst.inc(sy_.dma_start(y_scr.ap()[i * 128:(i + 1) * 128, :],
                                           y_sb[:, i % 2, :]), i)
                for i in range(NT):
                    for c in range(4):
                        sy_.wait_ge(sexp, i * 4 + c + 1)
                        if i * 4 + c >= 2:
                            sy_.wait_ge(sptd, i * 4 + c - 1)
                        spt.inc(sy_.dma_start(
                            PT.ap()[:, c % 2, :, :],
                            P.ap()[:, c % 2, :].rearrange("p (a b) -> p a b", a=NDC),
                            transpose=True,
                        ), i * 4 + c)
                    emit_af(i)
                    if i >= 1:
                        emit_yst(i - 1)
                emit_yst(NT - 1)
                for k in range(2):
                    sy_.wait_ge(syst.sems[k], 16 * ((NT + 1 - k) // 2))

        with nc.Block() as blk:
            emit_stat_math_phase(blk, p_sum, p_sumsq, p_negm, p_var, p_rstd, NT, D)
        with nc.Block() as blk:
            emit_rstd_phase(blk, p_var, p_rstd, NT)
        with contextlib.ExitStack() as es6:
            emit_residual_phase(nc, es6.enter_context,
                                lambda i: x_in.ap()[i * 128:(i + 1) * 128, :],
                                lambda i: y_scr.ap()[i * 128:(i + 1) * 128, :],
                                lambda i: x_out.ap()[i * 128:(i + 1) * 128, :],
                                NT, xt, p_negm, p_rstd, SA=SA2)


def build_l1_attn():
    nc0 = bass.Bass("TRN2", target_bir_lowering=False)
    nc = NcView(nc0, "c_")
    x_in = nc0.dram_tensor("x_in", [S_CORE, D], F32, kind="ExternalInput")
    x_oth = nc0.dram_tensor("x_oth", [S_CORE, D], F32, kind="ExternalInput")
    wds = [nc0.dram_tensor(f"w{k}", [D, D], F32, kind="ExternalInput") for k in range(1, 5)]
    bds = [nc0.dram_tensor(f"b{k}", [1, D], F32, kind="ExternalInput") for k in range(1, 5)]
    x_out = nc0.dram_tensor("x_out", [S_CORE, D], F32, kind="ExternalOutput")
    with contextlib.ExitStack() as es:
        emit_l1_attn(nc, lambda h: h, x_in, x_oth, wds, bds, x_out)
    return nc0




def build_fused():
    nc0 = bass.Bass("TRN2", target_bir_lowering=False)
    x_in = nc0.dram_tensor("x_in", [S_CORE, D], F32, kind="ExternalInput")
    x_oth = nc0.dram_tensor("x_oth", [S_CORE, D], F32, kind="ExternalInput")
    awq = nc0.dram_tensor("awq", [D, D], F32, kind="ExternalInput")
    awk = nc0.dram_tensor("awk", [D, D], F32, kind="ExternalInput")
    awv = nc0.dram_tensor("awv", [D, D], F32, kind="ExternalInput")
    awo = nc0.dram_tensor("awo", [D, D], F32, kind="ExternalInput")
    abo = nc0.dram_tensor("abo", [1, D], F32, kind="ExternalInput")
    bwg = nc0.dram_tensor("bwg", [4096, D], F32, kind="ExternalInput")
    bwu = nc0.dram_tensor("bwu", [4096, D], F32, kind="ExternalInput")
    bwd = nc0.dram_tensor("bwd", [D, 4096], F32, kind="ExternalInput")
    cws = [nc0.dram_tensor(f"cw{k}", [D, D], F32, kind="ExternalInput") for k in range(1, 5)]
    cbs = [nc0.dram_tensor(f"cb{k}", [1, D], F32, kind="ExternalInput") for k in range(1, 5)]
    dwg = nc0.dram_tensor("dwg", [3072, D], I32, kind="ExternalInput")
    dwu = nc0.dram_tensor("dwu", [3072, D], I32, kind="ExternalInput")
    dwd = nc0.dram_tensor("dwd", [D, 3072], I32, kind="ExternalInput")
    x_out = nc0.dram_tensor("x_out", [S_CORE, D], F32, kind="ExternalOutput")
    x1 = nc0.dram_tensor("x1", [S_CORE, D], F32)
    x2 = nc0.dram_tensor("x2", [S_CORE, D], F32)
    x3 = nc0.dram_tensor("x3", [S_CORE, D], F32)
    rg = [[0, 1], [2, 3], [4, 5], [6, 7]]

    def scoped_sa():
        bag = []

        def SA(h):
            bag.append(h)
            return h
        return bag, SA

    def release(bag):
        nc0.clear_and_free_semaphores(bag)
        nc0.all_engine_barrier()

    bag, SA = scoped_sa()
    emit_l0_attn(NcView(nc0, "a_"), SA, x_in, x_oth, awq, awk, awv, awo, abo, x1)
    release(bag)
    bag, SA = scoped_sa()
    emit_l0_ffn(NcView(nc0, "b_"), SA, x1, bwg, bwu, bwd, x2)
    release(bag)
    bag, SA = scoped_sa()
    emit_l1_attn(NcView(nc0, "c_"), SA, x2, None, cws, cbs, x3,
                 fused=True, replica_groups=rg)
    release(bag)
    bag, SA = scoped_sa()
    emit_l1_ffn(NcView(nc0, "d_"), SA, x3, dwg, dwu, dwd, x_out)
    return nc0


def _run_fused(inputs):
    nc = _get_nc("fused", build_fused)
    x = np.asarray(inputs["L0_attn_x"], dtype=np.float32)
    xs = _scatter_x(x)
    com = {
        "awq": np.ascontiguousarray(inputs["L0_attn_wq"], dtype=np.float32),
        "awk": np.ascontiguousarray(inputs["L0_attn_wk"], dtype=np.float32),
        "awv": np.ascontiguousarray(inputs["L0_attn_wv"], dtype=np.float32),
        "awo": np.ascontiguousarray(inputs["L0_attn_wo"], dtype=np.float32),
        "abo": np.ascontiguousarray(inputs["L0_attn_bo"], dtype=np.float32).reshape(1, D),
        "bwg": np.ascontiguousarray(inputs["L0_ffn_w_gate"], dtype=np.float32),
        "bwu": np.ascontiguousarray(inputs["L0_ffn_w_up"], dtype=np.float32),
        "bwd": np.ascontiguousarray(inputs["L0_ffn_w_down"], dtype=np.float32),
        "dwg": np.ascontiguousarray(inputs["L1_ffn_w_gate"], dtype=np.int32),
        "dwu": np.ascontiguousarray(inputs["L1_ffn_w_up"], dtype=np.int32),
        "dwd": np.ascontiguousarray(inputs["L1_ffn_w_down"], dtype=np.int32),
    }
    for k in range(1, 5):
        com[f"cw{k}"] = np.ascontiguousarray(inputs[f"L1_attn_W{k}"], dtype=np.float32)
        com[f"cb{k}"] = np.ascontiguousarray(inputs[f"L1_attn_b{k}"], dtype=np.float32).reshape(1, D)
    in_maps = []
    for c in range(8):
        m = dict(com)
        m["x_in"] = xs[c]
        m["x_oth"] = xs[c ^ 1]
        in_maps.append(m)
    res = run_bass_kernel_spmd(nc, in_maps, core_ids=list(range(8)))
    return _gather_x(res)



_NC_CACHE = {}


def _get_nc(name, builder):
    if name not in _NC_CACHE:
        _NC_CACHE[name] = builder()
    return _NC_CACHE[name]


def _scatter_x(x):
    out = []
    for core in range(8):
        b, half = core // 2, core % 2
        out.append(np.ascontiguousarray(
            x[b, half * S_CORE:(half + 1) * S_CORE, :], dtype=np.float32))
    return out


def _gather_x(res, key="x_out"):
    out = np.empty((B, S, D), dtype=np.float32)
    for core in range(8):
        b, half = core // 2, core % 2
        out[b, half * S_CORE:(half + 1) * S_CORE, :] = res.results[core][key]
    return out


def _run_l1_ffn(x, wg, wu, wd):
    nc = _get_nc("l1_ffn", build_l1_ffn)
    wg = np.ascontiguousarray(wg, dtype=np.int32)
    wu = np.ascontiguousarray(wu, dtype=np.int32)
    wd = np.ascontiguousarray(wd, dtype=np.int32)
    xs = _scatter_x(x)
    in_maps = [{"x_in": xs[c], "wg": wg, "wu": wu, "wd": wd} for c in range(8)]
    res = run_bass_kernel_spmd(nc, in_maps, core_ids=list(range(8)))
    return _gather_x(res)


def _run_l0_ffn(x, wg, wu, wd):
    nc = _get_nc("l0_ffn", build_l0_ffn)
    wg = np.ascontiguousarray(wg, dtype=np.float32)
    wu = np.ascontiguousarray(wu, dtype=np.float32)
    wd = np.ascontiguousarray(wd, dtype=np.float32)
    xs = _scatter_x(x)
    in_maps = [{"x_in": xs[c], "wg": wg, "wu": wu, "wd": wd} for c in range(8)]
    res = run_bass_kernel_spmd(nc, in_maps, core_ids=list(range(8)))
    return _gather_x(res)


def _run_l0_attn(x, wq, wk, wv, wo, bo):
    nc = _get_nc("l0_attn", build_l0_attn)
    wq = np.ascontiguousarray(wq, dtype=np.float32)
    wk = np.ascontiguousarray(wk, dtype=np.float32)
    wv = np.ascontiguousarray(wv, dtype=np.float32)
    wo = np.ascontiguousarray(wo, dtype=np.float32)
    bo = np.ascontiguousarray(bo, dtype=np.float32).reshape(1, D)
    xs = _scatter_x(x)
    in_maps = []
    for c in range(8):
        oth = c ^ 1
        in_maps.append({"x_in": xs[c], "x_oth": xs[oth], "wq": wq, "wk": wk,
                        "wv": wv, "wo": wo, "bo": bo})
    res = run_bass_kernel_spmd(nc, in_maps, core_ids=list(range(8)))
    return _gather_x(res)


def _run_l1_attn(x, w1, b1, w2, b2, w3, b3, w4, b4):
    nc = _get_nc("l1_attn", build_l1_attn)
    ws = [np.ascontiguousarray(w, dtype=np.float32) for w in (w1, w2, w3, w4)]
    bs = [np.ascontiguousarray(b, dtype=np.float32).reshape(1, D) for b in (b1, b2, b3, b4)]
    xs = _scatter_x(x)
    in_maps = []
    for c in range(8):
        m = {"x_in": xs[c], "x_oth": xs[c ^ 1]}
        for k in range(4):
            m[f"w{k+1}"] = ws[k]
            m[f"b{k+1}"] = bs[k]
        in_maps.append(m)
    res = run_bass_kernel_spmd(nc, in_maps, core_ids=list(range(8)))
    return _gather_x(res)


def kernel(**inputs):
    return _run_fused(inputs)
